# version 16
# baseline (speedup 1.0000x reference)
import math
from contextlib import ExitStack

import numpy as np
import ml_dtypes

import concourse.bass as bass
import concourse.mybir as mybir
from concourse.bass_utils import run_bass_kernel_spmd

F32 = mybir.dt.float32
BF16 = mybir.dt.bfloat16
AF = mybir.ActivationFunctionType
ALU = mybir.AluOpType
AX = mybir.AxisListType

N_CORES = 8
CONST_VALS = [1e-5, 1e-6, 1e-12, 1.0, -1.0, 0.5, -0.5, 2.0, 1.0 / 1024.0,
              1.0 / 2048.0, 1.0 / 32.0, 80.0, 0.125, -0.125, 1e-4]
S = 2048
D = 1024
FD = D // 128
NEG = -1.0e30
BF = ml_dtypes.bfloat16



class Buf:
    __slots__ = ("name", "last_w", "readers")

    def __init__(self, name):
        self.name = name
        self.last_w = None
        self.readers = {}


class TT:
    __slots__ = ("h", "buf")

    def __init__(self, h, name):
        self.h = h
        self.buf = Buf(name)


ENGINES = ("tensor", "vector", "scalar", "sync", "gpsimd")


NSEM = 12


class Prog:

    def __init__(self, nc, stack):
        self.nc = nc
        self.sems = {e: stack.enter_context(nc.semaphore("sem_" + e)) for e in ENGINES}
        self.qpool = {}
        self.quse = {}
        self.qidx = {}
        for q in ("sync", "gpsimd"):
            self.qpool[q] = [stack.enter_context(nc.semaphore(f"q_{q}_{j}"))
                             for j in range(NSEM)]
            self.quse[q] = [0] * NSEM
            self.qidx[q] = 0
        self.count = {e: 0 for e in ENGINES}
        self.waited = {e: {} for e in ENGINES}
        self.items = {e: [] for e in ENGINES}

    def _resolve(self, engine, deps):
        waits = []
        w = self.waited[engine]
        for key, (sem, val) in deps.items():
            if key == "tensor" and engine == "tensor":
                continue
            if val > w.get(key, 0):
                waits.append((sem, val))
                w[key] = val
        return waits

    def _deps(self, reads, writes):
        deps = {}
        for b in reads:
            if b.last_w is not None:
                key, sem, val = b.last_w
                if key not in deps or deps[key][1] < val:
                    deps[key] = (sem, val)
        for b in writes:
            if b.last_w is not None:
                key, sem, val = b.last_w
                if key not in deps or deps[key][1] < val:
                    deps[key] = (sem, val)
            for key, (sem, val) in b.readers.items():
                if key not in deps or deps[key][1] < val:
                    deps[key] = (sem, val)
        return deps

    def emit(self, engine, fn, reads=(), writes=()):
        deps = self._deps(reads, writes)
        waits = self._resolve(engine, deps)
        self.count[engine] += 1
        seq = self.count[engine]
        self.items[engine].append((waits, fn, self.sems[engine], 1))
        tok = (engine, self.sems[engine], seq)
        for b in writes:
            b.last_w = tok
            b.readers = {}
        for b in reads:
            key = engine
            if key not in b.readers or b.readers[key][1] < seq:
                b.readers[key] = (self.sems[engine], seq)
        return seq

    def dma(self, queue, out_ap, in_ap, reads=(), writes=(), **kw):
        deps = self._deps(reads, writes)
        waits = self._resolve(queue, deps)
        j = self.qidx[queue] % NSEM
        self.qidx[queue] += 1
        sem = self.qpool[queue][j]
        prev = self.quse[queue][j]
        key = (queue, j)
        if prev > 0:
            w = self.waited[queue]
            if 16 * prev > w.get(key, 0):
                waits.append((sem, 16 * prev))
                w[key] = 16 * prev
        self.quse[queue][j] = prev + 1
        val = 16 * (prev + 1)
        self.items[queue].append((waits, lambda E: E.dma_start(out_ap, in_ap, **kw),
                                  sem, 16))
        tok = (key, sem, val)
        for b in writes:
            b.last_w = tok
            b.readers = {}
        for b in reads:
            if key not in b.readers or b.readers[key][1] < val:
                b.readers[key] = (sem, val)
        return val

    def group(self, engine, fns, reads=(), writes=()):
        deps = self._deps(reads, writes)
        waits = self._resolve(engine, deps)
        sem = self.sems[engine]
        for i, fn in enumerate(fns):
            self.count[engine] += 1
            self.items[engine].append((waits if i == 0 else [], fn, sem, 1))
        seq = self.count[engine]
        tok = (engine, sem, seq)
        for b in writes:
            b.last_w = tok
            b.readers = {}
        for b in reads:
            if engine not in b.readers or b.readers[engine][1] < seq:
                b.readers[engine] = (sem, seq)
        return seq

    def t(self, fn, reads=(), writes=()):
        return self.emit("tensor", fn, reads, writes)

    def v(self, fn, reads=(), writes=()):
        return self.emit("vector", fn, reads, writes)

    def s(self, fn, reads=(), writes=()):
        return self.emit("scalar", fn, reads, writes)

    def finalize(self):
        targets = [(self.sems[e], self.count[e]) for e in ("tensor", "vector", "scalar")
                   if self.count[e] > 0]
        for q in ("sync", "gpsimd"):
            for j in range(NSEM):
                if self.quse[q][j] > 0:
                    targets.append((self.qpool[q][j], 16 * self.quse[q][j]))
        items = self.items

        with self.nc.Block() as block:
            def runner(eng_name, tail=False):
                def run(E):
                    for waits, fn, own_sem, inc in items[eng_name]:
                        for sem, val in waits:
                            E.wait_ge(sem, val)
                        fn(E).then_inc(own_sem, inc)
                    if tail:
                        for sem, val in targets:
                            E.wait_ge(sem, val)
                return run

            block.tensor(runner("tensor"))
            block.vector(runner("vector"))
            block.scalar(runner("scalar"))
            block.sync(runner("sync", tail=True))
            block.gpsimd(runner("gpsimd"))



class Arena:
    def __init__(self, nc, stack):
        c = stack.enter_context
        self.A = [TT(c(nc.sbuf_tensor(f"A{i}", [128, 2048], BF16)), f"A{i}") for i in range(32)]
        self.N = [TT(c(nc.sbuf_tensor(f"N{i}", [128, 2048], BF16)), f"N{i}") for i in range(FD)]
        self.W = [TT(c(nc.sbuf_tensor(f"W{i}", [128, 2048], BF16)), f"W{i}") for i in range(2)]
        self.Sg = [TT(c(nc.sbuf_tensor(f"Sg{i}", [128, 2048], F32)), f"Sg{i}") for i in range(2)]
        self.M = [TT(c(nc.sbuf_tensor(f"M{i}", [128, 2048], BF16)), f"M{i}") for i in range(2)]
        self.vec = [TT(c(nc.sbuf_tensor(f"vec{i}", [128, 256], F32)), f"vec{i}") for i in range(4)]
        self.col = [TT(c(nc.sbuf_tensor(f"col{i}", [128, 64], F32)), f"col{i}") for i in range(4)]
        self.tri = TT(c(nc.sbuf_tensor("tri", [128, 128], F32)), "tri")
        self.esel = TT(c(nc.sbuf_tensor("esel", [128, 1024], BF16)), "esel")
        self.gwt = TT(c(nc.sbuf_tensor("gwt", [128, 64], BF16)), "gwt")
        self.ones = TT(c(nc.sbuf_tensor("ones", [128, 128], BF16)), "ones")
        self.identf = TT(c(nc.sbuf_tensor("identf", [128, 128], F32)), "identf")
        self.trils = TT(c(nc.sbuf_tensor("trils", [128, 128], F32)), "trils")
        self.trili = TT(c(nc.sbuf_tensor("trili", [128, 128], F32)), "trili")
        self.onesf = TT(c(nc.sbuf_tensor("onesf", [128, 128], F32)), "onesf")
        self.ident = TT(c(nc.sbuf_tensor("ident", [128, 128], BF16)), "ident")
        self.P = [TT(c(nc.psum_tensor(f"P{i}", [128, 512], F32)), f"P{i}") for i in range(6)]
        self.Tp = [TT(c(nc.psum_tensor(f"T{i}", [128, 1024], BF16)), f"T{i}") for i in range(2)]
        self._rr = 0

    def bank(self):
        b = self.P[self._rr % 6]
        self._rr += 1
        return b



def f32view(tt, lo=None, hi=None):
    ap = tt.h[:, :].bitcast(F32)
    if lo is None:
        return ap
    return ap[:, lo:hi]


def emit_layernorm(p, ar, x_dram, x_buf, eps=1e-5):
    for i in range(FD):
        s0 = ar.A[16 + (i % 2) * 2]
        s1 = ar.A[17 + (i % 2) * 2]
        a0 = f32view(s0)
        a1 = f32view(s1)
        p.dma("sync", a0, x_dram[128 * i:128 * (i + 1), 0:1024],
              reads=(x_buf,), writes=(s0.buf,))
        p.dma("sync", a1, x_dram[128 * i:128 * (i + 1), 1024:2048],
              reads=(x_buf,), writes=(s1.buf,))
        p.s(lambda E, o=ar.N[i].h, a=a0: E.activation(o[:, 0:1024], a, AF.Copy),
            reads=(s0.buf,), writes=(ar.N[i].buf,))
        p.s(lambda E, o=ar.N[i].h, a=a1: E.activation(o[:, 1024:2048], a, AF.Copy),
            reads=(s1.buf,), writes=(ar.N[i].buf,))
        p.v(lambda E, o=ar.A[i].h, x_=ar.N[i].h: E.tensor_mul(o[:, :], x_[:, :], x_[:, :]),
            reads=(ar.N[i].buf,), writes=(ar.A[i].buf,))
    inv = 1.0 / float(D)
    for tc in range(4):
        sl = slice(512 * tc, 512 * (tc + 1))
        for i in range(FD):
            p.t(lambda E, i=i, sl=sl: E.matmul(
                ar.P[4].h[:, :], ar.ones.h[:, :], ar.N[i].h[:, sl],
                start=(i == 0), stop=(i == FD - 1)),
                reads=(ar.ones.buf, ar.N[i].buf), writes=(ar.P[4].buf,))
        for i in range(FD):
            p.t(lambda E, i=i, sl=sl: E.matmul(
                ar.P[5].h[:, :], ar.ones.h[:, :], ar.A[i].h[:, sl],
                start=(i == 0), stop=(i == FD - 1)),
                reads=(ar.ones.buf, ar.A[i].buf), writes=(ar.P[5].buf,))
        mean = ar.Sg[0].h[:, sl]
        ex2 = ar.Sg[1].h[:, sl]
        rstd = ar.M[0].h[:, sl]
        mr = ar.M[1].h[:, sl]
        p.s(lambda E, mean=mean: E.activation(mean, ar.P[4].h[:, :], AF.Copy, scale=inv),
            reads=(ar.P[4].buf,), writes=(ar.Sg[0].buf,))
        p.s(lambda E, ex2=ex2: E.activation(ex2, ar.P[5].h[:, :], AF.Copy, scale=inv),
            reads=(ar.P[5].buf,), writes=(ar.Sg[1].buf,))
        p.s(lambda E, mean=mean: E.activation(ar.P[4].h[:, :], mean, AF.Square),
            reads=(ar.Sg[0].buf,), writes=(ar.P[4].buf,))
        p.v(lambda E, ex2=ex2: E.tensor_sub(ex2, ex2, ar.P[4].h[:, :]),
            reads=(ar.Sg[1].buf, ar.P[4].buf), writes=(ar.Sg[1].buf,))
        p.s(lambda E, ex2=ex2: E.activation(ar.P[5].h[:, :], ex2, AF.Ln, bias=eps),
            reads=(ar.Sg[1].buf,), writes=(ar.P[5].buf,))
        p.s(lambda E, rstd=rstd: E.activation(rstd, ar.P[5].h[:, :], AF.Exp, scale=-0.5),
            reads=(ar.P[5].buf,), writes=(ar.M[0].buf,))
        p.v(lambda E, mr=mr, mean=mean, rstd=rstd: E.tensor_mul(mr, mean, rstd),
            reads=(ar.Sg[0].buf, ar.M[0].buf), writes=(ar.M[1].buf,))
    for i in range(FD):
        p.v(lambda E, i=i: E.tensor_mul(ar.A[i].h[:, :], ar.N[i].h[:, :], ar.M[0].h[:, :]),
            reads=(ar.N[i].buf, ar.M[0].buf), writes=(ar.A[i].buf,))
        p.v(lambda E, i=i: E.tensor_sub(ar.N[i].h[:, :], ar.A[i].h[:, :], ar.M[1].h[:, :]),
            reads=(ar.A[i].buf, ar.M[1].buf), writes=(ar.N[i].buf,))


def _wtiles(wT, nch, nob, owid=128, halves=1):
    wT = np.asarray(wT, np.float32)
    arr = wT.reshape(nch, 128, nob, owid).transpose(2, 0, 1, 3)
    if halves == 1:
        out = arr.transpose(0, 2, 1, 3).reshape(nob, 128, nch * owid)
    else:
        hc = nch // halves
        out = arr.reshape(nob, halves, hc, 128, owid).transpose(0, 1, 3, 2, 4).reshape(
            nob, halves, 128, hc * owid)
    return np.ascontiguousarray(out)


def load_wtile(p, ar, slot, wt_dram, wt_buf, half, nc_chunks, ob, owid=128):
    if half is None:
        in_ap = wt_dram[ob, :, :]
    else:
        in_ap = wt_dram[ob, half, :, :]
    p.dma("gpsimd", slot.h[:, 0:nc_chunks * owid], in_ap,
          reads=(wt_buf,), writes=(slot.buf,))


def emit_ffn_ternary(p, ar, io, wd, l):
    x_src, xsb, x_dst, xdb = io
    pre = f"L{l}_ffn_"
    w1T, w1b = wd[pre + "w1_w"]
    w2T, w2b = wd[pre + "w2_w"]
    h1T, h1b = wd[pre + "h1_w"]
    h2T, h2b = wd[pre + "h2_w"]
    w3T, w3b = wd[pre + "w3_w"]
    sc, scb = wd[pre + "scales"]
    g1buf, g1b = wd[f"g1buf"]
    g2buf, g2b = wd[f"g2buf"]

    emit_layernorm(p, ar, x_src, xsb)

    p.dma("sync", ar.vec[0].h[:, 0:136], sc[:, :], reads=(scb,), writes=(ar.vec[0].buf,))
    p.v(lambda E: E.tensor_scalar_mul(ar.vec[0].h[:, 136:144], ar.vec[0].h[:, 128:136], 0.5),
        reads=(ar.vec[0].buf,), writes=(ar.vec[0].buf,))

    def col_of(idx):
        return ar.vec[0].h[:, idx:idx + 1]

    for ob in range(32):
        slot = ar.W[ob % 2]
        p.dma("gpsimd", slot.h[:, 0:1024], w1T[ob, :, :], reads=(w1b,), writes=(slot.buf,))
        p.dma("gpsimd", slot.h[:, 1024:2048], w2T[ob, :, :], reads=(w2b,), writes=(slot.buf,))
        for tc in range(4):
            sl = slice(512 * tc, 512 * (tc + 1))
            pa = ar.bank()
            pb = ar.bank()
            for c in range(FD):
                p.t(lambda E, c=c, sl=sl, pa=pa, slot=slot: E.matmul(
                    pa.h[:, :], slot.h[:, c * 128:(c + 1) * 128], ar.N[c].h[:, sl],
                    start=(c == 0), stop=(c == FD - 1)),
                    reads=(slot.buf, ar.N[c].buf), writes=(pa.buf,))
            for c in range(FD):
                p.t(lambda E, c=c, sl=sl, pb=pb, slot=slot: E.matmul(
                    pb.h[:, :], slot.h[:, 1024 + c * 128:1024 + (c + 1) * 128], ar.N[c].h[:, sl],
                    start=(c == 0), stop=(c == FD - 1)),
                    reads=(slot.buf, ar.N[c].buf), writes=(pb.buf,))
            p.s(lambda E, sl=sl, pa=pa, ob=ob: E.activation(
                ar.Sg[0].h[:, sl], pa.h[:, :], AF.Silu, scale=col_of(ob)),
                reads=(pa.buf, ar.vec[0].buf), writes=(ar.Sg[0].buf,))
            p.v(lambda E, sl=sl, pb=pb, ob=ob: E.scalar_tensor_tensor(
                ar.A[ob].h[:, sl], pb.h[:, :], col_of(32 + ob), ar.Sg[0].h[:, sl],
                ALU.mult, ALU.mult),
                reads=(pb.buf, ar.Sg[0].buf, ar.vec[0].buf), writes=(ar.A[ob].buf,))

    def dense4096(wT, wTb, src_tiles, n_ob, evac_fn):
        for ob in range(n_ob):
            slot0, slot1 = ar.W[0], ar.W[1]
            load_wtile(p, ar, slot0, wT, wTb, 0, 16, ob)
            load_wtile(p, ar, slot1, wT, wTb, 1, 16, ob)
            banks = [ar.bank() for _ in range(4)]
            for half, slot in ((0, slot0), (1, slot1)):
                for tc in range(4):
                    sl = slice(512 * tc, 512 * (tc + 1))
                    pk = banks[tc]
                    for c in range(16):
                        src = src_tiles[16 * half + c]
                        p.t(lambda E, c=c, sl=sl, pk=pk, slot=slot, src=src, half=half: E.matmul(
                            pk.h[:, :], slot.h[:, c * 128:(c + 1) * 128], src.h[:, sl],
                            start=(half == 0 and c == 0), stop=(half == 1 and c == 15)),
                            reads=(slot.buf, src.buf), writes=(pk.buf,))
            for tc in range(4):
                evac_fn(banks[tc], ob, tc, slice(512 * tc, 512 * (tc + 1)))

    def mid_phase(hT, hTb, gbuf, gbufb, scale_base):
        stg = ar.Sg[1].h[:, :].bitcast(BF16)

        def evac(pk, ob, tc, sl):
            ssl = slice(1024 * (tc % 2), 1024 * (tc % 2) + 512)
            p.s(lambda E, pk=pk, ssl=ssl, ob=ob: E.activation(
                stg[:, ssl], pk.h[:, :], AF.Gelu, scale=col_of(scale_base + ob)),
                reads=(pk.buf, ar.vec[0].buf), writes=(ar.Sg[1].buf,))
            p.dma("sync", gbuf[ob * 128:(ob + 1) * 128, sl], stg[:, ssl],
                  reads=(ar.Sg[1].buf,), writes=(gbufb,))

        dense4096(hT, hTb, ar.A, 32, evac)
        for i in range(32):
            p.dma("sync", ar.A[i].h[:, :], gbuf[i * 128:(i + 1) * 128, :],
                  reads=(gbufb,), writes=(ar.A[i].buf,))

    mid_phase(h1T, h1b, g1buf, g1b, 64)
    mid_phase(h2T, h2b, g2buf, g2b, 96)

    def evac_out(pk, ob, tc, sl):
        if tc == 0:
            p.dma("sync", ar.Sg[1].h[:, :], x_src[ob * 128:(ob + 1) * 128, :],
                  reads=(xsb,), writes=(ar.Sg[1].buf,))
        p.v(lambda E, sl=sl, pk=pk, ob=ob: E.scalar_tensor_tensor(
            ar.Sg[0].h[:, sl], pk.h[:, :], col_of(136 + ob), ar.Sg[1].h[:, sl],
            ALU.mult, ALU.add),
            reads=(pk.buf, ar.Sg[1].buf, ar.vec[0].buf), writes=(ar.Sg[0].buf,))
        if tc == 3:
            p.dma("sync", x_dst[ob * 128:(ob + 1) * 128, :], ar.Sg[0].h[:, :],
                  reads=(ar.Sg[0].buf,), writes=(xdb,))

    dense4096(w3T, w3b, ar.A, FD, evac_out)




def emit_attn_l3(p, ar, io, wd, l):
    x_src, xsb, x_dst, xdb = io
    qkvT, qkvb = wd["L3_attn_qkv_w"]
    projT, projb = wd["L3_attn_proj_w"]
    pbias, pbb = wd["L3_attn_proj_bias"]
    trow, trb = wd["L3_attn_temp"]
    onesrow, orb = wd["c_onesrow"]

    emit_layernorm(p, ar, x_src, xsb)

    H, HD = 16, 64

    for ob in range(16):
        slot = ar.W[ob % 2]
        load_wtile(p, ar, slot, qkvT, qkvb, None, 8, ob)
        banks = [ar.bank() for _ in range(4)]
        for tc in range(4):
            sl = slice(512 * tc, 512 * (tc + 1))
            for c in range(FD):
                p.t(lambda E, c=c, sl=sl, pk=banks[tc], slot=slot: E.matmul(
                    pk.h[:, :], slot.h[:, c * 128:(c + 1) * 128], ar.N[c].h[:, sl],
                    start=(c == 0), stop=(c == FD - 1)),
                    reads=(slot.buf, ar.N[c].buf), writes=(banks[tc].buf,))
        dst = ar.A[ob]
        for tc in range(4):
            sl = slice(512 * tc, 512 * (tc + 1))
            p.s(lambda E, sl=sl, pk=banks[tc], dst=dst: E.activation(
                dst.h[:, sl], pk.h[:, :], AF.Copy),
                reads=(banks[tc].buf,), writes=(dst.buf,))

    qkvR, qkvRb = wd["L3_attn_qkv_raw"]
    for c in range(FD):
        vw = ar.A[24 + c // 2]
        off = (c % 2) * 1024
        in_ap = qkvR[c * 128:(c + 1) * 128, 0:1024]
        p.dma("gpsimd", vw.h[:, off:off + 1024], in_ap, reads=(qkvRb,), writes=(vw.buf,))
    for tb in range(16):
        for dvc in range(2):
            pk = ar.bank()
            for c in range(FD):
                vw = ar.A[24 + c // 2]
                off = (c % 2) * 1024 + dvc * 512
                p.t(lambda E, c=c, tb=tb, pk=pk, vw=vw, off=off: E.matmul(
                    pk.h[:, :], ar.N[c].h[:, tb * 128:(tb + 1) * 128],
                    vw.h[:, off:off + 512],
                    start=(c == 0), stop=(c == FD - 1)),
                    reads=(ar.N[c].buf, vw.buf), writes=(pk.buf,))
            vt = ar.A[16 + tb // 2]
            voff = (tb % 2) * 1024 + dvc * 512
            p.s(lambda E, pk=pk, vt=vt, voff=voff: E.activation(
                vt.h[:, voff:voff + 512], pk.h[:, :], AF.Copy),
                reads=(pk.buf,), writes=(vt.buf,))

    for which, base, mdst in (("q", 0, ar.M[0]), ("k", 8, ar.M[1])):
        for tc in range(4):
            sl = slice(512 * tc, 512 * (tc + 1))
            for i in range(FD):
                sqt = ar.A[24 + i % 2]
                p.s(lambda E, i=i, sl=sl, base=base, sqt=sqt: E.activation(
                    sqt.h[:, 0:512], ar.A[base + i].h[:, sl], AF.Square),
                    reads=(ar.A[base + i].buf,), writes=(sqt.buf,))
                p.t(lambda E, i=i, sqt=sqt: E.matmul(
                    ar.P[4].h[:, :], ar.ones.h[:, :], sqt.h[:, 0:512],
                    start=(i == 0), stop=(i == FD - 1)),
                    reads=(ar.ones.buf, sqt.buf), writes=(ar.P[4].buf,))
            p.s(lambda E: E.activation(ar.P[5].h[:, :], ar.P[4].h[:, :], AF.Ln),
                reads=(ar.P[4].buf,), writes=(ar.P[5].buf,))
            p.s(lambda E: E.activation(ar.Sg[1].h[:, 0:512], ar.P[5].h[:, :], AF.Exp, scale=0.5),
                reads=(ar.P[5].buf,), writes=(ar.Sg[1].buf,))
            p.v(lambda E: E.tensor_scalar_max(ar.Sg[1].h[:, 512:1024], ar.Sg[1].h[:, 0:512], 1e-12),
                reads=(ar.Sg[1].buf,), writes=(ar.Sg[1].buf,))
            p.v(lambda E: E.reciprocal(ar.Sg[1].h[:, 1024:1536], ar.Sg[1].h[:, 512:1024]),
                reads=(ar.Sg[1].buf,), writes=(ar.Sg[1].buf,))
            p.v(lambda E, mdst=mdst, sl=sl: E.tensor_scalar_mul(
                mdst.h[:, sl], ar.Sg[1].h[:, 1024:1536], 0.125),
                reads=(ar.Sg[1].buf,), writes=(mdst.buf,))
    for i in range(FD):
        p.v(lambda E, i=i: E.tensor_mul(ar.A[i].h[:, :], ar.A[i].h[:, :], ar.M[0].h[:, :]),
            reads=(ar.A[i].buf, ar.M[0].buf), writes=(ar.A[i].buf,))
        p.v(lambda E, i=i: E.tensor_mul(ar.A[8 + i].h[:, :], ar.A[8 + i].h[:, :], ar.M[1].h[:, :]),
            reads=(ar.A[8 + i].buf, ar.M[1].buf), writes=(ar.A[8 + i].buf,))

    tr = ar.vec[1]
    p.dma("sync", tr.h[0:1, 0:16], trow, reads=(trb,), writes=(tr.buf,))
    p.dma("sync", tr.h[0:1, 32:160], onesrow, reads=(orb,), writes=(tr.buf,))
    p.v(lambda E: E.reciprocal(tr.h[0:1, 16:32], tr.h[0:1, 0:16]),
        reads=(tr.buf,), writes=(tr.buf,))
    p.t(lambda E: E.matmul(ar.P[4].h[:, 0:16], tr.h[0:1, 32:160], tr.h[0:1, 16:32],
                           start=True, stop=True),
        reads=(tr.buf,), writes=(ar.P[4].buf,))
    p.s(lambda E: E.activation(ar.col[0].h[:, 0:16], ar.P[4].h[:, 0:16], AF.Copy),
        reads=(ar.P[4].buf,), writes=(ar.col[0].buf,))

    def l3_iter(idx, h, qb, stage):
        pi = idx % 2
        sg = ar.Sg[pi]
        P_sb = sg.h[:, :].bitcast(BF16)
        Pt_sb = ar.M[pi]
        cl = ar.col[1 + pi]
        qa = ar.A[h // 2]
        ka = ar.A[8 + h // 2]
        prow = 64 * (h % 2)
        banks = [ar.P[k] for k in range(4)]
        pvb = ar.P[4 + pi]
        if stage == 0:
            for kc in range(4):
                pk = banks[kc]
                p.t(lambda E, pk=pk, kc=kc, qa=qa, ka=ka, prow=prow, qb=qb: E.matmul(
                    pk.h[:, :], qa.h[prow:prow + 64, qb * 128:(qb + 1) * 128],
                    ka.h[prow:prow + 64, 512 * kc:512 * (kc + 1)],
                    start=True, stop=True),
                    reads=(qa.buf, ka.buf), writes=(pk.buf,))
                p.v(lambda E, pk=pk, kc=kc, cl=cl, h=h: E.reduce_max(
                    cl.h[:, 8 + kc:9 + kc], pk.h[:, :], AX.X),
                    reads=(pk.buf,), writes=(cl.buf,))
        elif stage == 1:
            p.v(lambda E, cl=cl: E.reduce_max(cl.h[:, 0:1], cl.h[:, 8:12], AX.X, negate=True),
                reads=(cl.buf,), writes=(cl.buf,))
            p.v(lambda E, cl=cl, h=h: E.tensor_scalar(
                cl.h[:, 1:2], cl.h[:, 0:1], ar.col[0].h[:, h:h + 1], None, ALU.mult),
                reads=(cl.buf, ar.col[0].buf), writes=(cl.buf,))
            p.group("scalar",
                    [(lambda E, pk=banks[kc], kc=kc, cl=cl, h=h, P_sb=P_sb: E.activation(
                        P_sb[:, 512 * kc:512 * (kc + 1)], pk.h[:, :], AF.Exp,
                        bias=cl.h[:, 1:2], scale=ar.col[0].h[:, h:h + 1],
                        accum_out=cl.h[:, 16 + kc:17 + kc]))
                     for kc in range(4)],
                    reads=tuple(b.buf for b in banks) + (cl.buf, ar.col[0].buf),
                    writes=(sg.buf, cl.buf))
            p.v(lambda E, cl=cl: E.reduce_sum(cl.h[:, 2:3], cl.h[:, 16:20], AX.X),
                reads=(cl.buf,), writes=(cl.buf,))
            p.v(lambda E, cl=cl: E.reciprocal(cl.h[:, 3:4], cl.h[:, 2:3]),
                reads=(cl.buf,), writes=(cl.buf,))
            p.v(lambda E, cl=cl, P_sb=P_sb, sg=sg: E.tensor_scalar(
                P_sb[:, 0:2048], P_sb[:, 0:2048], cl.h[:, 3:4], None, ALU.mult),
                reads=(sg.buf, cl.buf), writes=(sg.buf,))
        else:
            tp = ar.Tp[pi]
            for g in range(2):
                for j in range(8):
                    kc = g * 8 + j
                    p.t(lambda E, tp=tp, j=j, kc=kc, P_sb=P_sb, sg=sg: E.transpose(
                        tp.h[:, j * 128:(j + 1) * 128], P_sb[:, kc * 128:(kc + 1) * 128],
                        ar.ident.h[:, :]),
                        reads=(sg.buf, ar.ident.buf), writes=(tp.buf,))
                p.s(lambda E, tp=tp, g=g, Pt_sb=Pt_sb: E.activation(
                    Pt_sb.h[:, g * 1024:(g + 1) * 1024], tp.h[:, :], AF.Copy),
                    reads=(tp.buf,), writes=(Pt_sb.buf,))
            pk = pvb
            for kc in range(16):
                vt = ar.A[16 + kc // 2]
                voff = (kc % 2) * 1024 + h * 64
                p.t(lambda E, pk=pk, kc=kc, vt=vt, voff=voff, Pt_sb=Pt_sb: E.matmul(
                    pk.h[0:64, 0:128], vt.h[:, voff:voff + 64],
                    Pt_sb.h[:, kc * 128:(kc + 1) * 128],
                    start=(kc == 0), stop=(kc == 15)),
                    reads=(vt.buf, Pt_sb.buf), writes=(pk.buf,))
            od = ar.A[24 + h // 2]
            p.v(lambda E, pk=pk, od=od, prow=prow, qb=qb: E.tensor_copy(
                od.h[prow:prow + 64, qb * 128:(qb + 1) * 128], pk.h[0:64, 0:128]),
                reads=(pk.buf,), writes=(od.buf,))

    seq3 = [(h, qb) for h in range(H) for qb in range(16)]
    for idx, (h, qb) in enumerate(seq3):
        l3_iter(idx, h, qb, 0)
        if idx:
            l3_iter(idx - 1, *seq3[idx - 1], 2)
        l3_iter(idx, h, qb, 1)
    l3_iter(len(seq3) - 1, *seq3[-1], 2)

    p.dma("sync", ar.vec[2].h[:, 0:8], pbias[:, :], reads=(pbb,), writes=(ar.vec[2].buf,))
    for ob in range(FD):
        slot = ar.W[ob % 2]
        load_wtile(p, ar, slot, projT, projb, None, 8, ob)
        p.dma("sync", ar.Sg[1].h[:, :], x_src[ob * 128:(ob + 1) * 128, :],
              reads=(xsb,), writes=(ar.Sg[1].buf,))
        banks = [ar.bank() for _ in range(4)]
        for tc in range(4):
            sl = slice(512 * tc, 512 * (tc + 1))
            for c in range(FD):
                p.t(lambda E, c=c, sl=sl, pk=banks[tc], slot=slot: E.matmul(
                    pk.h[:, :], slot.h[:, c * 128:(c + 1) * 128], ar.A[24 + c].h[:, sl],
                    start=(c == 0), stop=(c == FD - 1)),
                    reads=(slot.buf, ar.A[24 + c].buf), writes=(banks[tc].buf,))
        for tc in range(4):
            sl = slice(512 * tc, 512 * (tc + 1))
            pk = banks[tc]
            p.v(lambda E, sl=sl, pk=pk, ob=ob: E.scalar_tensor_tensor(
                ar.Sg[0].h[:, sl], pk.h[:, :], ar.vec[2].h[:, ob:ob + 1], ar.Sg[1].h[:, sl],
                ALU.add, ALU.add),
                reads=(pk.buf, ar.Sg[1].buf, ar.vec[2].buf), writes=(ar.Sg[0].buf,))
        p.dma("sync", x_dst[ob * 128:(ob + 1) * 128, :], ar.Sg[0].h[:, :],
              reads=(ar.Sg[0].buf,), writes=(xdb,))


def decl_attn_l3(nc, wd, l):
    shapes = {
        "L3_attn_qkv_w": ((16, 128, 1024), F32),
        "L3_attn_qkv_raw": ((1024, 1024), F32),
        "L3_attn_proj_w": ((8, 128, 1024), F32),
        "L3_attn_proj_bias": ((128, 8), F32),
        "L3_attn_temp": ((1, 16), F32),
        "c_onesrow": ((1, 128), F32),
    }
    for k, (shp, dt) in shapes.items():
        if k not in wd:
            h = nc.dram_tensor(k, list(shp), dt, kind="ExternalInput")
            wd[k] = (h.ap(), Buf(k))


def prep_attn_l3(inputs, l):
    d = {}
    qkvT = _t(inputs["L3_attn_qkv_w"])
    d["L3_attn_qkv_w"] = _wtiles(qkvT[:, 0:2048], 8, 16)
    d["L3_attn_qkv_raw"] = np.ascontiguousarray(qkvT[:, 2048:3072])
    d["L3_attn_proj_w"] = _wtiles(_t(inputs["L3_attn_proj_w"]), 8, 8)
    d["L3_attn_proj_bias"] = _cols(inputs["L3_attn_proj_b"])
    d["L3_attn_temp"] = np.ascontiguousarray(
        np.asarray(inputs["L3_attn_temperature"], np.float32).reshape(1, 16))
    d["c_onesrow"] = np.ones((1, 128), np.float32)
    return d




def emit_rms_stats(p, ar, src_tiles, eps, sq_scratch):
    inv = 1.0 / float(D)
    for tc in range(4):
        sl = slice(512 * tc, 512 * (tc + 1))
        for i in range(FD):
            sqt = sq_scratch[i % 2]
            p.s(lambda E, i=i, sl=sl, sqt=sqt: E.activation(
                sqt.h[:, 0:512], src_tiles[i].h[:, sl], AF.Square),
                reads=(src_tiles[i].buf,), writes=(sqt.buf,))
            p.t(lambda E, i=i, sqt=sqt: E.matmul(
                ar.P[4].h[:, :], ar.ones.h[:, :], sqt.h[:, 0:512],
                start=(i == 0), stop=(i == FD - 1)),
                reads=(ar.ones.buf, sqt.buf), writes=(ar.P[4].buf,))
        p.s(lambda E, eps=eps: E.activation(ar.P[5].h[:, :], ar.P[4].h[:, :], AF.Ln,
                                            scale=inv, bias=eps),
            reads=(ar.P[4].buf,), writes=(ar.P[5].buf,))
        p.s(lambda E, sl=sl: E.activation(ar.M[0].h[:, sl], ar.P[5].h[:, :], AF.Exp, scale=-0.5),
            reads=(ar.P[5].buf,), writes=(ar.M[0].buf,))


FFN0_PHASE = 3


def emit_ffn0(p, ar, io, wd, l):
    x_src, xsb, x_dst, xdb = io
    wqT, wqb = wd["L0_ffn_wq"]
    wkT, wkb = wd["L0_ffn_wk"]
    wvT, wvb = wd["L0_ffn_wv"]
    woT, wob = wd["L0_ffn_wo"]
    gwT, gwb = wd["L0_ffn_gate_w"]
    expT, expb = wd["L0_ffn_expert_w"]
    anw, anwb = wd["L0_ffn_attn_norm_w"]
    fnw, fnwb = wd["L0_ffn_ffn_norm_w"]
    H, HD, KVH = 16, 64, 4

    emit_layernorm(p, ar, x_src, xsb)

    p.dma("sync", ar.vec[2].h[:, 0:8], anw[:, :], reads=(anwb,), writes=(ar.vec[2].buf,))
    p.dma("sync", ar.vec[2].h[:, 8:16], fnw[:, :], reads=(fnwb,), writes=(ar.vec[2].buf,))
    p.dma("gpsimd", ar.gwt.h[:, 0:64], gwT[0, :, :], reads=(gwb,), writes=(ar.gwt.buf,))

    emit_rms_stats(p, ar, ar.N, 1e-6, (ar.A[30], ar.A[31]))
    for i in range(FD):
        p.v(lambda E, i=i: E.scalar_tensor_tensor(
            ar.A[i].h[:, :], ar.N[i].h[:, :], ar.vec[2].h[:, i:i + 1], ar.M[0].h[:, :],
            ALU.mult, ALU.mult),
            reads=(ar.N[i].buf, ar.vec[2].buf, ar.M[0].buf), writes=(ar.A[i].buf,))
    rn = ar.A[0:8]

    for ob in range(8):
        slot = ar.W[ob % 2]
        load_wtile(p, ar, slot, wqT, wqb, None, 8, ob)
        banks = [ar.bank() for _ in range(4)]
        for tc in range(4):
            sl = slice(512 * tc, 512 * (tc + 1))
            for c in range(FD):
                p.t(lambda E, c=c, sl=sl, pk=banks[tc], slot=slot: E.matmul(
                    pk.h[:, :], slot.h[:, c * 128:(c + 1) * 128], rn[c].h[:, sl],
                    start=(c == 0), stop=(c == FD - 1)),
                    reads=(slot.buf, rn[c].buf), writes=(banks[tc].buf,))
        for tc in range(4):
            sl = slice(512 * tc, 512 * (tc + 1))
            p.s(lambda E, sl=sl, pk=banks[tc], ob=ob: E.activation(
                ar.A[8 + ob].h[:, sl], pk.h[:, :], AF.Copy, scale=0.125),
                reads=(banks[tc].buf,), writes=(ar.A[8 + ob].buf,))
    for ob in range(2):
        slot = ar.W[ob % 2]
        load_wtile(p, ar, slot, wkT, wkb, None, 8, ob)
        banks = [ar.bank() for _ in range(4)]
        for tc in range(4):
            sl = slice(512 * tc, 512 * (tc + 1))
            for c in range(FD):
                p.t(lambda E, c=c, sl=sl, pk=banks[tc], slot=slot: E.matmul(
                    pk.h[:, :], slot.h[:, c * 128:(c + 1) * 128], rn[c].h[:, sl],
                    start=(c == 0), stop=(c == FD - 1)),
                    reads=(slot.buf, rn[c].buf), writes=(banks[tc].buf,))
        for tc in range(4):
            sl = slice(512 * tc, 512 * (tc + 1))
            p.s(lambda E, sl=sl, pk=banks[tc], ob=ob: E.activation(
                ar.A[16 + ob].h[:, sl], pk.h[:, :], AF.Copy),
                reads=(banks[tc].buf,), writes=(ar.A[16 + ob].buf,))

    p.dma("gpsimd", ar.W[0].h[:, :], wvT[0, :, :], reads=(wvb,), writes=(ar.W[0].buf,))
    for tb in range(16):
        pk = ar.bank()
        for c in range(FD):
            p.t(lambda E, c=c, tb=tb, pk=pk: E.matmul(
                pk.h[:, 0:256], rn[c].h[:, tb * 128:(tb + 1) * 128],
                ar.W[0].h[:, c * 256:(c + 1) * 256],
                start=(c == 0), stop=(c == FD - 1)),
                reads=(rn[c].buf, ar.W[0].buf), writes=(pk.buf,))
        vt = ar.A[18 + tb // 8]
        p.s(lambda E, pk=pk, vt=vt, tb=tb: E.activation(
            vt.h[:, (tb % 8) * 256:(tb % 8 + 1) * 256], pk.h[:, 0:256], AF.Copy),
            reads=(pk.buf,), writes=(vt.buf,))

    def gqa_iter(idx, h, qb, stage):
        pi = idx % 2
        sg = ar.Sg[pi]
        P_sb = sg.h[:, :].bitcast(BF16)
        Pt_sb = ar.M[pi]
        cl = ar.col[1 + pi]
        qa = ar.A[8 + h // 2]
        prow = 64 * (h % 2)
        ka = ar.A[20 + h % 2]
        krow = prow
        vis = (qb + 1) * 128
        nfull = vis // 512
        rem = vis - nfull * 512
        nch = qb + 1
        nb = nfull + (1 if rem else 0)
        widths = [512] * nfull + ([rem] if rem else [])
        dbank = (vis - 1) // 512
        doff = (vis - 128) - 512 * dbank
        if stage == 0:
            if qb == 0:
                kvh = h // 4
                ka_src = ar.A[16 + kvh // 2]
                skrow = 64 * (kvh % 2)
                p.s(lambda E, ka=ka, ka_src=ka_src, skrow=skrow, prow=prow: E.activation(
                    ka.h[prow:prow + 64, :], ka_src.h[skrow:skrow + 64, :], AF.Copy),
                    reads=(ka_src.buf,), writes=(ka.buf,))
            for kc in range(nb):
                pk = ar.P[kc]
                w = widths[kc]
                p.t(lambda E, pk=pk, kc=kc, w=w, qa=qa, ka=ka, krow=krow, prow=prow, qb=qb: E.matmul(
                    pk.h[:, 0:w], qa.h[prow:prow + 64, qb * 128:(qb + 1) * 128],
                    ka.h[krow:krow + 64, 512 * kc:512 * kc + w],
                    start=True, stop=True),
                    reads=(qa.buf, ka.buf), writes=(pk.buf,))
                if kc == dbank:
                    p.v(lambda E, pk=pk, doff=doff: E.tensor_add(
                        pk.h[:, doff:doff + 128], pk.h[:, doff:doff + 128], ar.tri.h[:, :]),
                        reads=(pk.buf, ar.tri.buf), writes=(pk.buf,))
                p.v(lambda E, pk=pk, kc=kc, w=w, cl=cl: E.reduce_max(
                    cl.h[:, 8 + kc:9 + kc], pk.h[:, 0:w], AX.X),
                    reads=(pk.buf,), writes=(cl.buf,))
        elif stage == 1:
            p.v(lambda E, cl=cl, nb=nb: E.reduce_max(
                cl.h[:, 0:1], cl.h[:, 8:8 + nb], AX.X, negate=True),
                reads=(cl.buf,), writes=(cl.buf,))
            p.group("scalar",
                    [(lambda E, pk=ar.P[kc], kc=kc, w=widths[kc], cl=cl, P_sb=P_sb: E.activation(
                        P_sb[:, 512 * kc:512 * kc + w], pk.h[:, 0:w], AF.Exp,
                        bias=cl.h[:, 0:1], accum_out=cl.h[:, 16 + kc:17 + kc]))
                     for kc in range(nb)],
                    reads=tuple(ar.P[kc].buf for kc in range(nb)) + (cl.buf,),
                    writes=(sg.buf, cl.buf))
            p.v(lambda E, cl=cl, nb=nb: E.reduce_sum(
                cl.h[:, 1:2], cl.h[:, 16:16 + nb], AX.X),
                reads=(cl.buf,), writes=(cl.buf,))
            p.v(lambda E, cl=cl: E.reciprocal(cl.h[:, 2:3], cl.h[:, 1:2]),
                reads=(cl.buf,), writes=(cl.buf,))
            p.v(lambda E, vis=vis, cl=cl, P_sb=P_sb, sg=sg: E.tensor_scalar(
                P_sb[:, 0:vis], P_sb[:, 0:vis], cl.h[:, 2:3], None, ALU.mult),
                reads=(sg.buf, cl.buf), writes=(sg.buf,))
        else:
            tp = ar.Tp[pi]
            for g in range((nch + 7) // 8):
                lo = g * 8
                hi = min(nch, lo + 8)
                for j in range(lo, hi):
                    p.t(lambda E, tp=tp, j=j, lo=lo, P_sb=P_sb, sg=sg: E.transpose(
                        tp.h[:, (j - lo) * 128:(j - lo + 1) * 128],
                        P_sb[:, j * 128:(j + 1) * 128], ar.ident.h[:, :]),
                        reads=(sg.buf, ar.ident.buf), writes=(tp.buf,))
                p.s(lambda E, tp=tp, lo=lo, hi=hi, Pt_sb=Pt_sb: E.activation(
                    Pt_sb.h[:, lo * 128:hi * 128], tp.h[:, 0:(hi - lo) * 128], AF.Copy),
                    reads=(tp.buf,), writes=(Pt_sb.buf,))
            pk = ar.P[4 + pi]
            for kc in range(nch):
                vt = ar.A[18 + kc // 8]
                voff = (kc % 8) * 256 + (h // 4) * 64
                p.t(lambda E, pk=pk, kc=kc, vt=vt, voff=voff, nch=nch, Pt_sb=Pt_sb: E.matmul(
                    pk.h[0:64, 0:128], vt.h[:, voff:voff + 64],
                    Pt_sb.h[:, kc * 128:(kc + 1) * 128],
                    start=(kc == 0), stop=(kc == nch - 1)),
                    reads=(vt.buf, Pt_sb.buf), writes=(pk.buf,))
            od = ar.A[24 + h // 2]
            p.s(lambda E, pk=pk, od=od, prow=prow, qb=qb: E.activation(
                od.h[prow:prow + 64, qb * 128:(qb + 1) * 128], pk.h[0:64, 0:128], AF.Copy),
                reads=(pk.buf,), writes=(od.buf,))

    seqg = [(h, qb) for h in range(H) for qb in range(16)]
    for idx, (h, qb) in enumerate(seqg):
        gqa_iter(idx, h, qb, 0)
        if idx:
            gqa_iter(idx - 1, *seqg[idx - 1], 2)
        gqa_iter(idx, h, qb, 1)
    gqa_iter(len(seqg) - 1, *seqg[-1], 2)

    for ob in range(FD):
        slot = ar.W[ob % 2]
        load_wtile(p, ar, slot, woT, wob, None, 8, ob)
        banks = [ar.bank() for _ in range(4)]
        for tc in range(4):
            sl = slice(512 * tc, 512 * (tc + 1))
            for c in range(FD):
                p.t(lambda E, c=c, sl=sl, pk=banks[tc], slot=slot: E.matmul(
                    pk.h[:, :], slot.h[:, c * 128:(c + 1) * 128], ar.A[24 + c].h[:, sl],
                    start=(c == 0), stop=(c == FD - 1)),
                    reads=(slot.buf, ar.A[24 + c].buf), writes=(banks[tc].buf,))
        for tc in range(4):
            sl = slice(512 * tc, 512 * (tc + 1))
            ht = ar.A[8 + 2 * ob + tc // 2]
            hv = f32view(ht)[:, (tc % 2) * 512:(tc % 2) * 512 + 512]
            p.v(lambda E, sl=sl, pk=banks[tc], ob=ob, hv=hv: E.tensor_add(
                hv, pk.h[:, :], ar.N[ob].h[:, sl]),
                reads=(banks[tc].buf, ar.N[ob].buf), writes=(ht.buf,))

    if FFN0_PHASE == 1:
        for ob in range(FD):
            p.dma("sync", x_dst[ob * 128:(ob + 1) * 128, 0:1024],
                  f32view(ar.A[8 + 2 * ob]), reads=(ar.A[8 + 2 * ob].buf,), writes=(xdb,))
            p.dma("sync", x_dst[ob * 128:(ob + 1) * 128, 1024:2048],
                  f32view(ar.A[9 + 2 * ob]), reads=(ar.A[9 + 2 * ob].buf,), writes=(xdb,))
        return

    def hint_f32(ob):
        return (f32view(ar.A[8 + 2 * ob]), f32view(ar.A[9 + 2 * ob]),
                ar.A[8 + 2 * ob], ar.A[9 + 2 * ob])

    for ob in range(FD):
        v0, v1, t0, t1 = hint_f32(ob)
        p.s(lambda E, ob=ob, v0=v0: E.activation(ar.N[ob].h[:, 0:1024], v0, AF.Copy),
            reads=(t0.buf,), writes=(ar.N[ob].buf,))
        p.s(lambda E, ob=ob, v1=v1: E.activation(ar.N[ob].h[:, 1024:2048], v1, AF.Copy),
            reads=(t1.buf,), writes=(ar.N[ob].buf,))
    emit_rms_stats(p, ar, ar.N, 1e-6, (ar.A[30], ar.A[31]))
    for i in range(FD):
        p.v(lambda E, i=i: E.scalar_tensor_tensor(
            ar.A[i].h[:, :], ar.N[i].h[:, :], ar.vec[2].h[:, 8 + i:9 + i], ar.M[0].h[:, :],
            ALU.mult, ALU.mult),
            reads=(ar.N[i].buf, ar.vec[2].buf, ar.M[0].buf), writes=(ar.A[i].buf,))

    for tb in range(16):
        pk = ar.bank()
        for c in range(FD):
            p.t(lambda E, c=c, tb=tb, pk=pk: E.matmul(
                pk.h[:, 0:8], ar.A[c].h[:, tb * 128:(tb + 1) * 128],
                ar.gwt.h[:, c * 8:(c + 1) * 8],
                start=(c == 0), stop=(c == FD - 1)),
                reads=(ar.A[c].buf, ar.gwt.buf), writes=(pk.buf,))
        L = ar.Sg[0].h[:, 0:8]
        p.s(lambda E, pk=pk: E.activation(L, pk.h[:, 0:8], AF.Copy),
            reads=(pk.buf,), writes=(ar.Sg[0].buf,))
        cl = ar.col[1].h
        p.v(lambda E: E.reduce_max(cl[:, 0:1], L, AX.X),
            reads=(ar.Sg[0].buf,), writes=(ar.col[1].buf,))
        m1k = ar.Sg[0].h[:, 8:16]
        p.v(lambda E: E.tensor_scalar(m1k, L, cl[:, 0:1], None, ALU.is_ge),
            reads=(ar.Sg[0].buf, ar.col[1].buf), writes=(ar.Sg[0].buf,))
        L2 = ar.Sg[0].h[:, 16:24]
        p.v(lambda E: E.scalar_tensor_tensor(L2, m1k, -1.0e30, L, ALU.mult, ALU.add),
            reads=(ar.Sg[0].buf,), writes=(ar.Sg[0].buf,))
        p.v(lambda E: E.reduce_max(cl[:, 1:2], L2, AX.X),
            reads=(ar.Sg[0].buf,), writes=(ar.col[1].buf,))
        m2k = ar.Sg[0].h[:, 24:32]
        p.v(lambda E: E.tensor_scalar(m2k, L2, cl[:, 1:2], None, ALU.is_ge),
            reads=(ar.Sg[0].buf, ar.col[1].buf), writes=(ar.Sg[0].buf,))
        p.v(lambda E: E.tensor_sub(cl[:, 2:3], cl[:, 1:2], cl[:, 0:1]),
            reads=(ar.col[1].buf,), writes=(ar.col[1].buf,))
        p.s(lambda E: E.activation(cl[:, 3:4], cl[:, 2:3], AF.Exp),
            reads=(ar.col[1].buf,), writes=(ar.col[1].buf,))
        p.v(lambda E: E.tensor_scalar_add(cl[:, 4:5], cl[:, 3:4], 1.0),
            reads=(ar.col[1].buf,), writes=(ar.col[1].buf,))
        p.v(lambda E: E.reciprocal(cl[:, 5:6], cl[:, 4:5]),
            reads=(ar.col[1].buf,), writes=(ar.col[1].buf,))
        p.v(lambda E: E.tensor_scalar_mul(cl[:, 6:7], cl[:, 5:6], 0.5),
            reads=(ar.col[1].buf,), writes=(ar.col[1].buf,))
        p.v(lambda E: E.tensor_scalar(cl[:, 7:8], cl[:, 6:7], -1.0, 0.5, ALU.mult, ALU.add),
            reads=(ar.col[1].buf,), writes=(ar.col[1].buf,))
        comb = ar.Sg[0].h[:, 32:40]
        p.v(lambda E: E.tensor_scalar(comb, m1k, cl[:, 6:7], None, ALU.mult),
            reads=(ar.Sg[0].buf, ar.col[1].buf), writes=(ar.Sg[0].buf,))
        tmp8 = ar.Sg[0].h[:, 40:48]
        p.v(lambda E: E.tensor_scalar(tmp8, m2k, cl[:, 7:8], None, ALU.mult),
            reads=(ar.Sg[0].buf, ar.col[1].buf), writes=(ar.Sg[0].buf,))
        p.v(lambda E: E.tensor_add(comb, comb, tmp8),
            reads=(ar.Sg[0].buf,), writes=(ar.Sg[0].buf,))
        p.s(lambda E: E.activation(ar.A[30].h[:, 0:8], comb, AF.Copy),
            reads=(ar.Sg[0].buf,), writes=(ar.A[30].buf,))
        p.t(lambda E: E.transpose(
            ar.Tp[0].h[0:8, 0:128], ar.A[30].h[:, 0:8], ar.ident.h[:, :]),
            reads=(ar.A[30].buf, ar.ident.buf), writes=(ar.Tp[0].buf,))
        p.s(lambda E, tb=tb: E.activation(
            ar.M[1].h[0:8, tb * 128:(tb + 1) * 128], ar.Tp[0].h[0:8, 0:128], AF.Copy),
            reads=(ar.Tp[0].buf,), writes=(ar.M[1].buf,))

    for e in range(8):
        for tc in range(4):
            sl = slice(512 * tc, 512 * (tc + 1))
            pk = ar.P[4 + tc % 2]
            p.t(lambda E, e=e, sl=sl, pk=pk: E.matmul(
                pk.h[:, :], ar.esel.h[0:8, e * 128:(e + 1) * 128], ar.M[1].h[0:8, sl],
                start=True, stop=True),
                reads=(ar.esel.buf, ar.M[1].buf), writes=(pk.buf,))
            p.s(lambda E, e=e, sl=sl, pk=pk: E.activation(
                ar.A[24 + e].h[:, sl], pk.h[:, :], AF.Copy),
                reads=(pk.buf,), writes=(ar.A[24 + e].buf,))

    if FFN0_PHASE == 2:
        for e in range(8):
            p.dma("gpsimd", x_dst[e * 128:(e + 1) * 128, :],
                  ar.A[24 + e].h[:, :], reads=(ar.A[24 + e].buf,), writes=(xdb,))
        return

    for ob in range(FD):
        p.dma("sync", ar.Sg[1].h[:, :], x_src[ob * 128:(ob + 1) * 128, :],
              reads=(xsb,), writes=(ar.Sg[1].buf,))
        v0, v1, t0, t1 = hint_f32(ob)
        p.v(lambda E, v0=v0: E.scalar_tensor_tensor(
            ar.Sg[0].h[:, 0:1024], v0, 0.5, ar.Sg[1].h[:, 0:1024], ALU.mult, ALU.add),
            reads=(t0.buf, ar.Sg[1].buf), writes=(ar.Sg[0].buf,))
        p.v(lambda E, v1=v1: E.scalar_tensor_tensor(
            ar.Sg[0].h[:, 1024:2048], v1, 0.5, ar.Sg[1].h[:, 1024:2048], ALU.mult, ALU.add),
            reads=(t1.buf, ar.Sg[1].buf), writes=(ar.Sg[0].buf,))
        for e in range(8):
            slot = ar.W[e % 2]
            p.dma("gpsimd", slot.h[:, 0:1024], expT[e, ob, :, :],
                  reads=(expb,), writes=(slot.buf,))
            banks = [ar.bank() for _ in range(4)]
            for tc in range(4):
                sl = slice(512 * tc, 512 * (tc + 1))
                for c in range(FD):
                    p.t(lambda E, c=c, sl=sl, pk=banks[tc], slot=slot: E.matmul(
                        pk.h[:, :], slot.h[:, c * 128:(c + 1) * 128], ar.A[c].h[:, sl],
                        start=(c == 0), stop=(c == FD - 1)),
                        reads=(slot.buf, ar.A[c].buf), writes=(banks[tc].buf,))
            p.group("vector",
                    [(lambda E, tc=tc, pk=banks[tc], e=e: E.tensor_mul(
                        ar.M[0].h[:, 512 * tc:512 * (tc + 1)], pk.h[:, :],
                        ar.A[24 + e].h[:, 512 * tc:512 * (tc + 1)]))
                     for tc in range(4)],
                    reads=(banks[0].buf, banks[1].buf, banks[2].buf, banks[3].buf,
                           ar.A[24 + e].buf),
                    writes=(ar.M[0].buf,))
            p.group("vector",
                    [(lambda E, tc=tc: E.tensor_add(
                        ar.Sg[0].h[:, 512 * tc:512 * (tc + 1)],
                        ar.Sg[0].h[:, 512 * tc:512 * (tc + 1)],
                        ar.M[0].h[:, 512 * tc:512 * (tc + 1)]))
                     for tc in range(4)],
                    reads=(ar.M[0].buf,), writes=(ar.Sg[0].buf,))
        p.dma("sync", x_dst[ob * 128:(ob + 1) * 128, :], ar.Sg[0].h[:, :],
              reads=(ar.Sg[0].buf,), writes=(xdb,))


def decl_ffn0(nc, wd, l):
    shapes = {
        "L0_ffn_wq": (8, 128, 1024), "L0_ffn_wk": (2, 128, 1024), "L0_ffn_wv": (1, 128, 2048),
        "L0_ffn_wo": (8, 128, 1024), "L0_ffn_gate_w": (1, 128, 64),
        "L0_ffn_expert_w": (8, 8, 128, 1024),
        "L0_ffn_attn_norm_w": (128, 8), "L0_ffn_ffn_norm_w": (128, 8),
    }
    for k, shp in shapes.items():
        if k not in wd:
            h = nc.dram_tensor(k, list(shp), F32, kind="ExternalInput")
            wd[k] = (h.ap(), Buf(k))


def prep_ffn0(inputs, l):
    d = {}
    d["L0_ffn_wq"] = _wtiles(_t(inputs["L0_ffn_wq"]), 8, 8)
    d["L0_ffn_wk"] = _wtiles(_t(inputs["L0_ffn_wk"]), 8, 2)
    d["L0_ffn_wv"] = _wtiles(_t(inputs["L0_ffn_wv"]), 8, 1, owid=256)
    d["L0_ffn_wo"] = _wtiles(_t(inputs["L0_ffn_wo"]), 8, 8)
    d["L0_ffn_gate_w"] = _wtiles(_t(inputs["L0_ffn_gate_w"]), 8, 1, owid=8)
    ew = np.asarray(inputs["L0_ffn_expert_w"])
    d["L0_ffn_expert_w"] = np.stack(
        [_wtiles(np.ascontiguousarray(ew[e].T), 8, 8) for e in range(8)])
    d["L0_ffn_attn_norm_w"] = _cols(inputs["L0_ffn_attn_norm_w"])
    d["L0_ffn_ffn_norm_w"] = _cols(inputs["L0_ffn_ffn_norm_w"])
    return d




L1_PHASE = "full"


def emit_attn_l1(p, ar, io, wd, l):
    x_src, xsb, x_dst, xdb = io
    wq, wqb = wd["L1_attn_wq"]
    wk, wkb = wd["L1_attn_wk"]
    wv, wvb = wd["L1_attn_wv"]
    wgq, wgqb = wd["L1_attn_wgq"]
    wgk, wgkb = wd["L1_attn_wgk"]
    wgv, wgvb = wd["L1_attn_wgv"]
    rro, rrob = wd["L1_attn_r_out_w"]
    wout, woutb = wd["L1_attn_w_out"]
    glw, glwb = wd["L1_attn_glw"]
    grw, grwb = wd["L1_attn_grw"]
    lmask, lmaskb = wd["L1_attn_lmask"]
    tril, trilb = wd["c_tril"]

    H, HD = 16, 64

    emit_layernorm(p, ar, x_src, xsb)

    p.dma("gpsimd", ar.gwt.h[:, 0:8], glw[:, :], reads=(glwb,), writes=(ar.gwt.buf,))
    p.dma("gpsimd", ar.gwt.h[:, 8:16], grw[:, :], reads=(grwb,), writes=(ar.gwt.buf,))

    def proj_fm(wT, wTb, n_ob, dst_base, scale=None):
        for ob in range(n_ob):
            slot = ar.W[ob % 2]
            load_wtile(p, ar, slot, wT, wTb, None, 8, ob)
            banks = [ar.bank() for _ in range(4)]
            for tc in range(4):
                sl = slice(512 * tc, 512 * (tc + 1))
                for c in range(FD):
                    p.t(lambda E, c=c, sl=sl, pk=banks[tc], slot=slot: E.matmul(
                        pk.h[:, :], slot.h[:, c * 128:(c + 1) * 128], ar.N[c].h[:, sl],
                        start=(c == 0), stop=(c == FD - 1)),
                        reads=(slot.buf, ar.N[c].buf), writes=(banks[tc].buf,))
            dst = ar.A[dst_base + ob]
            for tc in range(4):
                sl = slice(512 * tc, 512 * (tc + 1))
                p.s(lambda E, sl=sl, pk=banks[tc], dst=dst: E.activation(
                    dst.h[:, sl], pk.h[:, :], AF.Copy),
                    reads=(banks[tc].buf,), writes=(dst.buf,))

    proj_fm(wq, wqb, 8, 0)
    proj_fm(wk, wkb, 8, 8)

    for c in range(FD):
        vw = ar.A[24 + c // 2]
        off = (c % 2) * 1024
        p.dma("gpsimd", vw.h[:, off:off + 1024], wv[c * 128:(c + 1) * 128, :],
              reads=(wvb,), writes=(vw.buf,))
    for tb in range(16):
        for half in range(2):
            pk = ar.bank()
            for c in range(FD):
                vw = ar.A[24 + c // 2]
                off = (c % 2) * 1024 + half * 512
                p.t(lambda E, c=c, tb=tb, pk=pk, half=half, vw=vw, off=off: E.matmul(
                    pk.h[:, :], ar.N[c].h[:, tb * 128:(tb + 1) * 128],
                    vw.h[:, off:off + 512],
                    start=(c == 0), stop=(c == FD - 1)),
                    reads=(ar.N[c].buf, vw.buf), writes=(pk.buf,))
            vt = ar.A[16 + tb // 2]
            p.s(lambda E, pk=pk, vt=vt, tb=tb, half=half: E.activation(
                vt.h[:, (tb % 2) * 1024 + half * 512:(tb % 2) * 1024 + (half + 1) * 512],
                pk.h[:, :], AF.Copy),
                reads=(pk.buf,), writes=(vt.buf,))

    for tb in range(16):
        for h in range(H):
            pg = ar.P[4 + h % 2]
            qa = ar.A[h // 2]
            prow = 64 * (h % 2)
            p.t(lambda E, pg=pg, qa=qa, prow=prow, tb=tb, h=h: E.matmul(
                pg.h[:, h // 2:h // 2 + 1], qa.h[prow:prow + 64, tb * 128:(tb + 1) * 128],
                ar.gwt.h[prow:prow + 64, h // 2:h // 2 + 1],
                start=True, stop=True),
                reads=(qa.buf, ar.gwt.buf), writes=(pg.buf,))
        p.s(lambda E, tb=tb: E.activation(ar.vec[1].h[:, tb * 16:tb * 16 + 8],
                                          ar.P[4].h[:, 0:8], AF.Sigmoid),
            reads=(ar.P[4].buf,), writes=(ar.vec[1].buf,))
        p.s(lambda E, tb=tb: E.activation(ar.vec[1].h[:, tb * 16 + 8:tb * 16 + 16],
                                          ar.P[5].h[:, 0:8], AF.Sigmoid),
            reads=(ar.P[5].buf,), writes=(ar.vec[1].buf,))

    def loc_iter(idx, h, qb, stage):
        pi = idx % 2
        sg = ar.Sg[pi]
        P_sb = sg.h[:, :].bitcast(BF16)
        Pt_sb = ar.M[pi]
        cl = ar.col[1 + pi]
        qa = ar.A[h // 2]
        ka = ar.A[8 + h // 2]
        prow = 64 * (h % 2)
        gsel = h // 2
        k0 = max(0, qb - 2)
        nch = qb - k0 + 1
        vis = nch * 128
        if stage == 0:
            pk = ar.P[2 * pi]
            p.t(lambda E, pk=pk, qa=qa, ka=ka, prow=prow, qb=qb, k0=k0, vis=vis: E.matmul(
                pk.h[:, 0:vis], qa.h[prow:prow + 64, qb * 128:(qb + 1) * 128],
                ka.h[prow:prow + 64, k0 * 128:k0 * 128 + vis],
                start=True, stop=True),
                reads=(qa.buf, ka.buf), writes=(pk.buf,))
            p.s(lambda E, pk=pk, vis=vis, sg=sg: E.activation(
                sg.h[:, 256:256 + vis], pk.h[:, 0:vis], AF.Copy, scale=0.125),
                reads=(pk.buf,), writes=(sg.buf,))
        elif stage == 1:
            if qb >= 2:
                p.v(lambda E, sg=sg: E.tensor_add(
                    sg.h[:, 256:384], sg.h[:, 256:384], ar.trili.h[:, :]),
                    reads=(sg.buf, ar.trili.buf), writes=(sg.buf,))
            p.v(lambda E, vis=vis, sg=sg: E.tensor_add(
                sg.h[:, 256 + vis - 128:256 + vis],
                sg.h[:, 256 + vis - 128:256 + vis], ar.tri.h[:, :]),
                reads=(sg.buf, ar.tri.buf), writes=(sg.buf,))
            p.v(lambda E, vis=vis, sg=sg, cl=cl: E.reduce_max(
                cl.h[:, 0:1], sg.h[:, 256:256 + vis], AX.X, negate=True),
                reads=(sg.buf,), writes=(cl.buf,))
            p.s(lambda E, vis=vis, sg=sg, cl=cl, P_sb=P_sb: E.activation(
                P_sb[:, 0:vis], sg.h[:, 256:256 + vis], AF.Exp,
                bias=cl.h[:, 0:1], accum_out=cl.h[:, 1:2]),
                reads=(sg.buf, cl.buf), writes=(sg.buf, cl.buf))
            p.v(lambda E, cl=cl: E.reciprocal(cl.h[:, 2:3], cl.h[:, 1:2]),
                reads=(cl.buf,), writes=(cl.buf,))
            gli = qb * 16 + (8 if h % 2 else 0) + h // 2
            p.v(lambda E, cl=cl, gli=gli: E.tensor_mul(
                cl.h[:, 3:4], cl.h[:, 2:3], ar.vec[1].h[:, gli:gli + 1]),
                reads=(cl.buf, ar.vec[1].buf), writes=(cl.buf,))
            p.v(lambda E, vis=vis, cl=cl, P_sb=P_sb, sg=sg: E.tensor_scalar(
                P_sb[:, 0:vis], P_sb[:, 0:vis], cl.h[:, 3:4], None, ALU.mult),
                reads=(sg.buf, cl.buf), writes=(sg.buf,))
        else:
            tp = ar.Tp[pi]
            for j in range(nch):
                p.t(lambda E, tp=tp, j=j, P_sb=P_sb, sg=sg: E.transpose(
                    tp.h[:, j * 128:(j + 1) * 128], P_sb[:, j * 128:(j + 1) * 128],
                    ar.ident.h[:, :]),
                    reads=(sg.buf, ar.ident.buf), writes=(tp.buf,))
            p.v(lambda E, tp=tp, nch=nch, Pt_sb=Pt_sb: E.tensor_copy(
                Pt_sb.h[:, 0:nch * 128], tp.h[:, 0:nch * 128]),
                reads=(tp.buf,), writes=(Pt_sb.buf,))
            pk2 = ar.P[4 + pi]
            for j in range(nch):
                kb = k0 + j
                vt = ar.A[16 + kb // 2]
                voff = (kb % 2) * 1024 + h * 64
                p.t(lambda E, pk2=pk2, j=j, vt=vt, voff=voff, nch=nch, Pt_sb=Pt_sb: E.matmul(
                    pk2.h[0:64, 0:128], vt.h[:, voff:voff + 64],
                    Pt_sb.h[:, j * 128:(j + 1) * 128],
                    start=(j == 0), stop=(j == nch - 1)),
                    reads=(vt.buf, Pt_sb.buf), writes=(pk2.buf,))
            od = ar.A[24 + h // 2]
            if L1_PHASE == "remote":
                p.v(lambda E, od=od, prow=prow, qb=qb, pk2=pk2: E.memset(
                    od.h[prow:prow + 64, qb * 128:(qb + 1) * 128], 0.0),
                    reads=(pk2.buf,), writes=(od.buf,))
            else:
                p.s(lambda E, pk2=pk2, od=od, prow=prow, qb=qb: E.activation(
                    od.h[prow:prow + 64, qb * 128:(qb + 1) * 128], pk2.h[0:64, 0:128], AF.Copy),
                    reads=(pk2.buf,), writes=(od.buf,))

    seql = [(h, qb) for h in range(H) for qb in range(16)]
    for idx, (h, qb) in enumerate(seql):
        loc_iter(idx, h, qb, 0)
        if idx:
            loc_iter(idx - 1, *seql[idx - 1], 2)
        loc_iter(idx, h, qb, 1)
    loc_iter(len(seql) - 1, *seql[-1], 2)

    for c in range(FD):
        p.v(lambda E, c=c: E.reduce_sum(
            ar.col[3].h[:, 0:64],
            ar.N[c].h[:, :].rearrange("p (n w) -> p n w", w=32), AX.X),
            reads=(ar.N[c].buf,), writes=(ar.col[3].buf,))
        p.v(lambda E, c=c: E.tensor_scalar_mul(
            ar.vec[1].h[:, :].bitcast(BF16)[:, c * 64:(c + 1) * 64],
            ar.col[3].h[:, 0:64], 1.0 / 32.0),
            reads=(ar.col[3].buf,), writes=(ar.vec[1].buf,))

    proj_fm(wgq, wgqb, 4, 8)
    gkt = ar.vec[2].h[:, :].bitcast(BF16)
    for ob in range(4):
        slot = ar.W[1]
        load_wtile(p, ar, slot, wgk, wgkb, None, 8, ob)
        pk = ar.bank()
        for c in range(FD):
            p.t(lambda E, c=c, pk=pk, slot=slot: E.matmul(
                pk.h[:, 0:64], slot.h[:, c * 128:(c + 1) * 128],
                ar.vec[1].h[:, :].bitcast(BF16)[:, c * 64:(c + 1) * 64],
                start=(c == 0), stop=(c == FD - 1)),
                reads=(slot.buf, ar.vec[1].buf), writes=(pk.buf,))
        p.s(lambda E, pk=pk, ob=ob: E.activation(
            gkt[:, ob * 64:(ob + 1) * 64], pk.h[:, 0:64], AF.Copy),
            reads=(pk.buf,), writes=(ar.vec[2].buf,))
    gvt = ar.vec[3].h[:, :].bitcast(BF16)
    pk = ar.bank()
    slot = ar.W[1]
    for half in range(2):
        p.dma("gpsimd", slot.h[:, :], wgv[0, half, :, :], reads=(wgvb,), writes=(slot.buf,))
        for c in range(4):
            cc = half * 4 + c
            p.t(lambda E, c=c, cc=cc, pk=pk, slot=slot, half=half: E.matmul(
                pk.h[0:64, 0:512], ar.vec[1].h[:, :].bitcast(BF16)[:, cc * 64:(cc + 1) * 64],
                slot.h[:, c * 512:(c + 1) * 512],
                start=(cc == 0), stop=(cc == 7)),
                reads=(ar.vec[1].buf, slot.buf), writes=(pk.buf,))
    p.s(lambda E, pk=pk: E.activation(gvt[0:64, 0:512], pk.h[0:64, 0:512], AF.Copy),
        reads=(pk.buf,), writes=(ar.vec[3].buf,))

    for tb in range(16):
        for h in range(H):
            pg = ar.P[4 + h % 2]
            qa = ar.A[h // 2]
            prow = 64 * (h % 2)
            p.t(lambda E, pg=pg, qa=qa, prow=prow, tb=tb, h=h: E.matmul(
                pg.h[:, h // 2:h // 2 + 1], qa.h[prow:prow + 64, tb * 128:(tb + 1) * 128],
                ar.gwt.h[prow:prow + 64, 8 + h // 2:9 + h // 2],
                start=True, stop=True),
                reads=(qa.buf, ar.gwt.buf), writes=(pg.buf,))
        p.s(lambda E: E.activation(ar.col[2].h[:, 0:8], ar.P[4].h[:, 0:8], AF.Sigmoid),
            reads=(ar.P[4].buf,), writes=(ar.col[2].buf,))
        p.s(lambda E: E.activation(ar.col[2].h[:, 8:16], ar.P[5].h[:, 0:8], AF.Sigmoid),
            reads=(ar.P[5].buf,), writes=(ar.col[2].buf,))
        gt = ar.M[0]
        p.v(lambda E, gt=gt: E.tensor_copy(gt.h[:, 0:16], ar.col[2].h[:, 0:16]),
            reads=(ar.col[2].buf,), writes=(gt.buf,))
        p.t(lambda E, gt=gt: E.transpose(
            ar.Tp[0].h[0:16, 0:128], gt.h[:, 0:16], ar.ident.h[:, :]),
            reads=(gt.buf, ar.ident.buf), writes=(ar.Tp[0].buf,))
        p.s(lambda E, tb=tb: E.activation(
            ar.M[1].h[32:48, tb * 128:(tb + 1) * 128], ar.Tp[0].h[0:16, 0:128], AF.Copy),
            reads=(ar.Tp[0].buf,), writes=(ar.M[1].buf,))

    P_sb = ar.Sg[1].h[:, :].bitcast(BF16)
    for qb in range(16):
        p.dma("sync", ar.col[2].h[:, 0:64], lmask[qb, :, :], reads=(lmaskb,),
              writes=(ar.col[2].buf,))
        for lh in range(8):
            ga = ar.A[8 + lh // 2]
            prow = 64 * (lh % 2)
            pk = ar.P[4 + lh % 2]
            p.t(lambda E, pk=pk, ga=ga, prow=prow, qb=qb, lh=lh: E.matmul(
                pk.h[:, 0:64], ga.h[prow:prow + 64, qb * 128:(qb + 1) * 128],
                ar.vec[2].h[:, :].bitcast(BF16)[prow:prow + 64, (lh // 2) * 64:(lh // 2 + 1) * 64],
                start=True, stop=True),
                reads=(ga.buf, ar.vec[2].buf), writes=(pk.buf,))
            p.s(lambda E, pk=pk: E.activation(
                ar.Sg[0].h[:, 1024:1088], pk.h[:, 0:64], AF.Copy, scale=0.125),
                reads=(pk.buf,), writes=(ar.Sg[0].buf,))
            p.v(lambda E: E.tensor_add(ar.Sg[0].h[:, 1024:1088], ar.Sg[0].h[:, 1024:1088],
                                       ar.col[2].h[:, 0:64]),
                reads=(ar.Sg[0].buf, ar.col[2].buf), writes=(ar.Sg[0].buf,))
            p.v(lambda E: E.reduce_max(ar.col[1].h[:, 0:1], ar.Sg[0].h[:, 1024:1088],
                                       AX.X, negate=True),
                reads=(ar.Sg[0].buf,), writes=(ar.col[1].buf,))
            p.s(lambda E: E.activation(
                P_sb[:, 0:64], ar.Sg[0].h[:, 1024:1088], AF.Exp,
                bias=ar.col[1].h[:, 0:1], accum_out=ar.col[1].h[:, 1:2]),
                reads=(ar.Sg[0].buf, ar.col[1].buf), writes=(ar.Sg[1].buf, ar.col[1].buf))
            p.v(lambda E: E.reciprocal(ar.col[1].h[:, 2:3], ar.col[1].h[:, 1:2]),
                reads=(ar.col[1].buf,), writes=(ar.col[1].buf,))
            p.v(lambda E: E.tensor_scalar(
                P_sb[:, 0:64], P_sb[:, 0:64], ar.col[1].h[:, 2:3], None, ALU.mult),
                reads=(ar.Sg[1].buf, ar.col[1].buf), writes=(ar.Sg[1].buf,))
            p.t(lambda E: E.transpose(
                ar.Tp[0].h[0:64, 0:128], P_sb[:, 0:64], ar.ident.h[:, :]),
                reads=(ar.Sg[1].buf, ar.ident.buf), writes=(ar.Tp[0].buf,))
            p.s(lambda E: E.activation(ar.M[0].h[0:64, 0:128], ar.Tp[0].h[0:64, 0:128],
                                       AF.Copy),
                reads=(ar.Tp[0].buf,), writes=(ar.M[0].buf,))
            pk2 = ar.bank()
            p.t(lambda E, pk2=pk2, lh=lh: E.matmul(
                pk2.h[0:64, 0:128], gvt[0:64, lh * 64:(lh + 1) * 64],
                ar.M[0].h[0:64, 0:128],
                start=True, stop=True),
                reads=(ar.vec[3].buf, ar.M[0].buf), writes=(pk2.buf,))
            rd = ar.A[12 + lh // 2]
            p.s(lambda E, pk2=pk2, rd=rd, prow=prow, qb=qb: E.activation(
                rd.h[prow:prow + 64, qb * 128:(qb + 1) * 128], pk2.h[0:64, 0:128], AF.Copy),
                reads=(pk2.buf,), writes=(rd.buf,))

    if L1_PHASE == "lat_dbg":
        p.dma("gpsimd", x_dst[384:512, :], ar.N[0].h[:, :],
              reads=(ar.N[0].buf,), writes=(xdb,))
        p.dma("sync", x_dst[512:640, 0:512], ar.Sg[0].h[:, 1024:1536],
              reads=(ar.Sg[0].buf,), writes=(xdb,))
        gk_f = ar.vec[2].h[:, :].bitcast(BF16)
        gv_f = ar.vec[3].h[:, :].bitcast(BF16)
        p.dma("gpsimd", x_dst[0:128, 0:512], gk_f[:, 0:512],
              reads=(ar.vec[2].buf,), writes=(xdb,))
        p.dma("gpsimd", x_dst[128:256, 0:512], gv_f[:, 0:512],
              reads=(ar.vec[3].buf,), writes=(xdb,))
        p.dma("gpsimd", x_dst[256:384, 0:512], ar.W[0].h[:, 0:512],
              reads=(ar.W[0].buf,), writes=(xdb,))
        return

    if L1_PHASE == "rem_raw":
        for i in range(4):
            p.dma("gpsimd", x_dst[i * 128:(i + 1) * 128, :], ar.A[12 + i].h[:, :],
                  reads=(ar.A[12 + i].buf,), writes=(xdb,))
        return

    for ob in ([] if L1_PHASE == "local" else range(FD)):
        slot = ar.W[ob % 2]
        p.dma("gpsimd", slot.h[:, 0:512], rro[ob, :, :], reads=(rrob,), writes=(slot.buf,))
        for tc in range(4):
            sl = slice(512 * tc, 512 * (tc + 1))
            pg = ar.P[4 + tc % 2]
            p.t(lambda E, ob=ob, sl=sl, pg=pg: E.matmul(
                pg.h[:, :], ar.esel.h[32:48, ob * 128:(ob + 1) * 128],
                ar.M[1].h[32:48, sl], start=True, stop=True),
                reads=(ar.esel.buf, ar.M[1].buf), writes=(pg.buf,))
            p.s(lambda E, sl=sl, pg=pg: E.activation(
                ar.M[0].h[:, sl], pg.h[:, :], AF.Copy),
                reads=(pg.buf,), writes=(ar.M[0].buf,))
        banks = [ar.bank() for _ in range(4)]
        for tc in range(4):
            sl = slice(512 * tc, 512 * (tc + 1))
            for c in range(4):
                p.t(lambda E, c=c, sl=sl, pk=banks[tc], slot=slot: E.matmul(
                    pk.h[:, :], slot.h[:, c * 128:(c + 1) * 128], ar.A[12 + c].h[:, sl],
                    start=(c == 0), stop=(c == 3)),
                    reads=(slot.buf, ar.A[12 + c].buf), writes=(banks[tc].buf,))
        for tc in range(4):
            sl = slice(512 * tc, 512 * (tc + 1))
            pk = banks[tc]
            tmp = ar.Sg[1].h[:, :].bitcast(BF16)[:, 0:512]
            p.v(lambda E, sl=sl, pk=pk, tmp=tmp: E.tensor_mul(tmp, pk.h[:, :], ar.M[0].h[:, sl]),
                reads=(pk.buf, ar.M[0].buf), writes=(ar.Sg[1].buf,))
            p.v(lambda E, sl=sl, ob=ob, tmp=tmp: E.tensor_add(
                ar.A[24 + ob].h[:, sl], ar.A[24 + ob].h[:, sl], tmp),
                reads=(ar.Sg[1].buf, ar.A[24 + ob].buf), writes=(ar.A[24 + ob].buf,))

    for ob in range(FD):
        slot = ar.W[ob % 2]
        load_wtile(p, ar, slot, wout, woutb, None, 8, ob)
        p.dma("sync", ar.Sg[1].h[:, :], x_src[ob * 128:(ob + 1) * 128, :],
              reads=(xsb,), writes=(ar.Sg[1].buf,))
        banks = [ar.bank() for _ in range(4)]
        for tc in range(4):
            sl = slice(512 * tc, 512 * (tc + 1))
            for c in range(FD):
                p.t(lambda E, c=c, sl=sl, pk=banks[tc], slot=slot: E.matmul(
                    pk.h[:, :], slot.h[:, c * 128:(c + 1) * 128], ar.A[24 + c].h[:, sl],
                    start=(c == 0), stop=(c == FD - 1)),
                    reads=(slot.buf, ar.A[24 + c].buf), writes=(banks[tc].buf,))
        for tc in range(4):
            sl = slice(512 * tc, 512 * (tc + 1))
            pk = banks[tc]
            p.v(lambda E, sl=sl, pk=pk: E.tensor_add(
                ar.Sg[0].h[:, sl], pk.h[:, :], ar.Sg[1].h[:, sl]),
                reads=(pk.buf, ar.Sg[1].buf), writes=(ar.Sg[0].buf,))
        p.dma("sync", x_dst[ob * 128:(ob + 1) * 128, :], ar.Sg[0].h[:, :],
              reads=(ar.Sg[0].buf,), writes=(xdb,))


def decl_attn_l1(nc, wd, l):
    shapes = {
        "L1_attn_wq": ((8, 128, 1024), F32), "L1_attn_wk": ((8, 128, 1024), F32),
        "L1_attn_wv": ((1024, 1024), F32),
        "L1_attn_wgq": ((4, 128, 1024), F32), "L1_attn_wgk": ((4, 128, 1024), F32),
        "L1_attn_wgv": ((1, 2, 128, 2048), F32),
        "L1_attn_r_out_w": ((8, 128, 512), F32), "L1_attn_w_out": ((8, 128, 1024), F32),
        "L1_attn_glw": ((128, 8), F32), "L1_attn_grw": ((128, 8), F32),
        "L1_attn_lmask": ((16, 128, 64), F32),
        "c_tril": ((128, 128), F32),
    }
    for k, (shp, dt) in shapes.items():
        if k not in wd:
            h = nc.dram_tensor(k, list(shp), dt, kind="ExternalInput")
            wd[k] = (h.ap(), Buf(k))


def prep_attn_l1(inputs, l):
    d = {}
    d["L1_attn_wv"] = np.ascontiguousarray(np.asarray(inputs["L1_attn_wv"], np.float32))
    d["L1_attn_wgv"] = _wtiles(inputs["L1_attn_wgv"], 8, 1, owid=512, halves=2)
    d["L1_attn_wq"] = _wtiles(inputs["L1_attn_wq"], 8, 8)
    d["L1_attn_wk"] = _wtiles(inputs["L1_attn_wk"], 8, 8)
    d["L1_attn_wgq"] = _wtiles(inputs["L1_attn_wgq"], 8, 4)
    d["L1_attn_wgk"] = _wtiles(inputs["L1_attn_wgk"], 8, 4)
    d["L1_attn_r_out_w"] = _wtiles(inputs["L1_attn_r_out_w"], 4, 8)
    d["L1_attn_w_out"] = _wtiles(inputs["L1_attn_w_out"], 8, 8)
    def gcols(g):
        g = np.asarray(g, np.float32)
        out = np.zeros((128, 8), np.float32)
        for h in range(16):
            out[64 * (h % 2):64 * (h % 2) + 64, h // 2] = g[h]
        return out
    d["L1_attn_glw"] = gcols(inputs["L1_attn_gate_local_w"])
    d["L1_attn_grw"] = gcols(inputs["L1_attn_gate_remote_w"])
    lm = np.zeros((16, 128, 64), np.float32)
    for qb in range(16):
        for iq in range(128):
            lim = qb * 4 + iq // 32
            lm[qb, iq, lim + 1:] = -1.0e30
    d["L1_attn_lmask"] = lm
    d["c_tril"] = np.tril(np.full((128, 128), -1.0e30, np.float32))
    return d




def emit_attn_l0(p, ar, io, wd, l):
    x_src, xsb, x_dst, xdb = io
    ipT, ipb = wd["L0_attn_in_proj_w"]
    opT, opb = wd["L0_attn_out_proj_w"]
    nrm, nrmb = wd["L0_attn_norm_w"]
    cwc, cwcb = wd["L0_attn_conv_w"]
    cbc, cbcb = wd["L0_attn_conv_b"]
    dcol, dcolb = wd["L0_attn_D"]
    alog, alogb = wd["L0_attn_A_log"]
    dtb, dtbb = wd["L0_attn_dt_bias"]
    g1buf, g1b = wd["g1buf"]
    NH, HD, DS = 32, 64, 128

    emit_layernorm(p, ar, x_src, xsb)

    p.dma("sync", ar.vec[2].h[:, 0:8], nrm[:, :], reads=(nrmb,), writes=(ar.vec[2].buf,))
    p.dma("sync", ar.vec[2].h[:, 8:72], cwc[:, :], reads=(cwcb,), writes=(ar.vec[2].buf,))
    p.dma("sync", ar.vec[2].h[:, 72:88], cbc[:, :], reads=(cbcb,), writes=(ar.vec[2].buf,))
    p.dma("sync", ar.vec[2].h[:, 88:104], dcol[:, :], reads=(dcolb,), writes=(ar.vec[2].buf,))
    p.dma("sync", ar.col[0].h[0:32, 0:1], alog[:, :], reads=(alogb,), writes=(ar.col[0].buf,))
    p.dma("sync", ar.col[0].h[0:32, 1:2], dtb[:, :], reads=(dtbb,), writes=(ar.col[0].buf,))

    emit_rms_stats(p, ar, ar.N, 1e-6, (ar.A[30], ar.A[31]))
    for i in range(FD):
        p.v(lambda E, i=i: E.scalar_tensor_tensor(
            ar.A[i].h[:, :], ar.N[i].h[:, :], ar.vec[2].h[:, i:i + 1], ar.M[0].h[:, :],
            ALU.mult, ALU.mult),
            reads=(ar.N[i].buf, ar.vec[2].buf, ar.M[0].buf), writes=(ar.A[i].buf,))
    rn = ar.A[0:8]

    stg = ar.Sg[1].h[:, :].bitcast(BF16)
    ipL, ipLb = wd["L0_attn_in_proj_last"]
    for ob in range(35):
        owid = 32 if ob == 34 else 128
        slot = ar.W[ob % 2]
        if ob == 34:
            p.dma("gpsimd", slot.h[:, 0:256], ipL[:, :], reads=(ipLb,), writes=(slot.buf,))
        else:
            load_wtile(p, ar, slot, ipT, ipb, None, 8, ob)
        banks = [ar.bank() for _ in range(4)]
        for tc in range(4):
            sl = slice(512 * tc, 512 * (tc + 1))
            for c in range(FD):
                p.t(lambda E, c=c, sl=sl, pk=banks[tc], slot=slot, owid=owid: E.matmul(
                    pk.h[0:owid, :], slot.h[:, c * owid:(c + 1) * owid], rn[c].h[:, sl],
                    start=(c == 0), stop=(c == FD - 1)),
                    reads=(slot.buf, rn[c].buf), writes=(banks[tc].buf,))
        if ob < 16:
            for tc in range(4):
                sl = slice(512 * tc, 512 * (tc + 1))
                ssl = slice(1024 * (tc % 2), 1024 * (tc % 2) + 512)
                p.s(lambda E, pk=banks[tc], ssl=ssl: E.activation(
                    stg[:, ssl], pk.h[:, :], AF.Copy),
                    reads=(banks[tc].buf,), writes=(ar.Sg[1].buf,))
                p.dma("sync", g1buf[ob * 128:(ob + 1) * 128, sl], stg[:, ssl],
                      reads=(ar.Sg[1].buf,), writes=(g1b,))
        else:
            if ob < 32:
                dst, drows = ar.A[8 + ob - 16], slice(0, 128)
            elif ob == 32:
                dst, drows = ar.A[24], slice(0, 128)
            elif ob == 33:
                dst, drows = ar.A[25], slice(0, 128)
            else:
                dst, drows = ar.A[26], slice(0, 32)
            for tc in range(4):
                sl = slice(512 * tc, 512 * (tc + 1))
                p.s(lambda E, pk=banks[tc], dst=dst, sl=sl, drows=drows, owid=owid: E.activation(
                    dst.h[drows, sl], pk.h[0:owid, :], AF.Copy),
                    reads=(banks[tc].buf,), writes=(dst.buf,))

    for i in range(16):
        xt = ar.A[8 + i]
        acc = ar.Sg[0].h[:, :]
        p.v(lambda E, xt=xt, i=i: E.tensor_scalar(
            acc, xt.h[:, :], ar.vec[2].h[:, 8 + 3 * 16 + i:9 + 3 * 16 + i],
            ar.vec[2].h[:, 72 + i:73 + i], ALU.mult, ALU.add),
            reads=(xt.buf, ar.vec[2].buf), writes=(ar.Sg[0].buf,))
        for k in range(3):
            d = 3 - k
            p.v(lambda E, xt=xt, i=i, k=k, d=d: E.scalar_tensor_tensor(
                acc[:, d:2048], xt.h[:, 0:2048 - d],
                ar.vec[2].h[:, 8 + k * 16 + i:9 + k * 16 + i], acc[:, d:2048],
                ALU.mult, ALU.add),
                reads=(xt.buf, ar.vec[2].buf, ar.Sg[0].buf), writes=(ar.Sg[0].buf,))
        p.s(lambda E, xt=xt: E.activation(xt.h[:, :], acc, AF.Silu),
            reads=(ar.Sg[0].buf,), writes=(xt.buf,))

    dtraw = ar.A[26]
    b0 = ar.Sg[1].h[0:32, :]
    b1 = ar.Sg[1].h[32:64, :]
    lam = ar.Sg[1].h[64:96, :]
    p.v(lambda E: E.tensor_scalar(b0, dtraw.h[0:32, :], ar.col[0].h[0:32, 1:2], 80.0,
                                  ALU.add, ALU.min),
        reads=(dtraw.buf, ar.col[0].buf), writes=(ar.Sg[1].buf,))
    p.s(lambda E: E.activation(b1, b0, AF.Exp),
        reads=(ar.Sg[1].buf,), writes=(ar.Sg[1].buf,))
    p.s(lambda E: E.activation(b0, b1, AF.Ln, bias=1.0),
        reads=(ar.Sg[1].buf,), writes=(ar.Sg[1].buf,))
    p.s(lambda E: E.activation(ar.col[0].h[0:32, 2:3], ar.col[0].h[0:32, 0:1], AF.Exp),
        reads=(ar.col[0].buf,), writes=(ar.col[0].buf,))
    p.v(lambda E: E.tensor_scalar_mul(ar.col[0].h[0:32, 3:4], ar.col[0].h[0:32, 2:3], -1.0),
        reads=(ar.col[0].buf,), writes=(ar.col[0].buf,))
    p.v(lambda E: E.tensor_scalar(lam, b0, ar.col[0].h[0:32, 3:4], None, ALU.mult),
        reads=(ar.Sg[1].buf, ar.col[0].buf), writes=(ar.Sg[1].buf,))
    F27 = ar.A[27].h[:, :].bitcast(F32)
    p.v(lambda E: E.memset(F27[64:96, 0:128], 1.0), writes=(ar.A[27].buf,))

    H = ar.A[28]
    p.v(lambda E: E.memset(H.h[:, :], 0.0), writes=(H.buf,))
    scr = ar.A[29]
    GT = ar.Sg[0].h[:, 0:128]
    DEC = ar.Sg[0].h[:, 128:256]
    RB = ar.Sg[0].h[:, 256:384]
    for c in range(16):
        csl = slice(128 * c, 128 * (c + 1))
        lam_c = ar.Sg[1].h[64:96, 128 * c:128 * (c + 1)]
        LAM = F27[64:96, 128:256]
        AEXP = F27[64:96, 256:384]
        BDEC = F27[64:96, 384:512]
        p.v(lambda E, lam_c=lam_c, LAM=LAM: E.tensor_tensor_scan(
            LAM, F27[64:96, 0:128], lam_c, 0.0, ALU.mult, ALU.add),
            reads=(ar.A[27].buf, ar.Sg[1].buf), writes=(ar.A[27].buf,))
        p.s(lambda E, LAM=LAM, AEXP=AEXP: E.activation(AEXP, LAM, AF.Exp),
            reads=(ar.A[27].buf,), writes=(ar.A[27].buf,))
        p.v(lambda E, LAM=LAM, BDEC=BDEC: E.tensor_scalar(
            BDEC, LAM, F27[64:96, 255:256], None, ALU.subtract),
            reads=(ar.A[27].buf,), writes=(ar.A[27].buf,))
        p.s(lambda E, BDEC=BDEC: E.activation(BDEC, BDEC, AF.Exp, scale=-1.0),
            reads=(ar.A[27].buf,), writes=(ar.A[27].buf,))
        LAMT = F27[:, 512:544]
        AEXPT = F27[:, 544:576]
        BDECT = F27[:, 576:608]
        DTT = F27[:, 608:640]
        DTOT = F27[:, 640:672]
        for (srcap, dstap) in ((LAM, LAMT), (AEXP, AEXPT), (BDEC, BDECT)):
            p.t(lambda E, srcap=srcap: E.transpose(
                ar.P[4].h[:, 0:32], srcap, ar.identf.h[64:96, 64:96]),
                reads=(ar.A[27].buf, ar.identf.buf), writes=(ar.P[4].buf,))
            p.s(lambda E, dstap=dstap: E.activation(dstap, ar.P[4].h[:, 0:32], AF.Copy),
                reads=(ar.P[4].buf,), writes=(ar.A[27].buf,))
        p.t(lambda E, csl=csl: E.transpose(
            ar.P[4].h[:, 0:32], ar.Sg[1].h[0:32, csl], ar.identf.h[0:32, 0:32]),
            reads=(ar.Sg[1].buf, ar.identf.buf), writes=(ar.P[4].buf,))
        p.s(lambda E: E.activation(DTT, ar.P[4].h[:, 0:32], AF.Copy),
            reads=(ar.P[4].buf,), writes=(ar.A[27].buf,))
        p.t(lambda E: E.matmul(ar.P[4].h[:, 0:32], ar.onesf.h[:, :],
                               AEXPT, start=True, stop=True),
            reads=(ar.onesf.buf, ar.A[27].buf), writes=(ar.P[4].buf,))
        p.s(lambda E: E.activation(DTOT, ar.P[4].h[:, 0:32], AF.Copy),
            reads=(ar.P[4].buf,), writes=(ar.A[27].buf,))
        p.t(lambda E, csl=csl: E.matmul(
            ar.P[5].h[:, 0:128], ar.A[24].h[:, csl], ar.A[25].h[:, csl],
            start=True, stop=True),
            reads=(ar.A[24].buf, ar.A[25].buf), writes=(ar.P[5].buf,))
        p.s(lambda E: E.activation(GT, ar.P[5].h[:, 0:128], AF.Copy),
            reads=(ar.P[5].buf,), writes=(ar.Sg[0].buf,))
        p.t(lambda E, csl=csl: E.transpose(
            ar.Tp[1].h[:, 0:128], ar.A[24].h[:, csl], ar.ident.h[:, :]),
            reads=(ar.A[24].buf, ar.ident.buf), writes=(ar.Tp[1].buf,))
        p.s(lambda E: E.activation(scr.h[:, 256:384], ar.Tp[1].h[:, 0:128], AF.Copy),
            reads=(ar.Tp[1].buf,), writes=(scr.buf,))
        def m2_iter(h, stage):
            pi = h % 2
            xt = ar.A[8 + h // 2]
            xrow = 64 * (h % 2)
            if pi == 0:
                RBp = ar.Sg[0].h[:, 256:384]
                DECp = ar.Sg[0].h[:, 128:256]
                rb_buf = dec_buf = ar.Sg[0].buf
                Mp = scr.h[:, 0:128]
                dtxp = scr.h[:, 128:192]
                btp = scr.h[:, 384:512]
                m_buf = ar.A[29].buf
                ycp = ar.Sg[0].h[:, 384:448]
                yc_buf = ar.Sg[0].buf
            else:
                A0 = ar.A[0]
                A1f = ar.A[1].h[:, :].bitcast(F32)
                RBp = A1f[:, 0:128]
                DECp = A1f[:, 128:256]
                rb_buf = dec_buf = ar.A[1].buf
                Mp = A0.h[:, 0:128]
                dtxp = A0.h[:, 128:192]
                btp = A0.h[:, 384:512]
                m_buf = ar.A[0].buf
                ycp = A1f[:, 256:320]
                yc_buf = ar.A[1].buf
            pR = ar.P[0 + pi]
            pYi = ar.P[2 + pi]
            pYb = ar.P[4 + pi]
            if stage == 0:
                p.v(lambda E, h=h, RBp=RBp: E.tensor_copy(
                    RBp, LAMT[:, h:h + 1].broadcast_to((128, 128))),
                    reads=(ar.A[27].buf,), writes=(rb_buf,))
                p.t(lambda E, RBp=RBp, pR=pR: E.transpose(
                    pR.h[:, 0:128], RBp, ar.identf.h[:, :]),
                    reads=(rb_buf, ar.identf.buf), writes=(pR.buf,))
                p.v(lambda E, h=h, DECp=DECp, pR=pR: E.tensor_scalar(
                    DECp, pR.h[:, 0:128], LAMT[:, h:h + 1], None, ALU.subtract),
                    reads=(pR.buf, ar.A[27].buf), writes=(dec_buf,))
                p.v(lambda E, DECp=DECp: E.tensor_add(DECp, DECp, ar.trils.h[:, :]),
                    reads=(dec_buf, ar.trils.buf), writes=(dec_buf,))
                p.s(lambda E, DECp=DECp: E.activation(DECp, DECp, AF.Exp),
                    reads=(dec_buf,), writes=(dec_buf,))
                p.v(lambda E, DECp=DECp, Mp=Mp: E.tensor_mul(Mp, DECp, GT),
                    reads=(dec_buf, ar.Sg[0].buf), writes=(m_buf,))
                p.t(lambda E, xt=xt, xrow=xrow, csl=csl, pi=pi: E.transpose(
                    ar.Tp[pi].h[:, 0:64], xt.h[xrow:xrow + 64, csl],
                    ar.ident.h[xrow:xrow + 64, xrow:xrow + 64]),
                    reads=(xt.buf, ar.ident.buf), writes=(ar.Tp[pi].buf,))
                p.v(lambda E, h=h, dtxp=dtxp, pi=pi: E.tensor_scalar(
                    dtxp, ar.Tp[pi].h[:, 0:64], DTT[:, h:h + 1], None, ALU.mult),
                    reads=(ar.Tp[pi].buf, ar.A[27].buf), writes=(m_buf,))
                p.v(lambda E, h=h, btp=btp: E.tensor_scalar(
                    btp, scr.h[:, 256:384], BDECT[:, h:h + 1], None, ALU.mult),
                    reads=(scr.buf, ar.A[27].buf), writes=(m_buf,))
            else:
                p.t(lambda E, pYi=pYi, Mp=Mp, dtxp=dtxp: E.matmul(
                    pYi.h[:, 0:64], Mp, dtxp, start=True, stop=True),
                    reads=(m_buf,), writes=(pYi.buf,))
                p.t(lambda E, pYb=pYb, csl=csl, h=h: E.matmul(
                    pYb.h[:, 0:64], ar.A[25].h[:, csl], H.h[:, h * 64:(h + 1) * 64],
                    start=True, stop=True),
                    reads=(ar.A[25].buf, H.buf), writes=(pYb.buf,))
                p.t(lambda E, pR=pR, btp=btp, dtxp=dtxp: E.matmul(
                    pR.h[:, 0:64], btp, dtxp, start=True, stop=True),
                    reads=(m_buf,), writes=(pR.buf,))
                p.v(lambda E, h=h, pR=pR: E.scalar_tensor_tensor(
                    H.h[:, h * 64:(h + 1) * 64], H.h[:, h * 64:(h + 1) * 64],
                    DTOT[:, h:h + 1], pR.h[:, 0:64], ALU.mult, ALU.add),
                    reads=(H.buf, ar.A[27].buf, pR.buf), writes=(H.buf,))
                p.v(lambda E, h=h, pYb=pYb, ycp=ycp: E.tensor_scalar(
                    ycp, pYb.h[:, 0:64], AEXPT[:, h:h + 1], None, ALU.mult),
                    reads=(pYb.buf, ar.A[27].buf), writes=(yc_buf,))
                p.v(lambda E, pYi=pYi, ycp=ycp: E.tensor_add(
                    ycp, ycp, pYi.h[:, 0:64]),
                    reads=(yc_buf, pYi.buf), writes=(yc_buf,))
                p.t(lambda E, ycp=ycp, pR=pR: E.transpose(
                    pR.h[0:64, 0:128], ycp, ar.identf.h[:, :]),
                    reads=(yc_buf, ar.identf.buf), writes=(pR.buf,))
                p.v(lambda E, xt=xt, xrow=xrow, csl=csl, h=h, pR=pR: E.scalar_tensor_tensor(
                    xt.h[xrow:xrow + 64, csl], xt.h[xrow:xrow + 64, csl],
                    ar.vec[2].h[xrow:xrow + 64, 88 + (h // 2):89 + (h // 2)],
                    pR.h[0:64, 0:128], ALU.mult, ALU.add),
                    reads=(xt.buf, ar.vec[2].buf, pR.buf), writes=(xt.buf,))

        for h in range(NH):
            m2_iter(h, 0)
            if h:
                m2_iter(h - 1, 1)
        m2_iter(NH - 1, 1)

    for i in range(16):
        zt = ar.M[0]
        p.dma("sync", zt.h[:, :], g1buf[i * 128:(i + 1) * 128, :], reads=(g1b,),
              writes=(zt.buf,))
        p.s(lambda E, zt=zt: E.activation(ar.M[1].h[:, :], zt.h[:, :], AF.Silu),
            reads=(zt.buf,), writes=(ar.M[1].buf,))
        p.v(lambda E, i=i: E.tensor_mul(ar.A[8 + i].h[:, :], ar.A[8 + i].h[:, :],
                                        ar.M[1].h[:, :]),
            reads=(ar.A[8 + i].buf, ar.M[1].buf), writes=(ar.A[8 + i].buf,))

    for ob in range(FD):
        slot = ar.W[ob % 2]
        load_wtile(p, ar, slot, opT, opb, None, 16, ob)
        p.dma("sync", ar.Sg[1].h[:, :], x_src[ob * 128:(ob + 1) * 128, :],
              reads=(xsb,), writes=(ar.Sg[1].buf,))
        banks = [ar.bank() for _ in range(4)]
        for tc in range(4):
            sl = slice(512 * tc, 512 * (tc + 1))
            for cch in range(16):
                p.t(lambda E, cch=cch, sl=sl, pk=banks[tc], slot=slot: E.matmul(
                    pk.h[:, :], slot.h[:, cch * 128:(cch + 1) * 128], ar.A[8 + cch].h[:, sl],
                    start=(cch == 0), stop=(cch == 15)),
                    reads=(slot.buf, ar.A[8 + cch].buf), writes=(banks[tc].buf,))
        for tc in range(4):
            sl = slice(512 * tc, 512 * (tc + 1))
            pk = banks[tc]
            p.v(lambda E, sl=sl, pk=pk, ob=ob: E.tensor_add(
                ar.Sg[0].h[:, sl], pk.h[:, :], ar.N[ob].h[:, sl]),
                reads=(pk.buf, ar.N[ob].buf), writes=(ar.Sg[0].buf,))
            p.v(lambda E, sl=sl: E.tensor_add(
                ar.Sg[0].h[:, sl], ar.Sg[0].h[:, sl], ar.Sg[1].h[:, sl]),
                reads=(ar.Sg[0].buf, ar.Sg[1].buf), writes=(ar.Sg[0].buf,))
        p.dma("sync", x_dst[ob * 128:(ob + 1) * 128, :], ar.Sg[0].h[:, :],
              reads=(ar.Sg[0].buf,), writes=(xdb,))


def decl_attn_l0(nc, wd, l):
    shapes = {
        "L0_attn_in_proj_w": (34, 128, 1024), "L0_attn_in_proj_last": (128, 256),
        "L0_attn_out_proj_w": (8, 128, 2048),
        "L0_attn_norm_w": (128, 8), "L0_attn_conv_w": (128, 64),
        "L0_attn_conv_b": (128, 16), "L0_attn_D": (128, 16),
        "L0_attn_A_log": (32, 1), "L0_attn_dt_bias": (32, 1),
    }
    for k, shp in shapes.items():
        if k not in wd:
            h = nc.dram_tensor(k, list(shp), F32, kind="ExternalInput")
            wd[k] = (h.ap(), Buf(k))


def prep_attn_l0(inputs, l):
    d = {}
    ipT = _t(inputs["L0_attn_in_proj_w"])
    d["L0_attn_in_proj_w"] = _wtiles(ipT[:, 0:4352], 8, 34)
    d["L0_attn_in_proj_last"] = _wtiles(ipT[:, 4352:4384], 8, 1, owid=32)[0]
    d["L0_attn_out_proj_w"] = _wtiles(_t(inputs["L0_attn_out_proj_w"]), 16, 8)
    d["L0_attn_norm_w"] = _cols(inputs["L0_attn_norm_w"])
    cw = np.asarray(inputs["L0_attn_conv_w"], np.float32)
    d["L0_attn_conv_w"] = np.concatenate(
        [_cols(cw[:, k]) for k in range(4)], axis=1)
    d["L0_attn_conv_b"] = _cols(inputs["L0_attn_conv_b"])
    d["L0_attn_D"] = _cols(np.repeat(np.asarray(inputs["L0_attn_D"], np.float32), 64))
    d["L0_attn_A_log"] = np.asarray(inputs["L0_attn_A_log"], np.float32).reshape(32, 1)
    d["L0_attn_dt_bias"] = np.asarray(inputs["L0_attn_dt_bias"], np.float32).reshape(32, 1)
    return d




def emit_attn_l2(p, ar, io, wd, l):
    x_src, xsb, x_dst, xdb = io
    ipT, ipb = wd["L2_attn_in_proj_w"]
    opT, opb = wd["L2_attn_out_proj_w"]
    nrm, nrmb = wd["L2_attn_norm_w"]
    g1buf, g1b = wd["g1buf"]
    bcb, bcbb = wd["bcbuf"]
    DI, DS, TC, TN = 2048, 16, 512, 128

    emit_layernorm(p, ar, x_src, xsb)

    p.dma("sync", ar.vec[2].h[:, 0:8], nrm[:, :], reads=(nrmb,), writes=(ar.vec[2].buf,))
    emit_rms_stats(p, ar, ar.N, 1e-5, (ar.A[30], ar.A[31]))
    for i in range(FD):
        p.v(lambda E, i=i: E.scalar_tensor_tensor(
            ar.A[i].h[:, :], ar.N[i].h[:, :], ar.vec[2].h[:, i:i + 1], ar.M[0].h[:, :],
            ALU.mult, ALU.mult),
            reads=(ar.N[i].buf, ar.vec[2].buf, ar.M[0].buf), writes=(ar.A[i].buf,))
    rn = ar.A[0:8]

    stg = ar.Sg[1].h[:, :].bitcast(BF16)
    for ob in range(32):
        slot = ar.W[ob % 2]
        load_wtile(p, ar, slot, ipT, ipb, None, 8, ob)
        banks = [ar.bank() for _ in range(4)]
        for tc in range(4):
            sl = slice(512 * tc, 512 * (tc + 1))
            for c in range(FD):
                p.t(lambda E, c=c, sl=sl, pk=banks[tc], slot=slot: E.matmul(
                    pk.h[:, :], slot.h[:, c * 128:(c + 1) * 128], rn[c].h[:, sl],
                    start=(c == 0), stop=(c == FD - 1)),
                    reads=(slot.buf, rn[c].buf), writes=(banks[tc].buf,))
        for tc in range(4):
            sl = slice(512 * tc, 512 * (tc + 1))
            ssl = slice(1024 * (tc % 2), 1024 * (tc % 2) + 512)
            p.s(lambda E, pk=banks[tc], ssl=ssl: E.activation(
                stg[:, ssl], pk.h[:, :], AF.Copy),
                reads=(banks[tc].buf,), writes=(ar.Sg[1].buf,))
            if ob < 16:
                p.dma("sync", g1buf[2048 + ob * 128:2048 + (ob + 1) * 128, sl], stg[:, ssl],
                      reads=(ar.Sg[1].buf,), writes=(g1b,))
            else:
                p.dma("sync", g1buf[(ob - 16) * 128:(ob - 15) * 128, sl], stg[:, ssl],
                      reads=(ar.Sg[1].buf,), writes=(g1b,))

    import os as _os
    _dirs = _os.environ.get("L2_DIRS", "fb")
    for direction in [d for d in "fb" if d in _dirs]:
        suf = "_" + direction
        cwc, cwcb = wd["L2_attn_conv" + suf]
        cbc, cbcb = wd["L2_attn_convb" + suf]
        xpT, xpTb = wd["L2_attn_xproj" + suf]
        dpT, dpTb = wd["L2_attn_dtproj" + suf]
        dpb, dpbb = wd["L2_attn_dtprojb" + suf]
        alg, algb = wd["L2_attn_alog" + suf]
        dcl, dclb = wd["L2_attn_D" + suf]

        p.dma("sync", ar.vec[2].h[:, 16:80], cwc[:, :], reads=(cwcb,), writes=(ar.vec[2].buf,))
        p.dma("sync", ar.vec[2].h[:, 80:96], cbc[:, :], reads=(cbcb,), writes=(ar.vec[2].buf,))
        p.dma("sync", ar.vec[2].h[:, 96:112], dpb[:, :], reads=(dpbb,), writes=(ar.vec[2].buf,))
        p.dma("sync", ar.vec[2].h[:, 112:128], dcl[:, :], reads=(dclb,), writes=(ar.vec[2].buf,))
        p.dma("sync", ar.vec[1].h[:, 0:256], alg[:, :], reads=(algb,), writes=(ar.vec[1].buf,))
        p.s(lambda E: E.activation(ar.vec[1].h[:, 0:256], ar.vec[1].h[:, 0:256], AF.Exp),
            reads=(ar.vec[1].buf,), writes=(ar.vec[1].buf,))
        negab = ar.vec[0].h[:, :].bitcast(BF16)
        p.v(lambda E, negab=negab: E.tensor_scalar_mul(
            negab[:, 0:256], ar.vec[1].h[:, 0:256], -1.0),
            reads=(ar.vec[1].buf,), writes=(ar.vec[0].buf,))
        p.dma("gpsimd", ar.W[0].h[:, 0:1536], xpT[0, :, :], reads=(xpTb,), writes=(ar.W[0].buf,))
        p.dma("gpsimd", ar.M[1].h[0:64, :], dpT[:, :], reads=(dpTb,), writes=(ar.M[1].buf,))
        p.v(lambda E: E.memset(ar.vec[3].h[:, :], 0.0), writes=(ar.vec[3].buf,))

        F30 = ar.A[30].h[:, :].bitcast(F32)
        F28 = ar.A[28].h[:, :].bitcast(F32)
        for q in range(4):
            qsl = slice(512 * q, 512 * (q + 1))
            for g in range(16):
                stage2 = ar.A[31].h[:, 0:515]
                acc = ar.Sg[0].h[:, 0:512]
                if direction == "f":
                    lo = 512 * q - 3
                    zpad = max(0, -lo)
                    p.dma("sync", ar.A[31].h[:, zpad:515],
                          g1buf[2048 + g * 128:2048 + (g + 1) * 128, lo + zpad:512 * q + 512],
                          reads=(g1b,), writes=(ar.A[31].buf,))
                    if zpad:
                        p.v(lambda E, zpad=zpad: E.memset(ar.A[31].h[:, 0:zpad], 0.0),
                            writes=(ar.A[31].buf,))
                else:
                    TH = 2050 - 512 * q
                    TL = TH - 514
                    lo = max(0, TL)
                    hi = min(2047, TH)
                    n = hi - lo + 1
                    p.dma("sync", ar.A[31].h[:, 515:515 + n],
                          g1buf[2048 + g * 128:2048 + (g + 1) * 128, lo:hi + 1],
                          reads=(g1b,), writes=(ar.A[31].buf,))
                    jlo = TH - hi
                    jhi = TH - lo
                    if jlo > 0:
                        p.v(lambda E, jlo=jlo: E.memset(ar.A[31].h[:, 0:jlo], 0.0),
                            writes=(ar.A[31].buf,))
                    if jhi < 514:
                        p.v(lambda E, jhi=jhi: E.memset(ar.A[31].h[:, jhi + 1:515], 0.0),
                            writes=(ar.A[31].buf,))
                    p.v(lambda E, jlo=jlo, jhi=jhi, n=n: E.tensor_copy(
                        ar.A[31].h[:, jlo:jhi + 1], ar.A[31].h[:, 515 + n - 1:514:-1]),
                        reads=(ar.A[31].buf,), writes=(ar.A[31].buf,))
                p.v(lambda E, g=g: E.tensor_scalar(
                    acc, stage2[:, 0:512], ar.vec[2].h[:, 16 + 0 * 16 + g:17 + 0 * 16 + g],
                    ar.vec[2].h[:, 80 + g:81 + g], ALU.mult, ALU.add),
                    reads=(ar.A[31].buf, ar.vec[2].buf), writes=(ar.Sg[0].buf,))
                for k in range(1, 4):
                    p.v(lambda E, g=g, k=k: E.scalar_tensor_tensor(
                        acc, stage2[:, k:k + 512], ar.vec[2].h[:, 16 + k * 16 + g:17 + k * 16 + g],
                        acc, ALU.mult, ALU.add),
                        reads=(ar.A[31].buf, ar.vec[2].buf, ar.Sg[0].buf), writes=(ar.Sg[0].buf,))
                xcq = ar.A[24 + g // 4]
                xoff = (g % 4) * 512
                p.s(lambda E, xcq=xcq, xoff=xoff: E.activation(
                    xcq.h[:, xoff:xoff + 512], acc, AF.Silu),
                    reads=(ar.Sg[0].buf,), writes=(xcq.buf,))
            pk = ar.P[5]
            for g in range(16):
                xcq = ar.A[24 + g // 4]
                xoff = (g % 4) * 512
                p.t(lambda E, g=g, pk=pk, xcq=xcq, xoff=xoff: E.matmul(
                    pk.h[0:96, :], ar.W[0].h[:, g * 96:(g + 1) * 96],
                    xcq.h[:, xoff:xoff + 512],
                    start=(g == 0), stop=(g == 15)),
                    reads=(ar.W[0].buf, xcq.buf), writes=(pk.buf,))
            p.s(lambda E, pk=pk: E.activation(ar.Sg[1].h[0:96, 0:512], pk.h[0:96, :], AF.Copy),
                reads=(pk.buf,), writes=(ar.Sg[1].buf,))
            p.s(lambda E, pk=pk: E.activation(ar.M[0].h[0:64, 0:512], pk.h[0:64, :], AF.Copy),
                reads=(pk.buf,), writes=(ar.M[0].buf,))
            p.dma("gpsimd", bcb[:, :, :].rearrange("ts n t -> n ts t"),
                  ar.Sg[1].h[64:96, 0:512].rearrange("n (ts t) -> n ts t", t=128),
                  reads=(ar.Sg[1].buf,), writes=(bcbb,))
            M0f = ar.M[0].h[:, :].bitcast(F32)
            ABUF = (ar.Sg[0], ar.Sg[1])
            HBUF = (ar.A[29], ar.A[30])

            def l2_delta(g):
                xcq = ar.A[24 + g // 4]
                xoff = (g % 4) * 512
                dt_dst = ar.A[0 + g // 4]
                dx_dst = ar.A[4 + g // 4]
                doff = (g % 4) * 512
                F2 = ar.Sg[0].h[:, :]
                pk2 = ar.P[4 + g % 2]
                p.t(lambda E, g=g, pk2=pk2: E.matmul(
                    pk2.h[:, :], ar.M[1].h[0:64, g * 128:(g + 1) * 128],
                    ar.M[0].h[0:64, 0:512], start=True, stop=True),
                    reads=(ar.M[1].buf, ar.M[0].buf), writes=(pk2.buf,))
                sl2 = slice(0, 512) if g % 2 == 0 else slice(512, 1024)
                p.v(lambda E, g=g, pk2=pk2, F2=F2, sl2=sl2: E.tensor_scalar(
                    F2[:, sl2], pk2.h[:, :], ar.vec[2].h[:, 96 + g:97 + g], 80.0,
                    ALU.add, ALU.min),
                    reads=(pk2.buf, ar.vec[2].buf), writes=(ar.Sg[0].buf,))
                p.s(lambda E, F2=F2, sl2=sl2: E.activation(F2[:, sl2], F2[:, sl2], AF.Exp),
                    reads=(ar.Sg[0].buf,), writes=(ar.Sg[0].buf,))
                p.s(lambda E, F2=F2, sl2=sl2: E.activation(F2[:, sl2], F2[:, sl2], AF.Ln, bias=1.0),
                    reads=(ar.Sg[0].buf,), writes=(ar.Sg[0].buf,))
                p.s(lambda E, F2=F2, sl2=sl2, dt_dst=dt_dst, doff=doff: E.activation(
                    dt_dst.h[:, doff:doff + 512], F2[:, sl2], AF.Copy),
                    reads=(ar.Sg[0].buf,), writes=(dt_dst.buf,))
                p.v(lambda E, xcq=xcq, xoff=xoff, dt_dst=dt_dst, dx_dst=dx_dst, doff=doff: E.tensor_mul(
                    dx_dst.h[:, doff:doff + 512], dt_dst.h[:, doff:doff + 512],
                    xcq.h[:, xoff:xoff + 512]),
                    reads=(dt_dst.buf, xcq.buf), writes=(dx_dst.buf,))

            for g in range(16):
                l2_delta(g)

            def l2_s0(g, ts):
                pi = g % 2
                ab = ABUF[pi]
                hb = HBUF[pi]
                ABT = ab.h[:, :].bitcast(BF16)
                dt_src = ar.A[0 + g // 4]
                dx_src = ar.A[4 + g // 4]
                doff = (g % 4) * 512 + 128 * ts
                p.emit("gpsimd", lambda E, hb=hb, dx_src=dx_src, doff=doff: E.tensor_tensor(
                    hb.h[:, 0:2048].rearrange("p (n t) -> p n t", t=128),
                    dx_src.h[:, doff:doff + 128]
                    .unsqueeze(1).broadcast_to((128, 16, 128)),
                    ar.A[28].h[:, 0:2048].rearrange("p (n t) -> p n t", t=128), ALU.mult),
                    reads=(dx_src.buf, ar.A[28].buf), writes=(hb.buf,))
                p.emit("gpsimd", lambda E, ABT=ABT, dt_src=dt_src, doff=doff, g=g: E.tensor_tensor(
                    ABT[:, 2048:4096].rearrange("p (n t) -> p n t", t=128),
                    dt_src.h[:, doff:doff + 128]
                    .unsqueeze(1).broadcast_to((128, 16, 128)),
                    negab[:, g * 16:(g + 1) * 16].unsqueeze(2)
                    .broadcast_to((128, 16, 128)), ALU.mult),
                    reads=(dt_src.buf, ar.vec[0].buf), writes=(ab.buf,))
                p.s(lambda E, ab=ab, ABT=ABT: E.activation(
                    ab.h[:, 0:2048], ABT[:, 2048:4096], AF.Exp),
                    reads=(ab.buf,), writes=(ab.buf,))

            def l2_s1(g, ts):
                pi = g % 2
                ab = ABUF[pi]
                hb = HBUF[pi]
                ABT = ab.h[:, :].bitcast(BF16)
                xcq = ar.A[24 + g // 4]
                xoff = (g % 4) * 512 + 128 * ts
                hcol = ar.vec[3].h[:, g * 16:(g + 1) * 16]
                p.v(lambda E, hcol=hcol, ab=ab: E.tensor_mul(
                    ar.col[3].h[:, 0:16], ab.h[:, 0:2048:128], hcol),
                    reads=(ab.buf, ar.vec[3].buf), writes=(ar.col[3].buf,))
                p.v(lambda E, hb=hb: E.tensor_add(
                    hb.h[:, 0:2048:128], hb.h[:, 0:2048:128], ar.col[3].h[:, 0:16]),
                    reads=(hb.buf, ar.col[3].buf), writes=(hb.buf,))
                p.v(lambda E, ab=ab: E.memset(ab.h[:, 0:2048:128], 0.0),
                    writes=(ab.buf,))
                p.v(lambda E, ab=ab, hb=hb: E.tensor_tensor_scan(
                    hb.h[:, 0:2048], ab.h[:, 0:2048], hb.h[:, 0:2048], 0.0,
                    ALU.mult, ALU.add),
                    reads=(ab.buf, hb.buf), writes=(hb.buf,))
                p.v(lambda E, hcol=hcol, hb=hb: E.tensor_copy(hcol, hb.h[:, 127:2048:128]),
                    reads=(hb.buf,), writes=(ar.vec[3].buf,))
                p.v(lambda E, hb=hb, ABT=ABT: E.tensor_mul(
                    ABT[:, 0:2048], hb.h[:, 0:2048], ar.A[31].h[:, 0:2048]),
                    reads=(hb.buf, ar.A[31].buf), writes=(ab.buf,))
                p.v(lambda E, ABT=ABT: E.tensor_add(ABT[:, 0:1024], ABT[:, 0:1024],
                                                    ABT[:, 1024:2048]),
                    reads=(ab.buf,), writes=(ab.buf,))
                p.v(lambda E, ABT=ABT: E.tensor_add(ABT[:, 0:512], ABT[:, 0:512],
                                                    ABT[:, 512:1024]),
                    reads=(ab.buf,), writes=(ab.buf,))
                p.v(lambda E, ABT=ABT: E.tensor_add(ABT[:, 0:256], ABT[:, 0:256],
                                                    ABT[:, 256:512]),
                    reads=(ab.buf,), writes=(ab.buf,))
                yo = M0f[:, 256 + 128 * pi:384 + 128 * pi]
                p.v(lambda E, ABT=ABT, yo=yo: E.tensor_add(
                    yo, ABT[:, 0:128], ABT[:, 128:256]),
                    reads=(ab.buf,), writes=(ar.M[0].buf,))
                if direction == "f":
                    t0 = 512 * q + 128 * ts
                    p.v(lambda E, g=g, t0=t0, xcq=xcq, xoff=xoff, yo=yo: E.scalar_tensor_tensor(
                        ar.A[8 + g].h[:, t0:t0 + 128], xcq.h[:, xoff:xoff + 128],
                        ar.vec[2].h[:, 112 + g:113 + g], yo, ALU.mult, ALU.add),
                        reads=(xcq.buf, ar.vec[2].buf, ar.M[0].buf), writes=(ar.A[8 + g].buf,))
                else:
                    t0 = 2048 - 512 * q - 128 * (ts + 1)
                    p.v(lambda E, g=g, xcq=xcq, xoff=xoff, yo=yo: E.scalar_tensor_tensor(
                        yo, xcq.h[:, xoff:xoff + 128],
                        ar.vec[2].h[:, 112 + g:113 + g], yo, ALU.mult, ALU.add),
                        reads=(xcq.buf, ar.vec[2].buf, ar.M[0].buf), writes=(ar.M[0].buf,))
                    revout = M0f[:, 512 + 128 * pi:640 + 128 * pi]
                    p.v(lambda E, pi=pi, revout=revout: E.tensor_copy(
                        revout, M0f[:, 383 + 128 * pi:255 + 128 * pi:-1]),
                        reads=(ar.M[0].buf,), writes=(ar.M[0].buf,))
                    p.v(lambda E, g=g, t0=t0: E.tensor_scalar_mul(
                        ar.A[8 + g].h[:, t0:t0 + 128], ar.A[8 + g].h[:, t0:t0 + 128], 0.5),
                        reads=(ar.A[8 + g].buf,), writes=(ar.A[8 + g].buf,))
                    p.v(lambda E, g=g, t0=t0, revout=revout: E.scalar_tensor_tensor(
                        ar.A[8 + g].h[:, t0:t0 + 128], revout,
                        0.5, ar.A[8 + g].h[:, t0:t0 + 128], ALU.mult, ALU.add),
                        reads=(ar.M[0].buf, ar.A[8 + g].buf), writes=(ar.A[8 + g].buf,))

            for ts in range(4):
                p.dma("gpsimd", ar.A[28].h[:, 0:2048],
                      bcb[ts, 0:16, :].rearrange("n t -> (n t)").unsqueeze(0)
                      .partition_broadcast(128),
                      reads=(bcbb,), writes=(ar.A[28].buf,))
                p.dma("gpsimd", ar.A[31].h[:, 0:2048],
                      bcb[ts, 16:32, :].rearrange("n t -> (n t)").unsqueeze(0)
                      .partition_broadcast(128),
                      reads=(bcbb,), writes=(ar.A[31].buf,))
                for g in range(16):
                    l2_s0(g, ts)
                    if g:
                        l2_s1(g - 1, ts)
                l2_s1(15, ts)
    for i in range(16):
        zt = ar.M[i % 2]
        st = ar.W[i % 2]
        p.dma("sync", zt.h[:, :], g1buf[i * 128:(i + 1) * 128, :], reads=(g1b,),
              writes=(zt.buf,))
        p.s(lambda E, zt=zt, st=st: E.activation(st.h[:, :], zt.h[:, :], AF.Silu),
            reads=(zt.buf,), writes=(st.buf,))
        p.emit("gpsimd", lambda E, i=i, st=st: E.tensor_tensor(
            ar.A[8 + i].h[:, :], ar.A[8 + i].h[:, :], st.h[:, :], ALU.mult),
            reads=(ar.A[8 + i].buf, st.buf), writes=(ar.A[8 + i].buf,))

    if _os.environ.get("L2_DEBUG") == "y":
        off = 8 if _os.environ.get("L2_HALF") == "1" else 0
        for i in range(8):
            p.dma("gpsimd", x_dst[i * 128:(i + 1) * 128, :], ar.A[8 + off + i].h[:, :],
                  reads=(ar.A[8 + off + i].buf,), writes=(xdb,))
        return
    for ob in range(FD):
        slot = ar.W[1]
        load_wtile(p, ar, slot, opT, opb, None, 16, ob)
        p.dma("sync", ar.Sg[1].h[:, :], x_src[ob * 128:(ob + 1) * 128, :],
              reads=(xsb,), writes=(ar.Sg[1].buf,))
        banks = [ar.bank() for _ in range(4)]
        for tc in range(4):
            sl = slice(512 * tc, 512 * (tc + 1))
            for cch in range(16):
                p.t(lambda E, cch=cch, sl=sl, pk=banks[tc], slot=slot: E.matmul(
                    pk.h[:, :], slot.h[:, cch * 128:(cch + 1) * 128], ar.A[8 + cch].h[:, sl],
                    start=(cch == 0), stop=(cch == 15)),
                    reads=(slot.buf, ar.A[8 + cch].buf), writes=(banks[tc].buf,))
        for tc in range(4):
            sl = slice(512 * tc, 512 * (tc + 1))
            pk = banks[tc]
            p.v(lambda E, sl=sl, pk=pk, ob=ob: E.tensor_add(
                ar.Sg[0].h[:, sl], pk.h[:, :], ar.N[ob].h[:, sl]),
                reads=(pk.buf, ar.N[ob].buf), writes=(ar.Sg[0].buf,))
            p.v(lambda E, sl=sl: E.tensor_add(
                ar.Sg[0].h[:, sl], ar.Sg[0].h[:, sl], ar.Sg[1].h[:, sl]),
                reads=(ar.Sg[0].buf, ar.Sg[1].buf), writes=(ar.Sg[0].buf,))
        p.dma("sync", x_dst[ob * 128:(ob + 1) * 128, :], ar.Sg[0].h[:, :],
              reads=(ar.Sg[0].buf,), writes=(xdb,))


def decl_attn_l2(nc, wd, l):
    shapes = {
        "L2_attn_in_proj_w": (32, 128, 1024), "L2_attn_out_proj_w": (8, 128, 2048),
        "L2_attn_norm_w": (128, 8),
    }
    for suf in ("_f", "_b"):
        shapes["L2_attn_conv" + suf] = (128, 64)
        shapes["L2_attn_convb" + suf] = (128, 16)
        shapes["L2_attn_xproj" + suf] = (1, 128, 1536)
        shapes["L2_attn_dtproj" + suf] = (64, 2048)
        shapes["L2_attn_dtprojb" + suf] = (128, 16)
        shapes["L2_attn_alog" + suf] = (128, 256)
        shapes["L2_attn_D" + suf] = (128, 16)
    for k, shp in shapes.items():
        if k not in wd:
            h = nc.dram_tensor(k, list(shp), F32, kind="ExternalInput")
            wd[k] = (h.ap(), Buf(k))
    if "bcbuf" not in wd:
        h = nc.dram_tensor("bcbuf", [4, 32, 128], BF16, kind="Internal")
        wd["bcbuf"] = (h.ap(), Buf("bcbuf"))


def prep_attn_l2(inputs, l):
    d = {}
    d["L2_attn_in_proj_w"] = _wtiles(_t(inputs["L2_attn_in_proj_w"]), 8, 32)
    d["L2_attn_out_proj_w"] = _wtiles(_t(inputs["L2_attn_out_proj_w"]), 16, 8)
    d["L2_attn_norm_w"] = _cols(inputs["L2_attn_norm_w"])
    for suf in ("_f", "_b"):
        cw = np.asarray(inputs["L2_attn_conv_w" + suf], np.float32).reshape(2048, 4)
        d["L2_attn_conv" + suf] = np.concatenate([_cols(cw[:, k]) for k in range(4)], axis=1)
        d["L2_attn_convb" + suf] = _cols(inputs["L2_attn_conv_b" + suf])
        d["L2_attn_xproj" + suf] = _wtiles(_t(inputs["L2_attn_xproj_w" + suf]), 16, 1, owid=96)
        d["L2_attn_dtproj" + suf] = _t(inputs["L2_attn_dtproj_w" + suf])
        d["L2_attn_dtprojb" + suf] = _cols(inputs["L2_attn_dtproj_b" + suf])
        al = np.asarray(inputs["L2_attn_A_log" + suf], np.float32)
        d["L2_attn_alog" + suf] = np.ascontiguousarray(
            al.reshape(16, 128, 16).transpose(1, 0, 2).reshape(128, 256))
        d["L2_attn_D" + suf] = _cols(inputs["L2_attn_D" + suf])
    return d



def _t(a):
    return np.ascontiguousarray(np.asarray(a).T)


def _cols(a, pad_to=None):
    a = np.asarray(a)
    n = a.shape[0]
    nb = n // 128
    out = np.ascontiguousarray(a.reshape(nb, 128).T)
    return out


def prep_ffn_ternary(inputs, l):
    pre = f"L{l}_ffn_"
    d = {}
    d[pre + "w1_w"] = _wtiles(_t(inputs[pre + "w1_w"]), 8, 32)
    d[pre + "w2_w"] = _wtiles(_t(inputs[pre + "w2_w"]), 8, 32)
    d[pre + "h1_w"] = _wtiles(_t(inputs[pre + "h1_w"]), 32, 32, halves=2)
    d[pre + "h2_w"] = _wtiles(_t(inputs[pre + "h2_w"]), 32, 32, halves=2)
    d[pre + "w3_w"] = _wtiles(_t(inputs[pre + "w3_w"]), 32, 8, halves=2)
    sc = np.concatenate([
        _cols(inputs[pre + "w1_s"]), _cols(inputs[pre + "w2_s"]),
        _cols(inputs[pre + "h1_s"]), _cols(inputs[pre + "h2_s"]),
        _cols(inputs[pre + "w3_s"])], axis=1)
    d[pre + "scales"] = np.ascontiguousarray(sc)
    return d


def decl_ffn_ternary(nc, wd, l):
    pre = f"L{l}_ffn_"
    shapes = {
        pre + "w1_w": (32, 128, 1024), pre + "w2_w": (32, 128, 1024),
        pre + "h1_w": (32, 2, 128, 2048), pre + "h2_w": (32, 2, 128, 2048),
        pre + "w3_w": (8, 2, 128, 2048), pre + "scales": (128, 136),
    }
    for k, shp in shapes.items():
        if k not in wd:
            h = nc.dram_tensor(k, list(shp), F32, kind="ExternalInput")
            wd[k] = (h.ap(), Buf(k))



def sublayer_chain():
    chain = []
    for l in range(4):
        if l == 0:
            ffn = ("ffn0", None)
            mix = ("mix0", None)
        elif l == 1:
            ffn = ("ffnT", l)
            mix = ("mix1", None)
        elif l == 2:
            ffn = ("ffnT", l)
            mix = ("mix2", None)
        else:
            ffn = ("ffnT", l)
            mix = ("mix3", None)
        chain.append(("ffn", l))
        chain.append(("mix", l))
        chain.append(("ffn", l))
    return chain


EMITTERS = {}
EMITTERS["ffn_0"] = (emit_ffn0, decl_ffn0, prep_ffn0)
EMITTERS["ffn_1"] = (emit_ffn_ternary, decl_ffn_ternary, prep_ffn_ternary)
EMITTERS["ffn_2"] = (emit_ffn_ternary, decl_ffn_ternary, prep_ffn_ternary)
EMITTERS["ffn_3"] = (emit_ffn_ternary, decl_ffn_ternary, prep_ffn_ternary)
EMITTERS["mix_3"] = (emit_attn_l3, decl_attn_l3, prep_attn_l3)
EMITTERS["mix_1"] = (emit_attn_l1, decl_attn_l1, prep_attn_l1)
EMITTERS["mix_0"] = (emit_attn_l0, decl_attn_l0, prep_attn_l0)
EMITTERS["mix_2"] = (emit_attn_l2, decl_attn_l2, prep_attn_l2)


def build_program(sublayers):
    nc = bass.Bass("TRN2", target_bir_lowering=False)
    stack = ExitStack()
    x_in = nc.dram_tensor("x_in", [D, S], F32, kind="ExternalInput").ap()
    x_out = nc.dram_tensor("x_out", [D, S], F32, kind="ExternalOutput").ap()
    xb0 = nc.dram_tensor("xb0", [D, S], F32, kind="Internal").ap()
    xb1 = nc.dram_tensor("xb1", [D, S], F32, kind="Internal").ap()
    xbufs = [(x_in, Buf("x_in")), (xb0, Buf("xb0")), (xb1, Buf("xb1")),
             (x_out, Buf("x_out"))]

    wd = {}
    g1 = nc.dram_tensor("g1buf", [4096, S], BF16, kind="Internal").ap()
    g2 = nc.dram_tensor("g2buf", [4096, S], BF16, kind="Internal").ap()
    wd["g1buf"] = (g1, Buf("g1buf"))
    wd["g2buf"] = (g2, Buf("g2buf"))

    for kind, l in sublayers:
        _, decl, _ = EMITTERS[f"{kind}_{l}"]
        decl(nc, wd, l)

    with stack:
        ar = Arena(nc, stack)
        p = Prog(nc, stack)
        c_ones = nc.dram_tensor("c_ones", [128, 128], BF16, kind="ExternalInput").ap()
        c_ident = nc.dram_tensor("c_ident", [128, 128], BF16, kind="ExternalInput").ap()
        c_consts = nc.dram_tensor("c_consts", [128, len(CONST_VALS)], F32,
                                  kind="ExternalInput").ap()
        cst = TT(stack.enter_context(
            nc.sbuf_tensor("cst", [128, len(CONST_VALS)], F32)), "cst")
        p.dma("sync", ar.ones.h[:, :], c_ones, writes=(ar.ones.buf,))
        p.dma("sync", ar.ident.h[:, :], c_ident, writes=(ar.ident.buf,))
        p.dma("sync", cst.h[:, :], c_consts, writes=(cst.buf,))
        c_tri = nc.dram_tensor("c_tri", [128, 128], F32, kind="ExternalInput").ap()
        p.dma("sync", ar.tri.h[:, :], c_tri, writes=(ar.tri.buf,))
        c_esel = nc.dram_tensor("c_esel", [8, 1024], BF16, kind="ExternalInput").ap()
        p.dma("sync", ar.esel.h[0:8, :], c_esel, writes=(ar.esel.buf,))
        c_hsel2 = nc.dram_tensor("c_hsel2", [16, 1024], BF16, kind="ExternalInput").ap()
        p.dma("sync", ar.esel.h[32:48, :], c_hsel2, writes=(ar.esel.buf,))
        c_identf = nc.dram_tensor("c_identf", [128, 128], F32, kind="ExternalInput").ap()
        p.dma("sync", ar.identf.h[:, :], c_identf, writes=(ar.identf.buf,))
        c_onesf = nc.dram_tensor("c_onesf", [128, 128], F32, kind="ExternalInput").ap()
        p.dma("sync", ar.onesf.h[:, :], c_onesf, writes=(ar.onesf.buf,))
        c_trils = nc.dram_tensor("c_trils", [128, 128], F32, kind="ExternalInput").ap()
        p.dma("sync", ar.trils.h[:, :], c_trils, writes=(ar.trils.buf,))
        c_trili = nc.dram_tensor("c_trili", [128, 128], F32, kind="ExternalInput").ap()
        p.dma("sync", ar.trili.h[:, :], c_trili, writes=(ar.trili.buf,))
        for k, v in enumerate(CONST_VALS):
            nc.const_aps.register(F32, v, cst.h[:, k:k + 1])
            nc.const_aps.register(BF16, v, cst.h[:, k:k + 1])
        for e in ENGINES:
            if e in ("sync",):
                continue
            p.emit(e, lambda E: E.nop(),
                   reads=(cst.buf, ar.ones.buf, ar.ident.buf, ar.identf.buf,
                          ar.onesf.buf, ar.tri.buf, ar.esel.buf))

        n = len(sublayers)
        for idx, (kind, l) in enumerate(sublayers):
            if idx == 0:
                src = xbufs[0]
            else:
                src = xbufs[1 + ((idx - 1) % 2)]
            if idx == n - 1:
                dst = xbufs[3]
            else:
                dst = xbufs[1 + (idx % 2)]
            emit, _, _ = EMITTERS[f"{kind}_{l}"]
            io = (src[0], src[1], dst[0], dst[1])
            emit(p, ar, io, wd, l)
        p.finalize()
    return nc


def prep_inputs_common():
    consts = np.repeat(np.asarray(CONST_VALS, np.float32)[None, :], 128, axis=0)
    return {
        "c_ones": np.ones((128, 128), dtype=BF),
        "c_ident": np.eye(128, dtype=np.float32).astype(BF),
        "c_consts": np.ascontiguousarray(consts),
        "c_tri": np.triu(np.full((128, 128), -1.0e30, np.float32), 1),
        "c_esel": _esel(),
        "c_hsel2": _hsel2(),
        "c_identf": np.eye(128, dtype=np.float32),
        "c_trils": np.tril(np.full((128, 128), -1.0e30, np.float32), -1),
        "c_trili": np.tril(np.full((128, 128), -1.0e30, np.float32)),
        "c_onesf": _e127(),
    }


def _esel():
    m = np.zeros((8, 1024), np.float32)
    for e in range(8):
        m[e, e * 128:(e + 1) * 128] = 1.0
    return m.astype(BF)


def _e127():
    m = np.zeros((128, 128), np.float32)
    m[127, :] = 1.0
    return m


def _hsel2():
    m = np.zeros((16, 1024), np.float32)
    for k in range(16):
        hk = 2 * (k % 8) + (k // 8)
        i, r = hk // 2, hk % 2
        m[k, i * 128 + r * 64:i * 128 + r * 64 + 64] = 1.0
    return m.astype(BF)


def kernel(**inputs):
    chain = [(k, l) for (k, l) in sublayer_chain()]
    nc = build_program(chain)
    x = np.asarray(inputs["L0_attn_x"])
    per_core_common = {}
    for (kind, l) in set(chain):
        _, _, prep = EMITTERS[f"{kind}_{l}"]
        per_core_common.update(prep(inputs, l))
    per_core_common.update(prep_inputs_common())
    in_maps = []
    for c in range(N_CORES):
        m = dict(per_core_common)
        m["x_in"] = np.ascontiguousarray(x[c].T)
        in_maps.append(m)
    res = run_bass_kernel_spmd(nc, in_maps, core_ids=list(range(N_CORES)))
    outs = [np.ascontiguousarray(r["x_out"].T) for r in res.results]
    return np.stack(outs, axis=0)


# version 17
# speedup vs baseline: 1.2418x; 1.2418x over previous
import math
from contextlib import ExitStack

import numpy as np
import ml_dtypes

import concourse.bass as bass
import concourse.mybir as mybir
from concourse.bass_utils import run_bass_kernel_spmd

F32 = mybir.dt.float32
BF16 = mybir.dt.bfloat16
AF = mybir.ActivationFunctionType
ALU = mybir.AluOpType
AX = mybir.AxisListType

N_CORES = 8
CONST_VALS = [1e-5, 1e-6, 1e-12, 1.0, -1.0, 0.5, -0.5, 2.0, 1.0 / 1024.0,
              1.0 / 2048.0, 1.0 / 32.0, 80.0, 0.125, -0.125, 1e-4]
S = 2048
D = 1024
FD = D // 128
NEG = -1.0e30
BF = ml_dtypes.bfloat16



class Buf:
    __slots__ = ("name", "last_w", "readers")

    def __init__(self, name):
        self.name = name
        self.last_w = None
        self.readers = {}


class TT:
    __slots__ = ("h", "buf")

    def __init__(self, h, name):
        self.h = h
        self.buf = Buf(name)


ENGINES = ("tensor", "vector", "scalar", "sync", "gpsimd")


NSEM = 12


class Prog:

    def __init__(self, nc, stack):
        self.nc = nc
        self.sems = {e: stack.enter_context(nc.semaphore("sem_" + e)) for e in ENGINES}
        self.qpool = {}
        self.quse = {}
        self.qidx = {}
        for q in ("sync", "gpsimd"):
            self.qpool[q] = [stack.enter_context(nc.semaphore(f"q_{q}_{j}"))
                             for j in range(NSEM)]
            self.quse[q] = [0] * NSEM
            self.qidx[q] = 0
        self.count = {e: 0 for e in ENGINES}
        self.waited = {e: {} for e in ENGINES}
        self.items = {e: [] for e in ENGINES}

    def _resolve(self, engine, deps):
        waits = []
        w = self.waited[engine]
        for key, (sem, val) in deps.items():
            if key == "tensor" and engine == "tensor":
                continue
            if val > w.get(key, 0):
                waits.append((sem, val))
                w[key] = val
        return waits

    def _deps(self, reads, writes):
        deps = {}
        for b in reads:
            if b.last_w is not None:
                key, sem, val = b.last_w
                if key not in deps or deps[key][1] < val:
                    deps[key] = (sem, val)
        for b in writes:
            if b.last_w is not None:
                key, sem, val = b.last_w
                if key not in deps or deps[key][1] < val:
                    deps[key] = (sem, val)
            for key, (sem, val) in b.readers.items():
                if key not in deps or deps[key][1] < val:
                    deps[key] = (sem, val)
        return deps

    def emit(self, engine, fn, reads=(), writes=()):
        deps = self._deps(reads, writes)
        waits = self._resolve(engine, deps)
        self.count[engine] += 1
        seq = self.count[engine]
        self.items[engine].append((waits, fn, self.sems[engine], 1))
        tok = (engine, self.sems[engine], seq)
        for b in writes:
            b.last_w = tok
            b.readers = {}
        for b in reads:
            key = engine
            if key not in b.readers or b.readers[key][1] < seq:
                b.readers[key] = (self.sems[engine], seq)
        return seq

    def dma(self, queue, out_ap, in_ap, reads=(), writes=(), **kw):
        deps = self._deps(reads, writes)
        waits = self._resolve(queue, deps)
        j = self.qidx[queue] % NSEM
        self.qidx[queue] += 1
        sem = self.qpool[queue][j]
        prev = self.quse[queue][j]
        key = (queue, j)
        if prev > 0:
            w = self.waited[queue]
            if 16 * prev > w.get(key, 0):
                waits.append((sem, 16 * prev))
                w[key] = 16 * prev
        self.quse[queue][j] = prev + 1
        val = 16 * (prev + 1)
        self.items[queue].append((waits, lambda E: E.dma_start(out_ap, in_ap, **kw),
                                  sem, 16))
        tok = (key, sem, val)
        for b in writes:
            b.last_w = tok
            b.readers = {}
        for b in reads:
            if key not in b.readers or b.readers[key][1] < val:
                b.readers[key] = (sem, val)
        return val

    def group(self, engine, fns, reads=(), writes=()):
        deps = self._deps(reads, writes)
        waits = self._resolve(engine, deps)
        sem = self.sems[engine]
        for i, fn in enumerate(fns):
            self.count[engine] += 1
            self.items[engine].append((waits if i == 0 else [], fn, sem, 1))
        seq = self.count[engine]
        tok = (engine, sem, seq)
        for b in writes:
            b.last_w = tok
            b.readers = {}
        for b in reads:
            if engine not in b.readers or b.readers[engine][1] < seq:
                b.readers[engine] = (sem, seq)
        return seq

    def t(self, fn, reads=(), writes=()):
        return self.emit("tensor", fn, reads, writes)

    def v(self, fn, reads=(), writes=()):
        return self.emit("vector", fn, reads, writes)

    def s(self, fn, reads=(), writes=()):
        return self.emit("scalar", fn, reads, writes)

    def finalize(self):
        targets = [(self.sems[e], self.count[e]) for e in ("tensor", "vector", "scalar")
                   if self.count[e] > 0]
        for q in ("sync", "gpsimd"):
            for j in range(NSEM):
                if self.quse[q][j] > 0:
                    targets.append((self.qpool[q][j], 16 * self.quse[q][j]))
        items = self.items

        with self.nc.Block() as block:
            def runner(eng_name, tail=False):
                def run(E):
                    for waits, fn, own_sem, inc in items[eng_name]:
                        for sem, val in waits:
                            E.wait_ge(sem, val)
                        fn(E).then_inc(own_sem, inc)
                    if tail:
                        for sem, val in targets:
                            E.wait_ge(sem, val)
                return run

            block.tensor(runner("tensor"))
            block.vector(runner("vector"))
            block.scalar(runner("scalar"))
            block.sync(runner("sync", tail=True))
            block.gpsimd(runner("gpsimd"))



class Arena:
    def __init__(self, nc, stack):
        c = stack.enter_context
        self.A = [TT(c(nc.sbuf_tensor(f"A{i}", [128, 2048], BF16)), f"A{i}") for i in range(32)]
        self.N = [TT(c(nc.sbuf_tensor(f"N{i}", [128, 2048], BF16)), f"N{i}") for i in range(FD)]
        self.W = [TT(c(nc.sbuf_tensor(f"W{i}", [128, 2048], BF16)), f"W{i}") for i in range(2)]
        self.Sg = [TT(c(nc.sbuf_tensor(f"Sg{i}", [128, 2048], F32)), f"Sg{i}") for i in range(2)]
        self.M = [TT(c(nc.sbuf_tensor(f"M{i}", [128, 2048], BF16)), f"M{i}") for i in range(2)]
        self.vec = [TT(c(nc.sbuf_tensor(f"vec{i}", [128, 256], F32)), f"vec{i}") for i in range(4)]
        self.col = [TT(c(nc.sbuf_tensor(f"col{i}", [128, 64], F32)), f"col{i}") for i in range(4)]
        self.tri = TT(c(nc.sbuf_tensor("tri", [128, 128], F32)), "tri")
        self.esel = TT(c(nc.sbuf_tensor("esel", [128, 1024], BF16)), "esel")
        self.gwt = TT(c(nc.sbuf_tensor("gwt", [128, 64], BF16)), "gwt")
        self.ones = TT(c(nc.sbuf_tensor("ones", [128, 128], BF16)), "ones")
        self.identf = TT(c(nc.sbuf_tensor("identf", [128, 128], F32)), "identf")
        self.trils = TT(c(nc.sbuf_tensor("trils", [128, 128], F32)), "trils")
        self.trili = TT(c(nc.sbuf_tensor("trili", [128, 128], F32)), "trili")
        self.onesf = TT(c(nc.sbuf_tensor("onesf", [128, 128], F32)), "onesf")
        self.ident = TT(c(nc.sbuf_tensor("ident", [128, 128], BF16)), "ident")
        self.P = [TT(c(nc.psum_tensor(f"P{i}", [128, 512], F32)), f"P{i}") for i in range(6)]
        self.Tp = [TT(c(nc.psum_tensor(f"T{i}", [128, 1024], BF16)), f"T{i}") for i in range(2)]
        self._rr = 0

    def bank(self):
        b = self.P[self._rr % 6]
        self._rr += 1
        return b



def f32view(tt, lo=None, hi=None):
    ap = tt.h[:, :].bitcast(F32)
    if lo is None:
        return ap
    return ap[:, lo:hi]


def emit_layernorm(p, ar, x_dram, x_buf, eps=1e-5):
    for i in range(FD):
        s0 = ar.A[16 + (i % 2) * 2]
        s1 = ar.A[17 + (i % 2) * 2]
        a0 = f32view(s0)
        a1 = f32view(s1)
        p.dma("sync", a0, x_dram[128 * i:128 * (i + 1), 0:1024],
              reads=(x_buf,), writes=(s0.buf,))
        p.dma("sync", a1, x_dram[128 * i:128 * (i + 1), 1024:2048],
              reads=(x_buf,), writes=(s1.buf,))
        p.s(lambda E, o=ar.N[i].h, a=a0: E.activation(o[:, 0:1024], a, AF.Copy),
            reads=(s0.buf,), writes=(ar.N[i].buf,))
        p.s(lambda E, o=ar.N[i].h, a=a1: E.activation(o[:, 1024:2048], a, AF.Copy),
            reads=(s1.buf,), writes=(ar.N[i].buf,))
        p.v(lambda E, o=ar.A[i].h, x_=ar.N[i].h: E.tensor_mul(o[:, :], x_[:, :], x_[:, :]),
            reads=(ar.N[i].buf,), writes=(ar.A[i].buf,))
    inv = 1.0 / float(D)
    for tc in range(4):
        sl = slice(512 * tc, 512 * (tc + 1))
        for i in range(FD):
            p.t(lambda E, i=i, sl=sl: E.matmul(
                ar.P[4].h[:, :], ar.ones.h[:, :], ar.N[i].h[:, sl],
                start=(i == 0), stop=(i == FD - 1)),
                reads=(ar.ones.buf, ar.N[i].buf), writes=(ar.P[4].buf,))
        for i in range(FD):
            p.t(lambda E, i=i, sl=sl: E.matmul(
                ar.P[5].h[:, :], ar.ones.h[:, :], ar.A[i].h[:, sl],
                start=(i == 0), stop=(i == FD - 1)),
                reads=(ar.ones.buf, ar.A[i].buf), writes=(ar.P[5].buf,))
        mean = ar.Sg[0].h[:, sl]
        ex2 = ar.Sg[1].h[:, sl]
        rstd = ar.M[0].h[:, sl]
        mr = ar.M[1].h[:, sl]
        p.s(lambda E, mean=mean: E.activation(mean, ar.P[4].h[:, :], AF.Copy, scale=inv),
            reads=(ar.P[4].buf,), writes=(ar.Sg[0].buf,))
        p.s(lambda E, ex2=ex2: E.activation(ex2, ar.P[5].h[:, :], AF.Copy, scale=inv),
            reads=(ar.P[5].buf,), writes=(ar.Sg[1].buf,))
        p.s(lambda E, mean=mean: E.activation(ar.P[4].h[:, :], mean, AF.Square),
            reads=(ar.Sg[0].buf,), writes=(ar.P[4].buf,))
        p.v(lambda E, ex2=ex2: E.tensor_sub(ex2, ex2, ar.P[4].h[:, :]),
            reads=(ar.Sg[1].buf, ar.P[4].buf), writes=(ar.Sg[1].buf,))
        p.s(lambda E, ex2=ex2: E.activation(ar.P[5].h[:, :], ex2, AF.Ln, bias=eps),
            reads=(ar.Sg[1].buf,), writes=(ar.P[5].buf,))
        p.s(lambda E, rstd=rstd: E.activation(rstd, ar.P[5].h[:, :], AF.Exp, scale=-0.5),
            reads=(ar.P[5].buf,), writes=(ar.M[0].buf,))
        p.v(lambda E, mr=mr, mean=mean, rstd=rstd: E.tensor_mul(mr, mean, rstd),
            reads=(ar.Sg[0].buf, ar.M[0].buf), writes=(ar.M[1].buf,))
    for i in range(FD):
        p.v(lambda E, i=i: E.tensor_mul(ar.A[i].h[:, :], ar.N[i].h[:, :], ar.M[0].h[:, :]),
            reads=(ar.N[i].buf, ar.M[0].buf), writes=(ar.A[i].buf,))
        p.v(lambda E, i=i: E.tensor_sub(ar.N[i].h[:, :], ar.A[i].h[:, :], ar.M[1].h[:, :]),
            reads=(ar.A[i].buf, ar.M[1].buf), writes=(ar.N[i].buf,))


def _wtiles(wT, nch, nob, owid=128, halves=1):
    wT = np.asarray(wT, np.float32)
    arr = wT.reshape(nch, 128, nob, owid).transpose(2, 0, 1, 3)
    if halves == 1:
        out = arr.transpose(0, 2, 1, 3).reshape(nob, 128, nch * owid)
    else:
        hc = nch // halves
        out = arr.reshape(nob, halves, hc, 128, owid).transpose(0, 1, 3, 2, 4).reshape(
            nob, halves, 128, hc * owid)
    return np.ascontiguousarray(out)


def load_wtile(p, ar, slot, wt_dram, wt_buf, half, nc_chunks, ob, owid=128):
    if half is None:
        in_ap = wt_dram[ob, :, :]
    else:
        in_ap = wt_dram[ob, half, :, :]
    p.dma("gpsimd", slot.h[:, 0:nc_chunks * owid], in_ap,
          reads=(wt_buf,), writes=(slot.buf,))


def emit_ffn_ternary(p, ar, io, wd, l):
    x_src, xsb, x_dst, xdb = io
    pre = f"L{l}_ffn_"
    w1T, w1b = wd[pre + "w1_w"]
    w2T, w2b = wd[pre + "w2_w"]
    h1T, h1b = wd[pre + "h1_w"]
    h2T, h2b = wd[pre + "h2_w"]
    w3T, w3b = wd[pre + "w3_w"]
    sc, scb = wd[pre + "scales"]
    g1buf, g1b = wd[f"g1buf"]
    g2buf, g2b = wd[f"g2buf"]

    emit_layernorm(p, ar, x_src, xsb)

    p.dma("sync", ar.vec[0].h[:, 0:136], sc[:, :], reads=(scb,), writes=(ar.vec[0].buf,))
    p.v(lambda E: E.tensor_scalar_mul(ar.vec[0].h[:, 136:144], ar.vec[0].h[:, 128:136], 0.5),
        reads=(ar.vec[0].buf,), writes=(ar.vec[0].buf,))

    def col_of(idx):
        return ar.vec[0].h[:, idx:idx + 1]

    for ob in range(32):
        slot = ar.W[ob % 2]
        p.dma("gpsimd", slot.h[:, 0:1024], w1T[ob, :, :], reads=(w1b,), writes=(slot.buf,))
        p.dma("gpsimd", slot.h[:, 1024:2048], w2T[ob, :, :], reads=(w2b,), writes=(slot.buf,))
        for tc in range(4):
            sl = slice(512 * tc, 512 * (tc + 1))
            pa = ar.bank()
            pb = ar.bank()
            for c in range(FD):
                p.t(lambda E, c=c, sl=sl, pa=pa, slot=slot: E.matmul(
                    pa.h[:, :], slot.h[:, c * 128:(c + 1) * 128], ar.N[c].h[:, sl],
                    start=(c == 0), stop=(c == FD - 1)),
                    reads=(slot.buf, ar.N[c].buf), writes=(pa.buf,))
            for c in range(FD):
                p.t(lambda E, c=c, sl=sl, pb=pb, slot=slot: E.matmul(
                    pb.h[:, :], slot.h[:, 1024 + c * 128:1024 + (c + 1) * 128], ar.N[c].h[:, sl],
                    start=(c == 0), stop=(c == FD - 1)),
                    reads=(slot.buf, ar.N[c].buf), writes=(pb.buf,))
            p.s(lambda E, sl=sl, pa=pa, ob=ob: E.activation(
                ar.Sg[0].h[:, sl], pa.h[:, :], AF.Silu, scale=col_of(ob)),
                reads=(pa.buf, ar.vec[0].buf), writes=(ar.Sg[0].buf,))
            p.v(lambda E, sl=sl, pb=pb, ob=ob: E.scalar_tensor_tensor(
                ar.A[ob].h[:, sl], pb.h[:, :], col_of(32 + ob), ar.Sg[0].h[:, sl],
                ALU.mult, ALU.mult),
                reads=(pb.buf, ar.Sg[0].buf, ar.vec[0].buf), writes=(ar.A[ob].buf,))

    def dense4096(wT, wTb, src_tiles, n_ob, evac_fn):
        for ob in range(n_ob):
            slot0, slot1 = ar.W[0], ar.W[1]
            load_wtile(p, ar, slot0, wT, wTb, 0, 16, ob)
            load_wtile(p, ar, slot1, wT, wTb, 1, 16, ob)
            banks = [ar.bank() for _ in range(4)]
            for half, slot in ((0, slot0), (1, slot1)):
                for tc in range(4):
                    sl = slice(512 * tc, 512 * (tc + 1))
                    pk = banks[tc]
                    for c in range(16):
                        src = src_tiles[16 * half + c]
                        p.t(lambda E, c=c, sl=sl, pk=pk, slot=slot, src=src, half=half: E.matmul(
                            pk.h[:, :], slot.h[:, c * 128:(c + 1) * 128], src.h[:, sl],
                            start=(half == 0 and c == 0), stop=(half == 1 and c == 15)),
                            reads=(slot.buf, src.buf), writes=(pk.buf,))
            for tc in range(4):
                evac_fn(banks[tc], ob, tc, slice(512 * tc, 512 * (tc + 1)))

    def mid_phase(hT, hTb, gbuf, gbufb, scale_base):
        stg = ar.Sg[1].h[:, :].bitcast(BF16)

        def evac(pk, ob, tc, sl):
            ssl = slice(1024 * (tc % 2), 1024 * (tc % 2) + 512)
            p.s(lambda E, pk=pk, ssl=ssl, ob=ob: E.activation(
                stg[:, ssl], pk.h[:, :], AF.Gelu, scale=col_of(scale_base + ob)),
                reads=(pk.buf, ar.vec[0].buf), writes=(ar.Sg[1].buf,))
            p.dma("sync", gbuf[ob * 128:(ob + 1) * 128, sl], stg[:, ssl],
                  reads=(ar.Sg[1].buf,), writes=(gbufb,))

        dense4096(hT, hTb, ar.A, 32, evac)
        for i in range(32):
            p.dma("sync", ar.A[i].h[:, :], gbuf[i * 128:(i + 1) * 128, :],
                  reads=(gbufb,), writes=(ar.A[i].buf,))

    mid_phase(h1T, h1b, g1buf, g1b, 64)
    mid_phase(h2T, h2b, g2buf, g2b, 96)

    def evac_out(pk, ob, tc, sl):
        if tc == 0:
            p.dma("sync", ar.Sg[1].h[:, :], x_src[ob * 128:(ob + 1) * 128, :],
                  reads=(xsb,), writes=(ar.Sg[1].buf,))
        p.v(lambda E, sl=sl, pk=pk, ob=ob: E.scalar_tensor_tensor(
            ar.Sg[0].h[:, sl], pk.h[:, :], col_of(136 + ob), ar.Sg[1].h[:, sl],
            ALU.mult, ALU.add),
            reads=(pk.buf, ar.Sg[1].buf, ar.vec[0].buf), writes=(ar.Sg[0].buf,))
        if tc == 3:
            p.dma("sync", x_dst[ob * 128:(ob + 1) * 128, :], ar.Sg[0].h[:, :],
                  reads=(ar.Sg[0].buf,), writes=(xdb,))

    dense4096(w3T, w3b, ar.A, FD, evac_out)




def emit_attn_l3(p, ar, io, wd, l):
    x_src, xsb, x_dst, xdb = io
    qkvT, qkvb = wd["L3_attn_qkv_w"]
    projT, projb = wd["L3_attn_proj_w"]
    pbias, pbb = wd["L3_attn_proj_bias"]
    trow, trb = wd["L3_attn_temp"]
    onesrow, orb = wd["c_onesrow"]

    emit_layernorm(p, ar, x_src, xsb)

    H, HD = 16, 64

    for ob in range(16):
        slot = ar.W[ob % 2]
        load_wtile(p, ar, slot, qkvT, qkvb, None, 8, ob)
        banks = [ar.bank() for _ in range(4)]
        for tc in range(4):
            sl = slice(512 * tc, 512 * (tc + 1))
            for c in range(FD):
                p.t(lambda E, c=c, sl=sl, pk=banks[tc], slot=slot: E.matmul(
                    pk.h[:, :], slot.h[:, c * 128:(c + 1) * 128], ar.N[c].h[:, sl],
                    start=(c == 0), stop=(c == FD - 1)),
                    reads=(slot.buf, ar.N[c].buf), writes=(banks[tc].buf,))
        dst = ar.A[ob]
        for tc in range(4):
            sl = slice(512 * tc, 512 * (tc + 1))
            p.s(lambda E, sl=sl, pk=banks[tc], dst=dst: E.activation(
                dst.h[:, sl], pk.h[:, :], AF.Copy),
                reads=(banks[tc].buf,), writes=(dst.buf,))

    qkvR, qkvRb = wd["L3_attn_qkv_raw"]
    for c in range(FD):
        vw = ar.A[24 + c // 2]
        off = (c % 2) * 1024
        in_ap = qkvR[c * 128:(c + 1) * 128, 0:1024]
        p.dma("gpsimd", vw.h[:, off:off + 1024], in_ap, reads=(qkvRb,), writes=(vw.buf,))
    for tb in range(16):
        for dvc in range(2):
            pk = ar.bank()
            for c in range(FD):
                vw = ar.A[24 + c // 2]
                off = (c % 2) * 1024 + dvc * 512
                p.t(lambda E, c=c, tb=tb, pk=pk, vw=vw, off=off: E.matmul(
                    pk.h[:, :], ar.N[c].h[:, tb * 128:(tb + 1) * 128],
                    vw.h[:, off:off + 512],
                    start=(c == 0), stop=(c == FD - 1)),
                    reads=(ar.N[c].buf, vw.buf), writes=(pk.buf,))
            vt = ar.A[16 + tb // 2]
            voff = (tb % 2) * 1024 + dvc * 512
            p.s(lambda E, pk=pk, vt=vt, voff=voff: E.activation(
                vt.h[:, voff:voff + 512], pk.h[:, :], AF.Copy),
                reads=(pk.buf,), writes=(vt.buf,))

    for which, base, mdst in (("q", 0, ar.M[0]), ("k", 8, ar.M[1])):
        for tc in range(4):
            sl = slice(512 * tc, 512 * (tc + 1))
            for i in range(FD):
                sqt = ar.A[24 + i % 2]
                p.s(lambda E, i=i, sl=sl, base=base, sqt=sqt: E.activation(
                    sqt.h[:, 0:512], ar.A[base + i].h[:, sl], AF.Square),
                    reads=(ar.A[base + i].buf,), writes=(sqt.buf,))
                p.t(lambda E, i=i, sqt=sqt: E.matmul(
                    ar.P[4].h[:, :], ar.ones.h[:, :], sqt.h[:, 0:512],
                    start=(i == 0), stop=(i == FD - 1)),
                    reads=(ar.ones.buf, sqt.buf), writes=(ar.P[4].buf,))
            p.s(lambda E: E.activation(ar.P[5].h[:, :], ar.P[4].h[:, :], AF.Ln),
                reads=(ar.P[4].buf,), writes=(ar.P[5].buf,))
            p.s(lambda E: E.activation(ar.Sg[1].h[:, 0:512], ar.P[5].h[:, :], AF.Exp, scale=0.5),
                reads=(ar.P[5].buf,), writes=(ar.Sg[1].buf,))
            p.v(lambda E: E.tensor_scalar_max(ar.Sg[1].h[:, 512:1024], ar.Sg[1].h[:, 0:512], 1e-12),
                reads=(ar.Sg[1].buf,), writes=(ar.Sg[1].buf,))
            p.v(lambda E: E.reciprocal(ar.Sg[1].h[:, 1024:1536], ar.Sg[1].h[:, 512:1024]),
                reads=(ar.Sg[1].buf,), writes=(ar.Sg[1].buf,))
            p.v(lambda E, mdst=mdst, sl=sl: E.tensor_scalar_mul(
                mdst.h[:, sl], ar.Sg[1].h[:, 1024:1536], 0.125),
                reads=(ar.Sg[1].buf,), writes=(mdst.buf,))
    for i in range(FD):
        p.v(lambda E, i=i: E.tensor_mul(ar.A[i].h[:, :], ar.A[i].h[:, :], ar.M[0].h[:, :]),
            reads=(ar.A[i].buf, ar.M[0].buf), writes=(ar.A[i].buf,))
        p.v(lambda E, i=i: E.tensor_mul(ar.A[8 + i].h[:, :], ar.A[8 + i].h[:, :], ar.M[1].h[:, :]),
            reads=(ar.A[8 + i].buf, ar.M[1].buf), writes=(ar.A[8 + i].buf,))

    tr = ar.vec[1]
    p.dma("sync", tr.h[0:1, 0:16], trow, reads=(trb,), writes=(tr.buf,))
    p.dma("sync", tr.h[0:1, 32:160], onesrow, reads=(orb,), writes=(tr.buf,))
    p.v(lambda E: E.reciprocal(tr.h[0:1, 16:32], tr.h[0:1, 0:16]),
        reads=(tr.buf,), writes=(tr.buf,))
    p.t(lambda E: E.matmul(ar.P[4].h[:, 0:16], tr.h[0:1, 32:160], tr.h[0:1, 16:32],
                           start=True, stop=True),
        reads=(tr.buf,), writes=(ar.P[4].buf,))
    p.s(lambda E: E.activation(ar.col[0].h[:, 0:16], ar.P[4].h[:, 0:16], AF.Copy),
        reads=(ar.P[4].buf,), writes=(ar.col[0].buf,))

    def l3_iter(idx, h, qb, stage):
        pi = idx % 2
        sg = ar.Sg[pi]
        P_sb = sg.h[:, :].bitcast(BF16)
        Pt_sb = ar.M[pi]
        cl = ar.col[1 + pi]
        qa = ar.A[h // 2]
        ka = ar.A[8 + h // 2]
        prow = 64 * (h % 2)
        banks = [ar.P[k] for k in range(4)]
        pvb = ar.P[4 + pi]
        if stage == 0:
            for kc in range(4):
                pk = banks[kc]
                p.t(lambda E, pk=pk, kc=kc, qa=qa, ka=ka, prow=prow, qb=qb: E.matmul(
                    pk.h[:, :], qa.h[prow:prow + 64, qb * 128:(qb + 1) * 128],
                    ka.h[prow:prow + 64, 512 * kc:512 * (kc + 1)],
                    start=True, stop=True),
                    reads=(qa.buf, ka.buf), writes=(pk.buf,))
                p.v(lambda E, pk=pk, kc=kc, cl=cl, h=h: E.reduce_max(
                    cl.h[:, 8 + kc:9 + kc], pk.h[:, :], AX.X),
                    reads=(pk.buf,), writes=(cl.buf,))
        elif stage == 1:
            p.v(lambda E, cl=cl: E.reduce_max(cl.h[:, 0:1], cl.h[:, 8:12], AX.X, negate=True),
                reads=(cl.buf,), writes=(cl.buf,))
            p.v(lambda E, cl=cl, h=h: E.tensor_scalar(
                cl.h[:, 1:2], cl.h[:, 0:1], ar.col[0].h[:, h:h + 1], None, ALU.mult),
                reads=(cl.buf, ar.col[0].buf), writes=(cl.buf,))
            p.group("scalar",
                    [(lambda E, pk=banks[kc], kc=kc, cl=cl, h=h, P_sb=P_sb: E.activation(
                        P_sb[:, 512 * kc:512 * (kc + 1)], pk.h[:, :], AF.Exp,
                        bias=cl.h[:, 1:2], scale=ar.col[0].h[:, h:h + 1],
                        accum_out=cl.h[:, 16 + kc:17 + kc]))
                     for kc in range(4)],
                    reads=tuple(b.buf for b in banks) + (cl.buf, ar.col[0].buf),
                    writes=(sg.buf, cl.buf))
            p.v(lambda E, cl=cl: E.reduce_sum(cl.h[:, 2:3], cl.h[:, 16:20], AX.X),
                reads=(cl.buf,), writes=(cl.buf,))
            p.v(lambda E, cl=cl: E.reciprocal(cl.h[:, 3:4], cl.h[:, 2:3]),
                reads=(cl.buf,), writes=(cl.buf,))
            p.v(lambda E, cl=cl, P_sb=P_sb, sg=sg: E.tensor_scalar(
                P_sb[:, 0:2048], P_sb[:, 0:2048], cl.h[:, 3:4], None, ALU.mult),
                reads=(sg.buf, cl.buf), writes=(sg.buf,))
        else:
            tp = ar.Tp[pi]
            for g in range(2):
                for j in range(8):
                    kc = g * 8 + j
                    p.t(lambda E, tp=tp, j=j, kc=kc, P_sb=P_sb, sg=sg: E.transpose(
                        tp.h[:, j * 128:(j + 1) * 128], P_sb[:, kc * 128:(kc + 1) * 128],
                        ar.ident.h[:, :]),
                        reads=(sg.buf, ar.ident.buf), writes=(tp.buf,))
                p.s(lambda E, tp=tp, g=g, Pt_sb=Pt_sb: E.activation(
                    Pt_sb.h[:, g * 1024:(g + 1) * 1024], tp.h[:, :], AF.Copy),
                    reads=(tp.buf,), writes=(Pt_sb.buf,))
            pk = pvb
            for kc in range(16):
                vt = ar.A[16 + kc // 2]
                voff = (kc % 2) * 1024 + h * 64
                p.t(lambda E, pk=pk, kc=kc, vt=vt, voff=voff, Pt_sb=Pt_sb: E.matmul(
                    pk.h[0:64, 0:128], vt.h[:, voff:voff + 64],
                    Pt_sb.h[:, kc * 128:(kc + 1) * 128],
                    start=(kc == 0), stop=(kc == 15)),
                    reads=(vt.buf, Pt_sb.buf), writes=(pk.buf,))
            od = ar.A[24 + h // 2]
            p.v(lambda E, pk=pk, od=od, prow=prow, qb=qb: E.tensor_copy(
                od.h[prow:prow + 64, qb * 128:(qb + 1) * 128], pk.h[0:64, 0:128]),
                reads=(pk.buf,), writes=(od.buf,))

    seq3 = [(h, qb) for h in range(H) for qb in range(16)]
    for idx, (h, qb) in enumerate(seq3):
        l3_iter(idx, h, qb, 0)
        if idx:
            l3_iter(idx - 1, *seq3[idx - 1], 2)
        l3_iter(idx, h, qb, 1)
    l3_iter(len(seq3) - 1, *seq3[-1], 2)

    p.dma("sync", ar.vec[2].h[:, 0:8], pbias[:, :], reads=(pbb,), writes=(ar.vec[2].buf,))
    for ob in range(FD):
        slot = ar.W[ob % 2]
        load_wtile(p, ar, slot, projT, projb, None, 8, ob)
        p.dma("sync", ar.Sg[1].h[:, :], x_src[ob * 128:(ob + 1) * 128, :],
              reads=(xsb,), writes=(ar.Sg[1].buf,))
        banks = [ar.bank() for _ in range(4)]
        for tc in range(4):
            sl = slice(512 * tc, 512 * (tc + 1))
            for c in range(FD):
                p.t(lambda E, c=c, sl=sl, pk=banks[tc], slot=slot: E.matmul(
                    pk.h[:, :], slot.h[:, c * 128:(c + 1) * 128], ar.A[24 + c].h[:, sl],
                    start=(c == 0), stop=(c == FD - 1)),
                    reads=(slot.buf, ar.A[24 + c].buf), writes=(banks[tc].buf,))
        for tc in range(4):
            sl = slice(512 * tc, 512 * (tc + 1))
            pk = banks[tc]
            p.v(lambda E, sl=sl, pk=pk, ob=ob: E.scalar_tensor_tensor(
                ar.Sg[0].h[:, sl], pk.h[:, :], ar.vec[2].h[:, ob:ob + 1], ar.Sg[1].h[:, sl],
                ALU.add, ALU.add),
                reads=(pk.buf, ar.Sg[1].buf, ar.vec[2].buf), writes=(ar.Sg[0].buf,))
        p.dma("sync", x_dst[ob * 128:(ob + 1) * 128, :], ar.Sg[0].h[:, :],
              reads=(ar.Sg[0].buf,), writes=(xdb,))


def decl_attn_l3(nc, wd, l):
    shapes = {
        "L3_attn_qkv_w": ((16, 128, 1024), F32),
        "L3_attn_qkv_raw": ((1024, 1024), F32),
        "L3_attn_proj_w": ((8, 128, 1024), F32),
        "L3_attn_proj_bias": ((128, 8), F32),
        "L3_attn_temp": ((1, 16), F32),
        "c_onesrow": ((1, 128), F32),
    }
    for k, (shp, dt) in shapes.items():
        if k not in wd:
            h = nc.dram_tensor(k, list(shp), dt, kind="ExternalInput")
            wd[k] = (h.ap(), Buf(k))


def prep_attn_l3(inputs, l):
    d = {}
    qkvT = _t(inputs["L3_attn_qkv_w"])
    d["L3_attn_qkv_w"] = _wtiles(qkvT[:, 0:2048], 8, 16)
    d["L3_attn_qkv_raw"] = np.ascontiguousarray(qkvT[:, 2048:3072])
    d["L3_attn_proj_w"] = _wtiles(_t(inputs["L3_attn_proj_w"]), 8, 8)
    d["L3_attn_proj_bias"] = _cols(inputs["L3_attn_proj_b"])
    d["L3_attn_temp"] = np.ascontiguousarray(
        np.asarray(inputs["L3_attn_temperature"], np.float32).reshape(1, 16))
    d["c_onesrow"] = np.ones((1, 128), np.float32)
    return d




def emit_rms_stats(p, ar, src_tiles, eps, sq_scratch):
    inv = 1.0 / float(D)
    for tc in range(4):
        sl = slice(512 * tc, 512 * (tc + 1))
        for i in range(FD):
            sqt = sq_scratch[i % 2]
            p.s(lambda E, i=i, sl=sl, sqt=sqt: E.activation(
                sqt.h[:, 0:512], src_tiles[i].h[:, sl], AF.Square),
                reads=(src_tiles[i].buf,), writes=(sqt.buf,))
            p.t(lambda E, i=i, sqt=sqt: E.matmul(
                ar.P[4].h[:, :], ar.ones.h[:, :], sqt.h[:, 0:512],
                start=(i == 0), stop=(i == FD - 1)),
                reads=(ar.ones.buf, sqt.buf), writes=(ar.P[4].buf,))
        p.s(lambda E, eps=eps: E.activation(ar.P[5].h[:, :], ar.P[4].h[:, :], AF.Ln,
                                            scale=inv, bias=eps),
            reads=(ar.P[4].buf,), writes=(ar.P[5].buf,))
        p.s(lambda E, sl=sl: E.activation(ar.M[0].h[:, sl], ar.P[5].h[:, :], AF.Exp, scale=-0.5),
            reads=(ar.P[5].buf,), writes=(ar.M[0].buf,))


FFN0_PHASE = 3


def emit_ffn0(p, ar, io, wd, l):
    x_src, xsb, x_dst, xdb = io
    wqT, wqb = wd["L0_ffn_wq"]
    wkT, wkb = wd["L0_ffn_wk"]
    wvT, wvb = wd["L0_ffn_wv"]
    woT, wob = wd["L0_ffn_wo"]
    gwT, gwb = wd["L0_ffn_gate_w"]
    expT, expb = wd["L0_ffn_expert_w"]
    anw, anwb = wd["L0_ffn_attn_norm_w"]
    fnw, fnwb = wd["L0_ffn_ffn_norm_w"]
    H, HD, KVH = 16, 64, 4

    emit_layernorm(p, ar, x_src, xsb)

    p.dma("sync", ar.vec[2].h[:, 0:8], anw[:, :], reads=(anwb,), writes=(ar.vec[2].buf,))
    p.dma("sync", ar.vec[2].h[:, 8:16], fnw[:, :], reads=(fnwb,), writes=(ar.vec[2].buf,))
    p.dma("gpsimd", ar.gwt.h[:, 0:64], gwT[0, :, :], reads=(gwb,), writes=(ar.gwt.buf,))

    emit_rms_stats(p, ar, ar.N, 1e-6, (ar.A[30], ar.A[31]))
    for i in range(FD):
        p.v(lambda E, i=i: E.scalar_tensor_tensor(
            ar.A[i].h[:, :], ar.N[i].h[:, :], ar.vec[2].h[:, i:i + 1], ar.M[0].h[:, :],
            ALU.mult, ALU.mult),
            reads=(ar.N[i].buf, ar.vec[2].buf, ar.M[0].buf), writes=(ar.A[i].buf,))
    rn = ar.A[0:8]

    for ob in range(8):
        slot = ar.W[ob % 2]
        load_wtile(p, ar, slot, wqT, wqb, None, 8, ob)
        banks = [ar.bank() for _ in range(4)]
        for tc in range(4):
            sl = slice(512 * tc, 512 * (tc + 1))
            for c in range(FD):
                p.t(lambda E, c=c, sl=sl, pk=banks[tc], slot=slot: E.matmul(
                    pk.h[:, :], slot.h[:, c * 128:(c + 1) * 128], rn[c].h[:, sl],
                    start=(c == 0), stop=(c == FD - 1)),
                    reads=(slot.buf, rn[c].buf), writes=(banks[tc].buf,))
        for tc in range(4):
            sl = slice(512 * tc, 512 * (tc + 1))
            p.s(lambda E, sl=sl, pk=banks[tc], ob=ob: E.activation(
                ar.A[8 + ob].h[:, sl], pk.h[:, :], AF.Copy, scale=0.125),
                reads=(banks[tc].buf,), writes=(ar.A[8 + ob].buf,))
    for ob in range(2):
        slot = ar.W[ob % 2]
        load_wtile(p, ar, slot, wkT, wkb, None, 8, ob)
        banks = [ar.bank() for _ in range(4)]
        for tc in range(4):
            sl = slice(512 * tc, 512 * (tc + 1))
            for c in range(FD):
                p.t(lambda E, c=c, sl=sl, pk=banks[tc], slot=slot: E.matmul(
                    pk.h[:, :], slot.h[:, c * 128:(c + 1) * 128], rn[c].h[:, sl],
                    start=(c == 0), stop=(c == FD - 1)),
                    reads=(slot.buf, rn[c].buf), writes=(banks[tc].buf,))
        for tc in range(4):
            sl = slice(512 * tc, 512 * (tc + 1))
            p.s(lambda E, sl=sl, pk=banks[tc], ob=ob: E.activation(
                ar.A[16 + ob].h[:, sl], pk.h[:, :], AF.Copy),
                reads=(banks[tc].buf,), writes=(ar.A[16 + ob].buf,))

    p.dma("gpsimd", ar.W[0].h[:, :], wvT[0, :, :], reads=(wvb,), writes=(ar.W[0].buf,))
    for tb in range(16):
        pk = ar.bank()
        for c in range(FD):
            p.t(lambda E, c=c, tb=tb, pk=pk: E.matmul(
                pk.h[:, 0:256], rn[c].h[:, tb * 128:(tb + 1) * 128],
                ar.W[0].h[:, c * 256:(c + 1) * 256],
                start=(c == 0), stop=(c == FD - 1)),
                reads=(rn[c].buf, ar.W[0].buf), writes=(pk.buf,))
        vt = ar.A[18 + tb // 8]
        p.s(lambda E, pk=pk, vt=vt, tb=tb: E.activation(
            vt.h[:, (tb % 8) * 256:(tb % 8 + 1) * 256], pk.h[:, 0:256], AF.Copy),
            reads=(pk.buf,), writes=(vt.buf,))

    def gqa_iter(idx, h, qb, stage):
        pi = idx % 2
        sg = ar.Sg[pi]
        P_sb = sg.h[:, :].bitcast(BF16)
        Pt_sb = ar.M[pi]
        cl = ar.col[1 + pi]
        qa = ar.A[8 + h // 2]
        prow = 64 * (h % 2)
        ka = ar.A[20 + h % 2]
        krow = prow
        vis = (qb + 1) * 128
        nfull = vis // 512
        rem = vis - nfull * 512
        nch = qb + 1
        nb = nfull + (1 if rem else 0)
        widths = [512] * nfull + ([rem] if rem else [])
        dbank = (vis - 1) // 512
        doff = (vis - 128) - 512 * dbank
        if stage == 0:
            if qb == 0:
                kvh = h // 4
                ka_src = ar.A[16 + kvh // 2]
                skrow = 64 * (kvh % 2)
                p.s(lambda E, ka=ka, ka_src=ka_src, skrow=skrow, prow=prow: E.activation(
                    ka.h[prow:prow + 64, :], ka_src.h[skrow:skrow + 64, :], AF.Copy),
                    reads=(ka_src.buf,), writes=(ka.buf,))
            for kc in range(nb):
                pk = ar.P[kc]
                w = widths[kc]
                p.t(lambda E, pk=pk, kc=kc, w=w, qa=qa, ka=ka, krow=krow, prow=prow, qb=qb: E.matmul(
                    pk.h[:, 0:w], qa.h[prow:prow + 64, qb * 128:(qb + 1) * 128],
                    ka.h[krow:krow + 64, 512 * kc:512 * kc + w],
                    start=True, stop=True),
                    reads=(qa.buf, ka.buf), writes=(pk.buf,))
                if kc == dbank:
                    p.v(lambda E, pk=pk, doff=doff: E.tensor_add(
                        pk.h[:, doff:doff + 128], pk.h[:, doff:doff + 128], ar.tri.h[:, :]),
                        reads=(pk.buf, ar.tri.buf), writes=(pk.buf,))
                p.v(lambda E, pk=pk, kc=kc, w=w, cl=cl: E.reduce_max(
                    cl.h[:, 8 + kc:9 + kc], pk.h[:, 0:w], AX.X),
                    reads=(pk.buf,), writes=(cl.buf,))
        elif stage == 1:
            p.v(lambda E, cl=cl, nb=nb: E.reduce_max(
                cl.h[:, 0:1], cl.h[:, 8:8 + nb], AX.X, negate=True),
                reads=(cl.buf,), writes=(cl.buf,))
            p.group("scalar",
                    [(lambda E, pk=ar.P[kc], kc=kc, w=widths[kc], cl=cl, P_sb=P_sb: E.activation(
                        P_sb[:, 512 * kc:512 * kc + w], pk.h[:, 0:w], AF.Exp,
                        bias=cl.h[:, 0:1], accum_out=cl.h[:, 16 + kc:17 + kc]))
                     for kc in range(nb)],
                    reads=tuple(ar.P[kc].buf for kc in range(nb)) + (cl.buf,),
                    writes=(sg.buf, cl.buf))
            p.v(lambda E, cl=cl, nb=nb: E.reduce_sum(
                cl.h[:, 1:2], cl.h[:, 16:16 + nb], AX.X),
                reads=(cl.buf,), writes=(cl.buf,))
            p.v(lambda E, cl=cl: E.reciprocal(cl.h[:, 2:3], cl.h[:, 1:2]),
                reads=(cl.buf,), writes=(cl.buf,))
            p.v(lambda E, vis=vis, cl=cl, P_sb=P_sb, sg=sg: E.tensor_scalar(
                P_sb[:, 0:vis], P_sb[:, 0:vis], cl.h[:, 2:3], None, ALU.mult),
                reads=(sg.buf, cl.buf), writes=(sg.buf,))
        else:
            tp = ar.Tp[pi]
            for g in range((nch + 7) // 8):
                lo = g * 8
                hi = min(nch, lo + 8)
                for j in range(lo, hi):
                    p.t(lambda E, tp=tp, j=j, lo=lo, P_sb=P_sb, sg=sg: E.transpose(
                        tp.h[:, (j - lo) * 128:(j - lo + 1) * 128],
                        P_sb[:, j * 128:(j + 1) * 128], ar.ident.h[:, :]),
                        reads=(sg.buf, ar.ident.buf), writes=(tp.buf,))
                p.s(lambda E, tp=tp, lo=lo, hi=hi, Pt_sb=Pt_sb: E.activation(
                    Pt_sb.h[:, lo * 128:hi * 128], tp.h[:, 0:(hi - lo) * 128], AF.Copy),
                    reads=(tp.buf,), writes=(Pt_sb.buf,))
            pk = ar.P[4 + pi]
            for kc in range(nch):
                vt = ar.A[18 + kc // 8]
                voff = (kc % 8) * 256 + (h // 4) * 64
                p.t(lambda E, pk=pk, kc=kc, vt=vt, voff=voff, nch=nch, Pt_sb=Pt_sb: E.matmul(
                    pk.h[0:64, 0:128], vt.h[:, voff:voff + 64],
                    Pt_sb.h[:, kc * 128:(kc + 1) * 128],
                    start=(kc == 0), stop=(kc == nch - 1)),
                    reads=(vt.buf, Pt_sb.buf), writes=(pk.buf,))
            od = ar.A[24 + h // 2]
            p.s(lambda E, pk=pk, od=od, prow=prow, qb=qb: E.activation(
                od.h[prow:prow + 64, qb * 128:(qb + 1) * 128], pk.h[0:64, 0:128], AF.Copy),
                reads=(pk.buf,), writes=(od.buf,))

    seqg = [(h, qb) for h in range(H) for qb in range(16)]
    for idx, (h, qb) in enumerate(seqg):
        gqa_iter(idx, h, qb, 0)
        if idx:
            gqa_iter(idx - 1, *seqg[idx - 1], 2)
        gqa_iter(idx, h, qb, 1)
    gqa_iter(len(seqg) - 1, *seqg[-1], 2)

    for ob in range(FD):
        slot = ar.W[ob % 2]
        load_wtile(p, ar, slot, woT, wob, None, 8, ob)
        banks = [ar.bank() for _ in range(4)]
        for tc in range(4):
            sl = slice(512 * tc, 512 * (tc + 1))
            for c in range(FD):
                p.t(lambda E, c=c, sl=sl, pk=banks[tc], slot=slot: E.matmul(
                    pk.h[:, :], slot.h[:, c * 128:(c + 1) * 128], ar.A[24 + c].h[:, sl],
                    start=(c == 0), stop=(c == FD - 1)),
                    reads=(slot.buf, ar.A[24 + c].buf), writes=(banks[tc].buf,))
        for tc in range(4):
            sl = slice(512 * tc, 512 * (tc + 1))
            ht = ar.A[8 + 2 * ob + tc // 2]
            hv = f32view(ht)[:, (tc % 2) * 512:(tc % 2) * 512 + 512]
            p.v(lambda E, sl=sl, pk=banks[tc], ob=ob, hv=hv: E.tensor_add(
                hv, pk.h[:, :], ar.N[ob].h[:, sl]),
                reads=(banks[tc].buf, ar.N[ob].buf), writes=(ht.buf,))

    if FFN0_PHASE == 1:
        for ob in range(FD):
            p.dma("sync", x_dst[ob * 128:(ob + 1) * 128, 0:1024],
                  f32view(ar.A[8 + 2 * ob]), reads=(ar.A[8 + 2 * ob].buf,), writes=(xdb,))
            p.dma("sync", x_dst[ob * 128:(ob + 1) * 128, 1024:2048],
                  f32view(ar.A[9 + 2 * ob]), reads=(ar.A[9 + 2 * ob].buf,), writes=(xdb,))
        return

    def hint_f32(ob):
        return (f32view(ar.A[8 + 2 * ob]), f32view(ar.A[9 + 2 * ob]),
                ar.A[8 + 2 * ob], ar.A[9 + 2 * ob])

    for ob in range(FD):
        v0, v1, t0, t1 = hint_f32(ob)
        p.s(lambda E, ob=ob, v0=v0: E.activation(ar.N[ob].h[:, 0:1024], v0, AF.Copy),
            reads=(t0.buf,), writes=(ar.N[ob].buf,))
        p.s(lambda E, ob=ob, v1=v1: E.activation(ar.N[ob].h[:, 1024:2048], v1, AF.Copy),
            reads=(t1.buf,), writes=(ar.N[ob].buf,))
    emit_rms_stats(p, ar, ar.N, 1e-6, (ar.A[30], ar.A[31]))
    for i in range(FD):
        p.v(lambda E, i=i: E.scalar_tensor_tensor(
            ar.A[i].h[:, :], ar.N[i].h[:, :], ar.vec[2].h[:, 8 + i:9 + i], ar.M[0].h[:, :],
            ALU.mult, ALU.mult),
            reads=(ar.N[i].buf, ar.vec[2].buf, ar.M[0].buf), writes=(ar.A[i].buf,))

    for tb in range(16):
        pk = ar.bank()
        for c in range(FD):
            p.t(lambda E, c=c, tb=tb, pk=pk: E.matmul(
                pk.h[:, 0:8], ar.A[c].h[:, tb * 128:(tb + 1) * 128],
                ar.gwt.h[:, c * 8:(c + 1) * 8],
                start=(c == 0), stop=(c == FD - 1)),
                reads=(ar.A[c].buf, ar.gwt.buf), writes=(pk.buf,))
        L = ar.Sg[0].h[:, 0:8]
        p.s(lambda E, pk=pk: E.activation(L, pk.h[:, 0:8], AF.Copy),
            reads=(pk.buf,), writes=(ar.Sg[0].buf,))
        cl = ar.col[1].h
        p.v(lambda E: E.reduce_max(cl[:, 0:1], L, AX.X),
            reads=(ar.Sg[0].buf,), writes=(ar.col[1].buf,))
        m1k = ar.Sg[0].h[:, 8:16]
        p.v(lambda E: E.tensor_scalar(m1k, L, cl[:, 0:1], None, ALU.is_ge),
            reads=(ar.Sg[0].buf, ar.col[1].buf), writes=(ar.Sg[0].buf,))
        L2 = ar.Sg[0].h[:, 16:24]
        p.v(lambda E: E.scalar_tensor_tensor(L2, m1k, -1.0e30, L, ALU.mult, ALU.add),
            reads=(ar.Sg[0].buf,), writes=(ar.Sg[0].buf,))
        p.v(lambda E: E.reduce_max(cl[:, 1:2], L2, AX.X),
            reads=(ar.Sg[0].buf,), writes=(ar.col[1].buf,))
        m2k = ar.Sg[0].h[:, 24:32]
        p.v(lambda E: E.tensor_scalar(m2k, L2, cl[:, 1:2], None, ALU.is_ge),
            reads=(ar.Sg[0].buf, ar.col[1].buf), writes=(ar.Sg[0].buf,))
        p.v(lambda E: E.tensor_sub(cl[:, 2:3], cl[:, 1:2], cl[:, 0:1]),
            reads=(ar.col[1].buf,), writes=(ar.col[1].buf,))
        p.s(lambda E: E.activation(cl[:, 3:4], cl[:, 2:3], AF.Exp),
            reads=(ar.col[1].buf,), writes=(ar.col[1].buf,))
        p.v(lambda E: E.tensor_scalar_add(cl[:, 4:5], cl[:, 3:4], 1.0),
            reads=(ar.col[1].buf,), writes=(ar.col[1].buf,))
        p.v(lambda E: E.reciprocal(cl[:, 5:6], cl[:, 4:5]),
            reads=(ar.col[1].buf,), writes=(ar.col[1].buf,))
        p.v(lambda E: E.tensor_scalar_mul(cl[:, 6:7], cl[:, 5:6], 0.5),
            reads=(ar.col[1].buf,), writes=(ar.col[1].buf,))
        p.v(lambda E: E.tensor_scalar(cl[:, 7:8], cl[:, 6:7], -1.0, 0.5, ALU.mult, ALU.add),
            reads=(ar.col[1].buf,), writes=(ar.col[1].buf,))
        comb = ar.Sg[0].h[:, 32:40]
        p.v(lambda E: E.tensor_scalar(comb, m1k, cl[:, 6:7], None, ALU.mult),
            reads=(ar.Sg[0].buf, ar.col[1].buf), writes=(ar.Sg[0].buf,))
        tmp8 = ar.Sg[0].h[:, 40:48]
        p.v(lambda E: E.tensor_scalar(tmp8, m2k, cl[:, 7:8], None, ALU.mult),
            reads=(ar.Sg[0].buf, ar.col[1].buf), writes=(ar.Sg[0].buf,))
        p.v(lambda E: E.tensor_add(comb, comb, tmp8),
            reads=(ar.Sg[0].buf,), writes=(ar.Sg[0].buf,))
        p.s(lambda E: E.activation(ar.A[30].h[:, 0:8], comb, AF.Copy),
            reads=(ar.Sg[0].buf,), writes=(ar.A[30].buf,))
        p.t(lambda E: E.transpose(
            ar.Tp[0].h[0:8, 0:128], ar.A[30].h[:, 0:8], ar.ident.h[:, :]),
            reads=(ar.A[30].buf, ar.ident.buf), writes=(ar.Tp[0].buf,))
        p.s(lambda E, tb=tb: E.activation(
            ar.M[1].h[0:8, tb * 128:(tb + 1) * 128], ar.Tp[0].h[0:8, 0:128], AF.Copy),
            reads=(ar.Tp[0].buf,), writes=(ar.M[1].buf,))

    for e in range(8):
        for tc in range(4):
            sl = slice(512 * tc, 512 * (tc + 1))
            pk = ar.P[4 + tc % 2]
            p.t(lambda E, e=e, sl=sl, pk=pk: E.matmul(
                pk.h[:, :], ar.esel.h[0:8, e * 128:(e + 1) * 128], ar.M[1].h[0:8, sl],
                start=True, stop=True),
                reads=(ar.esel.buf, ar.M[1].buf), writes=(pk.buf,))
            p.s(lambda E, e=e, sl=sl, pk=pk: E.activation(
                ar.A[24 + e].h[:, sl], pk.h[:, :], AF.Copy),
                reads=(pk.buf,), writes=(ar.A[24 + e].buf,))

    if FFN0_PHASE == 2:
        for e in range(8):
            p.dma("gpsimd", x_dst[e * 128:(e + 1) * 128, :],
                  ar.A[24 + e].h[:, :], reads=(ar.A[24 + e].buf,), writes=(xdb,))
        return

    for ob in range(FD):
        p.dma("sync", ar.Sg[1].h[:, :], x_src[ob * 128:(ob + 1) * 128, :],
              reads=(xsb,), writes=(ar.Sg[1].buf,))
        v0, v1, t0, t1 = hint_f32(ob)
        p.v(lambda E, v0=v0: E.scalar_tensor_tensor(
            ar.Sg[0].h[:, 0:1024], v0, 0.5, ar.Sg[1].h[:, 0:1024], ALU.mult, ALU.add),
            reads=(t0.buf, ar.Sg[1].buf), writes=(ar.Sg[0].buf,))
        p.v(lambda E, v1=v1: E.scalar_tensor_tensor(
            ar.Sg[0].h[:, 1024:2048], v1, 0.5, ar.Sg[1].h[:, 1024:2048], ALU.mult, ALU.add),
            reads=(t1.buf, ar.Sg[1].buf), writes=(ar.Sg[0].buf,))
        for e in range(8):
            slot = ar.W[e % 2]
            p.dma("gpsimd", slot.h[:, 0:1024], expT[e, ob, :, :],
                  reads=(expb,), writes=(slot.buf,))
            banks = [ar.bank() for _ in range(4)]
            for tc in range(4):
                sl = slice(512 * tc, 512 * (tc + 1))
                for c in range(FD):
                    p.t(lambda E, c=c, sl=sl, pk=banks[tc], slot=slot: E.matmul(
                        pk.h[:, :], slot.h[:, c * 128:(c + 1) * 128], ar.A[c].h[:, sl],
                        start=(c == 0), stop=(c == FD - 1)),
                        reads=(slot.buf, ar.A[c].buf), writes=(banks[tc].buf,))
            p.group("vector",
                    [(lambda E, tc=tc, pk=banks[tc], e=e: E.tensor_mul(
                        ar.M[0].h[:, 512 * tc:512 * (tc + 1)], pk.h[:, :],
                        ar.A[24 + e].h[:, 512 * tc:512 * (tc + 1)]))
                     for tc in range(4)],
                    reads=(banks[0].buf, banks[1].buf, banks[2].buf, banks[3].buf,
                           ar.A[24 + e].buf),
                    writes=(ar.M[0].buf,))
            p.group("vector",
                    [(lambda E, tc=tc: E.tensor_add(
                        ar.Sg[0].h[:, 512 * tc:512 * (tc + 1)],
                        ar.Sg[0].h[:, 512 * tc:512 * (tc + 1)],
                        ar.M[0].h[:, 512 * tc:512 * (tc + 1)]))
                     for tc in range(4)],
                    reads=(ar.M[0].buf,), writes=(ar.Sg[0].buf,))
        p.dma("sync", x_dst[ob * 128:(ob + 1) * 128, :], ar.Sg[0].h[:, :],
              reads=(ar.Sg[0].buf,), writes=(xdb,))


def decl_ffn0(nc, wd, l):
    shapes = {
        "L0_ffn_wq": (8, 128, 1024), "L0_ffn_wk": (2, 128, 1024), "L0_ffn_wv": (1, 128, 2048),
        "L0_ffn_wo": (8, 128, 1024), "L0_ffn_gate_w": (1, 128, 64),
        "L0_ffn_expert_w": (8, 8, 128, 1024),
        "L0_ffn_attn_norm_w": (128, 8), "L0_ffn_ffn_norm_w": (128, 8),
    }
    for k, shp in shapes.items():
        if k not in wd:
            h = nc.dram_tensor(k, list(shp), F32, kind="ExternalInput")
            wd[k] = (h.ap(), Buf(k))


def prep_ffn0(inputs, l):
    d = {}
    d["L0_ffn_wq"] = _wtiles(_t(inputs["L0_ffn_wq"]), 8, 8)
    d["L0_ffn_wk"] = _wtiles(_t(inputs["L0_ffn_wk"]), 8, 2)
    d["L0_ffn_wv"] = _wtiles(_t(inputs["L0_ffn_wv"]), 8, 1, owid=256)
    d["L0_ffn_wo"] = _wtiles(_t(inputs["L0_ffn_wo"]), 8, 8)
    d["L0_ffn_gate_w"] = _wtiles(_t(inputs["L0_ffn_gate_w"]), 8, 1, owid=8)
    ew = np.asarray(inputs["L0_ffn_expert_w"])
    d["L0_ffn_expert_w"] = np.stack(
        [_wtiles(np.ascontiguousarray(ew[e].T), 8, 8) for e in range(8)])
    d["L0_ffn_attn_norm_w"] = _cols(inputs["L0_ffn_attn_norm_w"])
    d["L0_ffn_ffn_norm_w"] = _cols(inputs["L0_ffn_ffn_norm_w"])
    return d




L1_PHASE = "full"


def emit_attn_l1(p, ar, io, wd, l):
    x_src, xsb, x_dst, xdb = io
    wq, wqb = wd["L1_attn_wq"]
    wk, wkb = wd["L1_attn_wk"]
    wv, wvb = wd["L1_attn_wv"]
    wgq, wgqb = wd["L1_attn_wgq"]
    wgk, wgkb = wd["L1_attn_wgk"]
    wgv, wgvb = wd["L1_attn_wgv"]
    rro, rrob = wd["L1_attn_r_out_w"]
    wout, woutb = wd["L1_attn_w_out"]
    glw, glwb = wd["L1_attn_glw"]
    grw, grwb = wd["L1_attn_grw"]
    lmask, lmaskb = wd["L1_attn_lmask"]
    tril, trilb = wd["c_tril"]

    H, HD = 16, 64

    emit_layernorm(p, ar, x_src, xsb)

    p.dma("gpsimd", ar.gwt.h[:, 0:8], glw[:, :], reads=(glwb,), writes=(ar.gwt.buf,))
    p.dma("gpsimd", ar.gwt.h[:, 8:16], grw[:, :], reads=(grwb,), writes=(ar.gwt.buf,))

    def proj_fm(wT, wTb, n_ob, dst_base, scale=None):
        for ob in range(n_ob):
            slot = ar.W[ob % 2]
            load_wtile(p, ar, slot, wT, wTb, None, 8, ob)
            banks = [ar.bank() for _ in range(4)]
            for tc in range(4):
                sl = slice(512 * tc, 512 * (tc + 1))
                for c in range(FD):
                    p.t(lambda E, c=c, sl=sl, pk=banks[tc], slot=slot: E.matmul(
                        pk.h[:, :], slot.h[:, c * 128:(c + 1) * 128], ar.N[c].h[:, sl],
                        start=(c == 0), stop=(c == FD - 1)),
                        reads=(slot.buf, ar.N[c].buf), writes=(banks[tc].buf,))
            dst = ar.A[dst_base + ob]
            for tc in range(4):
                sl = slice(512 * tc, 512 * (tc + 1))
                p.s(lambda E, sl=sl, pk=banks[tc], dst=dst: E.activation(
                    dst.h[:, sl], pk.h[:, :], AF.Copy),
                    reads=(banks[tc].buf,), writes=(dst.buf,))

    proj_fm(wq, wqb, 8, 0)
    proj_fm(wk, wkb, 8, 8)

    for c in range(FD):
        vw = ar.A[24 + c // 2]
        off = (c % 2) * 1024
        p.dma("gpsimd", vw.h[:, off:off + 1024], wv[c * 128:(c + 1) * 128, :],
              reads=(wvb,), writes=(vw.buf,))
    for tb in range(16):
        for half in range(2):
            pk = ar.bank()
            for c in range(FD):
                vw = ar.A[24 + c // 2]
                off = (c % 2) * 1024 + half * 512
                p.t(lambda E, c=c, tb=tb, pk=pk, half=half, vw=vw, off=off: E.matmul(
                    pk.h[:, :], ar.N[c].h[:, tb * 128:(tb + 1) * 128],
                    vw.h[:, off:off + 512],
                    start=(c == 0), stop=(c == FD - 1)),
                    reads=(ar.N[c].buf, vw.buf), writes=(pk.buf,))
            vt = ar.A[16 + tb // 2]
            p.s(lambda E, pk=pk, vt=vt, tb=tb, half=half: E.activation(
                vt.h[:, (tb % 2) * 1024 + half * 512:(tb % 2) * 1024 + (half + 1) * 512],
                pk.h[:, :], AF.Copy),
                reads=(pk.buf,), writes=(vt.buf,))

    for tb in range(16):
        for h in range(H):
            pg = ar.P[4 + h % 2]
            qa = ar.A[h // 2]
            prow = 64 * (h % 2)
            p.t(lambda E, pg=pg, qa=qa, prow=prow, tb=tb, h=h: E.matmul(
                pg.h[:, h // 2:h // 2 + 1], qa.h[prow:prow + 64, tb * 128:(tb + 1) * 128],
                ar.gwt.h[prow:prow + 64, h // 2:h // 2 + 1],
                start=True, stop=True),
                reads=(qa.buf, ar.gwt.buf), writes=(pg.buf,))
        p.s(lambda E, tb=tb: E.activation(ar.vec[1].h[:, tb * 16:tb * 16 + 8],
                                          ar.P[4].h[:, 0:8], AF.Sigmoid),
            reads=(ar.P[4].buf,), writes=(ar.vec[1].buf,))
        p.s(lambda E, tb=tb: E.activation(ar.vec[1].h[:, tb * 16 + 8:tb * 16 + 16],
                                          ar.P[5].h[:, 0:8], AF.Sigmoid),
            reads=(ar.P[5].buf,), writes=(ar.vec[1].buf,))

    def loc_iter(idx, h, qb, stage):
        pi = idx % 2
        sg = ar.Sg[pi]
        P_sb = sg.h[:, :].bitcast(BF16)
        Pt_sb = ar.M[pi]
        cl = ar.col[1 + pi]
        qa = ar.A[h // 2]
        ka = ar.A[8 + h // 2]
        prow = 64 * (h % 2)
        gsel = h // 2
        k0 = max(0, qb - 2)
        nch = qb - k0 + 1
        vis = nch * 128
        if stage == 0:
            pk = ar.P[2 * pi]
            p.t(lambda E, pk=pk, qa=qa, ka=ka, prow=prow, qb=qb, k0=k0, vis=vis: E.matmul(
                pk.h[:, 0:vis], qa.h[prow:prow + 64, qb * 128:(qb + 1) * 128],
                ka.h[prow:prow + 64, k0 * 128:k0 * 128 + vis],
                start=True, stop=True),
                reads=(qa.buf, ka.buf), writes=(pk.buf,))
            p.s(lambda E, pk=pk, vis=vis, sg=sg: E.activation(
                sg.h[:, 256:256 + vis], pk.h[:, 0:vis], AF.Copy, scale=0.125),
                reads=(pk.buf,), writes=(sg.buf,))
        elif stage == 1:
            if qb >= 2:
                p.v(lambda E, sg=sg: E.tensor_add(
                    sg.h[:, 256:384], sg.h[:, 256:384], ar.trili.h[:, :]),
                    reads=(sg.buf, ar.trili.buf), writes=(sg.buf,))
            p.v(lambda E, vis=vis, sg=sg: E.tensor_add(
                sg.h[:, 256 + vis - 128:256 + vis],
                sg.h[:, 256 + vis - 128:256 + vis], ar.tri.h[:, :]),
                reads=(sg.buf, ar.tri.buf), writes=(sg.buf,))
            p.v(lambda E, vis=vis, sg=sg, cl=cl: E.reduce_max(
                cl.h[:, 0:1], sg.h[:, 256:256 + vis], AX.X, negate=True),
                reads=(sg.buf,), writes=(cl.buf,))
            p.s(lambda E, vis=vis, sg=sg, cl=cl, P_sb=P_sb: E.activation(
                P_sb[:, 0:vis], sg.h[:, 256:256 + vis], AF.Exp,
                bias=cl.h[:, 0:1], accum_out=cl.h[:, 1:2]),
                reads=(sg.buf, cl.buf), writes=(sg.buf, cl.buf))
            p.v(lambda E, cl=cl: E.reciprocal(cl.h[:, 2:3], cl.h[:, 1:2]),
                reads=(cl.buf,), writes=(cl.buf,))
            gli = qb * 16 + (8 if h % 2 else 0) + h // 2
            p.v(lambda E, cl=cl, gli=gli: E.tensor_mul(
                cl.h[:, 3:4], cl.h[:, 2:3], ar.vec[1].h[:, gli:gli + 1]),
                reads=(cl.buf, ar.vec[1].buf), writes=(cl.buf,))
            p.v(lambda E, vis=vis, cl=cl, P_sb=P_sb, sg=sg: E.tensor_scalar(
                P_sb[:, 0:vis], P_sb[:, 0:vis], cl.h[:, 3:4], None, ALU.mult),
                reads=(sg.buf, cl.buf), writes=(sg.buf,))
        else:
            tp = ar.Tp[pi]
            for j in range(nch):
                p.t(lambda E, tp=tp, j=j, P_sb=P_sb, sg=sg: E.transpose(
                    tp.h[:, j * 128:(j + 1) * 128], P_sb[:, j * 128:(j + 1) * 128],
                    ar.ident.h[:, :]),
                    reads=(sg.buf, ar.ident.buf), writes=(tp.buf,))
            p.v(lambda E, tp=tp, nch=nch, Pt_sb=Pt_sb: E.tensor_copy(
                Pt_sb.h[:, 0:nch * 128], tp.h[:, 0:nch * 128]),
                reads=(tp.buf,), writes=(Pt_sb.buf,))
            pk2 = ar.P[4 + pi]
            for j in range(nch):
                kb = k0 + j
                vt = ar.A[16 + kb // 2]
                voff = (kb % 2) * 1024 + h * 64
                p.t(lambda E, pk2=pk2, j=j, vt=vt, voff=voff, nch=nch, Pt_sb=Pt_sb: E.matmul(
                    pk2.h[0:64, 0:128], vt.h[:, voff:voff + 64],
                    Pt_sb.h[:, j * 128:(j + 1) * 128],
                    start=(j == 0), stop=(j == nch - 1)),
                    reads=(vt.buf, Pt_sb.buf), writes=(pk2.buf,))
            od = ar.A[24 + h // 2]
            if L1_PHASE == "remote":
                p.v(lambda E, od=od, prow=prow, qb=qb, pk2=pk2: E.memset(
                    od.h[prow:prow + 64, qb * 128:(qb + 1) * 128], 0.0),
                    reads=(pk2.buf,), writes=(od.buf,))
            else:
                p.s(lambda E, pk2=pk2, od=od, prow=prow, qb=qb: E.activation(
                    od.h[prow:prow + 64, qb * 128:(qb + 1) * 128], pk2.h[0:64, 0:128], AF.Copy),
                    reads=(pk2.buf,), writes=(od.buf,))

    seql = [(h, qb) for h in range(H) for qb in range(16)]
    for idx, (h, qb) in enumerate(seql):
        loc_iter(idx, h, qb, 0)
        if idx:
            loc_iter(idx - 1, *seql[idx - 1], 2)
        loc_iter(idx, h, qb, 1)
    loc_iter(len(seql) - 1, *seql[-1], 2)

    for c in range(FD):
        p.v(lambda E, c=c: E.reduce_sum(
            ar.col[3].h[:, 0:64],
            ar.N[c].h[:, :].rearrange("p (n w) -> p n w", w=32), AX.X),
            reads=(ar.N[c].buf,), writes=(ar.col[3].buf,))
        p.v(lambda E, c=c: E.tensor_scalar_mul(
            ar.vec[1].h[:, :].bitcast(BF16)[:, c * 64:(c + 1) * 64],
            ar.col[3].h[:, 0:64], 1.0 / 32.0),
            reads=(ar.col[3].buf,), writes=(ar.vec[1].buf,))

    proj_fm(wgq, wgqb, 4, 8)
    gkt = ar.vec[2].h[:, :].bitcast(BF16)
    for ob in range(4):
        slot = ar.W[1]
        load_wtile(p, ar, slot, wgk, wgkb, None, 8, ob)
        pk = ar.bank()
        for c in range(FD):
            p.t(lambda E, c=c, pk=pk, slot=slot: E.matmul(
                pk.h[:, 0:64], slot.h[:, c * 128:(c + 1) * 128],
                ar.vec[1].h[:, :].bitcast(BF16)[:, c * 64:(c + 1) * 64],
                start=(c == 0), stop=(c == FD - 1)),
                reads=(slot.buf, ar.vec[1].buf), writes=(pk.buf,))
        p.s(lambda E, pk=pk, ob=ob: E.activation(
            gkt[:, ob * 64:(ob + 1) * 64], pk.h[:, 0:64], AF.Copy),
            reads=(pk.buf,), writes=(ar.vec[2].buf,))
    gvt = ar.vec[3].h[:, :].bitcast(BF16)
    pk = ar.bank()
    slot = ar.W[1]
    for half in range(2):
        p.dma("gpsimd", slot.h[:, :], wgv[0, half, :, :], reads=(wgvb,), writes=(slot.buf,))
        for c in range(4):
            cc = half * 4 + c
            p.t(lambda E, c=c, cc=cc, pk=pk, slot=slot, half=half: E.matmul(
                pk.h[0:64, 0:512], ar.vec[1].h[:, :].bitcast(BF16)[:, cc * 64:(cc + 1) * 64],
                slot.h[:, c * 512:(c + 1) * 512],
                start=(cc == 0), stop=(cc == 7)),
                reads=(ar.vec[1].buf, slot.buf), writes=(pk.buf,))
    p.s(lambda E, pk=pk: E.activation(gvt[0:64, 0:512], pk.h[0:64, 0:512], AF.Copy),
        reads=(pk.buf,), writes=(ar.vec[3].buf,))

    for tb in range(16):
        for h in range(H):
            pg = ar.P[4 + h % 2]
            qa = ar.A[h // 2]
            prow = 64 * (h % 2)
            p.t(lambda E, pg=pg, qa=qa, prow=prow, tb=tb, h=h: E.matmul(
                pg.h[:, h // 2:h // 2 + 1], qa.h[prow:prow + 64, tb * 128:(tb + 1) * 128],
                ar.gwt.h[prow:prow + 64, 8 + h // 2:9 + h // 2],
                start=True, stop=True),
                reads=(qa.buf, ar.gwt.buf), writes=(pg.buf,))
        p.s(lambda E: E.activation(ar.col[2].h[:, 0:8], ar.P[4].h[:, 0:8], AF.Sigmoid),
            reads=(ar.P[4].buf,), writes=(ar.col[2].buf,))
        p.s(lambda E: E.activation(ar.col[2].h[:, 8:16], ar.P[5].h[:, 0:8], AF.Sigmoid),
            reads=(ar.P[5].buf,), writes=(ar.col[2].buf,))
        gt = ar.M[0]
        p.v(lambda E, gt=gt: E.tensor_copy(gt.h[:, 0:16], ar.col[2].h[:, 0:16]),
            reads=(ar.col[2].buf,), writes=(gt.buf,))
        p.t(lambda E, gt=gt: E.transpose(
            ar.Tp[0].h[0:16, 0:128], gt.h[:, 0:16], ar.ident.h[:, :]),
            reads=(gt.buf, ar.ident.buf), writes=(ar.Tp[0].buf,))
        p.s(lambda E, tb=tb: E.activation(
            ar.M[1].h[32:48, tb * 128:(tb + 1) * 128], ar.Tp[0].h[0:16, 0:128], AF.Copy),
            reads=(ar.Tp[0].buf,), writes=(ar.M[1].buf,))

    P_sb = ar.Sg[1].h[:, :].bitcast(BF16)
    for qb in range(16):
        p.dma("sync", ar.col[2].h[:, 0:64], lmask[qb, :, :], reads=(lmaskb,),
              writes=(ar.col[2].buf,))
        for lh in range(8):
            ga = ar.A[8 + lh // 2]
            prow = 64 * (lh % 2)
            pk = ar.P[4 + lh % 2]
            p.t(lambda E, pk=pk, ga=ga, prow=prow, qb=qb, lh=lh: E.matmul(
                pk.h[:, 0:64], ga.h[prow:prow + 64, qb * 128:(qb + 1) * 128],
                ar.vec[2].h[:, :].bitcast(BF16)[prow:prow + 64, (lh // 2) * 64:(lh // 2 + 1) * 64],
                start=True, stop=True),
                reads=(ga.buf, ar.vec[2].buf), writes=(pk.buf,))
            p.s(lambda E, pk=pk: E.activation(
                ar.Sg[0].h[:, 1024:1088], pk.h[:, 0:64], AF.Copy, scale=0.125),
                reads=(pk.buf,), writes=(ar.Sg[0].buf,))
            p.v(lambda E: E.tensor_add(ar.Sg[0].h[:, 1024:1088], ar.Sg[0].h[:, 1024:1088],
                                       ar.col[2].h[:, 0:64]),
                reads=(ar.Sg[0].buf, ar.col[2].buf), writes=(ar.Sg[0].buf,))
            p.v(lambda E: E.reduce_max(ar.col[1].h[:, 0:1], ar.Sg[0].h[:, 1024:1088],
                                       AX.X, negate=True),
                reads=(ar.Sg[0].buf,), writes=(ar.col[1].buf,))
            p.s(lambda E: E.activation(
                P_sb[:, 0:64], ar.Sg[0].h[:, 1024:1088], AF.Exp,
                bias=ar.col[1].h[:, 0:1], accum_out=ar.col[1].h[:, 1:2]),
                reads=(ar.Sg[0].buf, ar.col[1].buf), writes=(ar.Sg[1].buf, ar.col[1].buf))
            p.v(lambda E: E.reciprocal(ar.col[1].h[:, 2:3], ar.col[1].h[:, 1:2]),
                reads=(ar.col[1].buf,), writes=(ar.col[1].buf,))
            p.v(lambda E: E.tensor_scalar(
                P_sb[:, 0:64], P_sb[:, 0:64], ar.col[1].h[:, 2:3], None, ALU.mult),
                reads=(ar.Sg[1].buf, ar.col[1].buf), writes=(ar.Sg[1].buf,))
            p.t(lambda E: E.transpose(
                ar.Tp[0].h[0:64, 0:128], P_sb[:, 0:64], ar.ident.h[:, :]),
                reads=(ar.Sg[1].buf, ar.ident.buf), writes=(ar.Tp[0].buf,))
            p.s(lambda E: E.activation(ar.M[0].h[0:64, 0:128], ar.Tp[0].h[0:64, 0:128],
                                       AF.Copy),
                reads=(ar.Tp[0].buf,), writes=(ar.M[0].buf,))
            pk2 = ar.bank()
            p.t(lambda E, pk2=pk2, lh=lh: E.matmul(
                pk2.h[0:64, 0:128], gvt[0:64, lh * 64:(lh + 1) * 64],
                ar.M[0].h[0:64, 0:128],
                start=True, stop=True),
                reads=(ar.vec[3].buf, ar.M[0].buf), writes=(pk2.buf,))
            rd = ar.A[12 + lh // 2]
            p.s(lambda E, pk2=pk2, rd=rd, prow=prow, qb=qb: E.activation(
                rd.h[prow:prow + 64, qb * 128:(qb + 1) * 128], pk2.h[0:64, 0:128], AF.Copy),
                reads=(pk2.buf,), writes=(rd.buf,))

    if L1_PHASE == "lat_dbg":
        p.dma("gpsimd", x_dst[384:512, :], ar.N[0].h[:, :],
              reads=(ar.N[0].buf,), writes=(xdb,))
        p.dma("sync", x_dst[512:640, 0:512], ar.Sg[0].h[:, 1024:1536],
              reads=(ar.Sg[0].buf,), writes=(xdb,))
        gk_f = ar.vec[2].h[:, :].bitcast(BF16)
        gv_f = ar.vec[3].h[:, :].bitcast(BF16)
        p.dma("gpsimd", x_dst[0:128, 0:512], gk_f[:, 0:512],
              reads=(ar.vec[2].buf,), writes=(xdb,))
        p.dma("gpsimd", x_dst[128:256, 0:512], gv_f[:, 0:512],
              reads=(ar.vec[3].buf,), writes=(xdb,))
        p.dma("gpsimd", x_dst[256:384, 0:512], ar.W[0].h[:, 0:512],
              reads=(ar.W[0].buf,), writes=(xdb,))
        return

    if L1_PHASE == "rem_raw":
        for i in range(4):
            p.dma("gpsimd", x_dst[i * 128:(i + 1) * 128, :], ar.A[12 + i].h[:, :],
                  reads=(ar.A[12 + i].buf,), writes=(xdb,))
        return

    for ob in ([] if L1_PHASE == "local" else range(FD)):
        slot = ar.W[ob % 2]
        p.dma("gpsimd", slot.h[:, 0:512], rro[ob, :, :], reads=(rrob,), writes=(slot.buf,))
        for tc in range(4):
            sl = slice(512 * tc, 512 * (tc + 1))
            pg = ar.P[4 + tc % 2]
            p.t(lambda E, ob=ob, sl=sl, pg=pg: E.matmul(
                pg.h[:, :], ar.esel.h[32:48, ob * 128:(ob + 1) * 128],
                ar.M[1].h[32:48, sl], start=True, stop=True),
                reads=(ar.esel.buf, ar.M[1].buf), writes=(pg.buf,))
            p.s(lambda E, sl=sl, pg=pg: E.activation(
                ar.M[0].h[:, sl], pg.h[:, :], AF.Copy),
                reads=(pg.buf,), writes=(ar.M[0].buf,))
        banks = [ar.bank() for _ in range(4)]
        for tc in range(4):
            sl = slice(512 * tc, 512 * (tc + 1))
            for c in range(4):
                p.t(lambda E, c=c, sl=sl, pk=banks[tc], slot=slot: E.matmul(
                    pk.h[:, :], slot.h[:, c * 128:(c + 1) * 128], ar.A[12 + c].h[:, sl],
                    start=(c == 0), stop=(c == 3)),
                    reads=(slot.buf, ar.A[12 + c].buf), writes=(banks[tc].buf,))
        for tc in range(4):
            sl = slice(512 * tc, 512 * (tc + 1))
            pk = banks[tc]
            tmp = ar.Sg[1].h[:, :].bitcast(BF16)[:, 0:512]
            p.v(lambda E, sl=sl, pk=pk, tmp=tmp: E.tensor_mul(tmp, pk.h[:, :], ar.M[0].h[:, sl]),
                reads=(pk.buf, ar.M[0].buf), writes=(ar.Sg[1].buf,))
            p.v(lambda E, sl=sl, ob=ob, tmp=tmp: E.tensor_add(
                ar.A[24 + ob].h[:, sl], ar.A[24 + ob].h[:, sl], tmp),
                reads=(ar.Sg[1].buf, ar.A[24 + ob].buf), writes=(ar.A[24 + ob].buf,))

    for ob in range(FD):
        slot = ar.W[ob % 2]
        load_wtile(p, ar, slot, wout, woutb, None, 8, ob)
        p.dma("sync", ar.Sg[1].h[:, :], x_src[ob * 128:(ob + 1) * 128, :],
              reads=(xsb,), writes=(ar.Sg[1].buf,))
        banks = [ar.bank() for _ in range(4)]
        for tc in range(4):
            sl = slice(512 * tc, 512 * (tc + 1))
            for c in range(FD):
                p.t(lambda E, c=c, sl=sl, pk=banks[tc], slot=slot: E.matmul(
                    pk.h[:, :], slot.h[:, c * 128:(c + 1) * 128], ar.A[24 + c].h[:, sl],
                    start=(c == 0), stop=(c == FD - 1)),
                    reads=(slot.buf, ar.A[24 + c].buf), writes=(banks[tc].buf,))
        for tc in range(4):
            sl = slice(512 * tc, 512 * (tc + 1))
            pk = banks[tc]
            p.v(lambda E, sl=sl, pk=pk: E.tensor_add(
                ar.Sg[0].h[:, sl], pk.h[:, :], ar.Sg[1].h[:, sl]),
                reads=(pk.buf, ar.Sg[1].buf), writes=(ar.Sg[0].buf,))
        p.dma("sync", x_dst[ob * 128:(ob + 1) * 128, :], ar.Sg[0].h[:, :],
              reads=(ar.Sg[0].buf,), writes=(xdb,))


def decl_attn_l1(nc, wd, l):
    shapes = {
        "L1_attn_wq": ((8, 128, 1024), F32), "L1_attn_wk": ((8, 128, 1024), F32),
        "L1_attn_wv": ((1024, 1024), F32),
        "L1_attn_wgq": ((4, 128, 1024), F32), "L1_attn_wgk": ((4, 128, 1024), F32),
        "L1_attn_wgv": ((1, 2, 128, 2048), F32),
        "L1_attn_r_out_w": ((8, 128, 512), F32), "L1_attn_w_out": ((8, 128, 1024), F32),
        "L1_attn_glw": ((128, 8), F32), "L1_attn_grw": ((128, 8), F32),
        "L1_attn_lmask": ((16, 128, 64), F32),
        "c_tril": ((128, 128), F32),
    }
    for k, (shp, dt) in shapes.items():
        if k not in wd:
            h = nc.dram_tensor(k, list(shp), dt, kind="ExternalInput")
            wd[k] = (h.ap(), Buf(k))


def prep_attn_l1(inputs, l):
    d = {}
    d["L1_attn_wv"] = np.ascontiguousarray(np.asarray(inputs["L1_attn_wv"], np.float32))
    d["L1_attn_wgv"] = _wtiles(inputs["L1_attn_wgv"], 8, 1, owid=512, halves=2)
    d["L1_attn_wq"] = _wtiles(inputs["L1_attn_wq"], 8, 8)
    d["L1_attn_wk"] = _wtiles(inputs["L1_attn_wk"], 8, 8)
    d["L1_attn_wgq"] = _wtiles(inputs["L1_attn_wgq"], 8, 4)
    d["L1_attn_wgk"] = _wtiles(inputs["L1_attn_wgk"], 8, 4)
    d["L1_attn_r_out_w"] = _wtiles(inputs["L1_attn_r_out_w"], 4, 8)
    d["L1_attn_w_out"] = _wtiles(inputs["L1_attn_w_out"], 8, 8)
    def gcols(g):
        g = np.asarray(g, np.float32)
        out = np.zeros((128, 8), np.float32)
        for h in range(16):
            out[64 * (h % 2):64 * (h % 2) + 64, h // 2] = g[h]
        return out
    d["L1_attn_glw"] = gcols(inputs["L1_attn_gate_local_w"])
    d["L1_attn_grw"] = gcols(inputs["L1_attn_gate_remote_w"])
    lm = np.zeros((16, 128, 64), np.float32)
    for qb in range(16):
        for iq in range(128):
            lim = qb * 4 + iq // 32
            lm[qb, iq, lim + 1:] = -1.0e30
    d["L1_attn_lmask"] = lm
    d["c_tril"] = np.tril(np.full((128, 128), -1.0e30, np.float32))
    return d




def emit_attn_l0(p, ar, io, wd, l):
    x_src, xsb, x_dst, xdb = io
    ipT, ipb = wd["L0_attn_in_proj_w"]
    opT, opb = wd["L0_attn_out_proj_w"]
    nrm, nrmb = wd["L0_attn_norm_w"]
    cwc, cwcb = wd["L0_attn_conv_w"]
    cbc, cbcb = wd["L0_attn_conv_b"]
    dcol, dcolb = wd["L0_attn_D"]
    alog, alogb = wd["L0_attn_A_log"]
    dtb, dtbb = wd["L0_attn_dt_bias"]
    g1buf, g1b = wd["g1buf"]
    NH, HD, DS = 32, 64, 128

    emit_layernorm(p, ar, x_src, xsb)

    p.dma("sync", ar.vec[2].h[:, 0:8], nrm[:, :], reads=(nrmb,), writes=(ar.vec[2].buf,))
    p.dma("sync", ar.vec[2].h[:, 8:72], cwc[:, :], reads=(cwcb,), writes=(ar.vec[2].buf,))
    p.dma("sync", ar.vec[2].h[:, 72:88], cbc[:, :], reads=(cbcb,), writes=(ar.vec[2].buf,))
    p.dma("sync", ar.vec[2].h[:, 88:104], dcol[:, :], reads=(dcolb,), writes=(ar.vec[2].buf,))
    p.dma("sync", ar.col[0].h[0:32, 0:1], alog[:, :], reads=(alogb,), writes=(ar.col[0].buf,))
    p.dma("sync", ar.col[0].h[0:32, 1:2], dtb[:, :], reads=(dtbb,), writes=(ar.col[0].buf,))

    emit_rms_stats(p, ar, ar.N, 1e-6, (ar.A[30], ar.A[31]))
    for i in range(FD):
        p.v(lambda E, i=i: E.scalar_tensor_tensor(
            ar.A[i].h[:, :], ar.N[i].h[:, :], ar.vec[2].h[:, i:i + 1], ar.M[0].h[:, :],
            ALU.mult, ALU.mult),
            reads=(ar.N[i].buf, ar.vec[2].buf, ar.M[0].buf), writes=(ar.A[i].buf,))
    rn = ar.A[0:8]

    stg = ar.Sg[1].h[:, :].bitcast(BF16)
    ipL, ipLb = wd["L0_attn_in_proj_last"]
    for ob in range(35):
        owid = 32 if ob == 34 else 128
        slot = ar.W[ob % 2]
        if ob == 34:
            p.dma("gpsimd", slot.h[:, 0:256], ipL[:, :], reads=(ipLb,), writes=(slot.buf,))
        else:
            load_wtile(p, ar, slot, ipT, ipb, None, 8, ob)
        banks = [ar.bank() for _ in range(4)]
        for tc in range(4):
            sl = slice(512 * tc, 512 * (tc + 1))
            for c in range(FD):
                p.t(lambda E, c=c, sl=sl, pk=banks[tc], slot=slot, owid=owid: E.matmul(
                    pk.h[0:owid, :], slot.h[:, c * owid:(c + 1) * owid], rn[c].h[:, sl],
                    start=(c == 0), stop=(c == FD - 1)),
                    reads=(slot.buf, rn[c].buf), writes=(banks[tc].buf,))
        if ob < 16:
            for tc in range(4):
                sl = slice(512 * tc, 512 * (tc + 1))
                ssl = slice(1024 * (tc % 2), 1024 * (tc % 2) + 512)
                p.s(lambda E, pk=banks[tc], ssl=ssl: E.activation(
                    stg[:, ssl], pk.h[:, :], AF.Copy),
                    reads=(banks[tc].buf,), writes=(ar.Sg[1].buf,))
                p.dma("sync", g1buf[ob * 128:(ob + 1) * 128, sl], stg[:, ssl],
                      reads=(ar.Sg[1].buf,), writes=(g1b,))
        else:
            if ob < 32:
                dst, drows = ar.A[8 + ob - 16], slice(0, 128)
            elif ob == 32:
                dst, drows = ar.A[24], slice(0, 128)
            elif ob == 33:
                dst, drows = ar.A[25], slice(0, 128)
            else:
                dst, drows = ar.A[26], slice(0, 32)
            for tc in range(4):
                sl = slice(512 * tc, 512 * (tc + 1))
                p.s(lambda E, pk=banks[tc], dst=dst, sl=sl, drows=drows, owid=owid: E.activation(
                    dst.h[drows, sl], pk.h[0:owid, :], AF.Copy),
                    reads=(banks[tc].buf,), writes=(dst.buf,))

    for i in range(16):
        xt = ar.A[8 + i]
        acc = ar.Sg[0].h[:, :]
        p.v(lambda E, xt=xt, i=i: E.tensor_scalar(
            acc, xt.h[:, :], ar.vec[2].h[:, 8 + 3 * 16 + i:9 + 3 * 16 + i],
            ar.vec[2].h[:, 72 + i:73 + i], ALU.mult, ALU.add),
            reads=(xt.buf, ar.vec[2].buf), writes=(ar.Sg[0].buf,))
        for k in range(3):
            d = 3 - k
            p.v(lambda E, xt=xt, i=i, k=k, d=d: E.scalar_tensor_tensor(
                acc[:, d:2048], xt.h[:, 0:2048 - d],
                ar.vec[2].h[:, 8 + k * 16 + i:9 + k * 16 + i], acc[:, d:2048],
                ALU.mult, ALU.add),
                reads=(xt.buf, ar.vec[2].buf, ar.Sg[0].buf), writes=(ar.Sg[0].buf,))
        p.s(lambda E, xt=xt: E.activation(xt.h[:, :], acc, AF.Silu),
            reads=(ar.Sg[0].buf,), writes=(xt.buf,))

    dtraw = ar.A[26]
    b0 = ar.Sg[1].h[0:32, :]
    b1 = ar.Sg[1].h[32:64, :]
    lam = ar.Sg[1].h[64:96, :]
    p.v(lambda E: E.tensor_scalar(b0, dtraw.h[0:32, :], ar.col[0].h[0:32, 1:2], 80.0,
                                  ALU.add, ALU.min),
        reads=(dtraw.buf, ar.col[0].buf), writes=(ar.Sg[1].buf,))
    p.s(lambda E: E.activation(b1, b0, AF.Exp),
        reads=(ar.Sg[1].buf,), writes=(ar.Sg[1].buf,))
    p.s(lambda E: E.activation(b0, b1, AF.Ln, bias=1.0),
        reads=(ar.Sg[1].buf,), writes=(ar.Sg[1].buf,))
    p.s(lambda E: E.activation(ar.col[0].h[0:32, 2:3], ar.col[0].h[0:32, 0:1], AF.Exp),
        reads=(ar.col[0].buf,), writes=(ar.col[0].buf,))
    p.v(lambda E: E.tensor_scalar_mul(ar.col[0].h[0:32, 3:4], ar.col[0].h[0:32, 2:3], -1.0),
        reads=(ar.col[0].buf,), writes=(ar.col[0].buf,))
    p.v(lambda E: E.tensor_scalar(lam, b0, ar.col[0].h[0:32, 3:4], None, ALU.mult),
        reads=(ar.Sg[1].buf, ar.col[0].buf), writes=(ar.Sg[1].buf,))
    F27 = ar.A[27].h[:, :].bitcast(F32)
    p.v(lambda E: E.memset(F27[64:96, 0:128], 1.0), writes=(ar.A[27].buf,))

    H = ar.A[28]
    p.v(lambda E: E.memset(H.h[:, :], 0.0), writes=(H.buf,))
    scr = ar.A[29]
    GT = ar.Sg[0].h[:, 0:128]
    DEC = ar.Sg[0].h[:, 128:256]
    RB = ar.Sg[0].h[:, 256:384]
    for c in range(16):
        csl = slice(128 * c, 128 * (c + 1))
        lam_c = ar.Sg[1].h[64:96, 128 * c:128 * (c + 1)]
        LAM = F27[64:96, 128:256]
        AEXP = F27[64:96, 256:384]
        BDEC = F27[64:96, 384:512]
        p.v(lambda E, lam_c=lam_c, LAM=LAM: E.tensor_tensor_scan(
            LAM, F27[64:96, 0:128], lam_c, 0.0, ALU.mult, ALU.add),
            reads=(ar.A[27].buf, ar.Sg[1].buf), writes=(ar.A[27].buf,))
        p.s(lambda E, LAM=LAM, AEXP=AEXP: E.activation(AEXP, LAM, AF.Exp),
            reads=(ar.A[27].buf,), writes=(ar.A[27].buf,))
        p.v(lambda E, LAM=LAM, BDEC=BDEC: E.tensor_scalar(
            BDEC, LAM, F27[64:96, 255:256], None, ALU.subtract),
            reads=(ar.A[27].buf,), writes=(ar.A[27].buf,))
        p.s(lambda E, BDEC=BDEC: E.activation(BDEC, BDEC, AF.Exp, scale=-1.0),
            reads=(ar.A[27].buf,), writes=(ar.A[27].buf,))
        LAMT = F27[:, 512:544]
        AEXPT = F27[:, 544:576]
        BDECT = F27[:, 576:608]
        DTT = F27[:, 608:640]
        DTOT = F27[:, 640:672]
        for (srcap, dstap) in ((LAM, LAMT), (AEXP, AEXPT), (BDEC, BDECT)):
            p.t(lambda E, srcap=srcap: E.transpose(
                ar.P[4].h[:, 0:32], srcap, ar.identf.h[64:96, 64:96]),
                reads=(ar.A[27].buf, ar.identf.buf), writes=(ar.P[4].buf,))
            p.s(lambda E, dstap=dstap: E.activation(dstap, ar.P[4].h[:, 0:32], AF.Copy),
                reads=(ar.P[4].buf,), writes=(ar.A[27].buf,))
        p.t(lambda E, csl=csl: E.transpose(
            ar.P[4].h[:, 0:32], ar.Sg[1].h[0:32, csl], ar.identf.h[0:32, 0:32]),
            reads=(ar.Sg[1].buf, ar.identf.buf), writes=(ar.P[4].buf,))
        p.s(lambda E: E.activation(DTT, ar.P[4].h[:, 0:32], AF.Copy),
            reads=(ar.P[4].buf,), writes=(ar.A[27].buf,))
        p.t(lambda E: E.matmul(ar.P[4].h[:, 0:32], ar.onesf.h[:, :],
                               AEXPT, start=True, stop=True),
            reads=(ar.onesf.buf, ar.A[27].buf), writes=(ar.P[4].buf,))
        p.s(lambda E: E.activation(DTOT, ar.P[4].h[:, 0:32], AF.Copy),
            reads=(ar.P[4].buf,), writes=(ar.A[27].buf,))
        p.t(lambda E, csl=csl: E.matmul(
            ar.P[5].h[:, 0:128], ar.A[24].h[:, csl], ar.A[25].h[:, csl],
            start=True, stop=True),
            reads=(ar.A[24].buf, ar.A[25].buf), writes=(ar.P[5].buf,))
        p.s(lambda E: E.activation(GT, ar.P[5].h[:, 0:128], AF.Copy),
            reads=(ar.P[5].buf,), writes=(ar.Sg[0].buf,))
        p.t(lambda E, csl=csl: E.transpose(
            ar.Tp[1].h[:, 0:128], ar.A[24].h[:, csl], ar.ident.h[:, :]),
            reads=(ar.A[24].buf, ar.ident.buf), writes=(ar.Tp[1].buf,))
        p.s(lambda E: E.activation(scr.h[:, 256:384], ar.Tp[1].h[:, 0:128], AF.Copy),
            reads=(ar.Tp[1].buf,), writes=(scr.buf,))
        def m2_iter(h, stage):
            pi = h % 2
            xt = ar.A[8 + h // 2]
            xrow = 64 * (h % 2)
            if pi == 0:
                RBp = ar.Sg[0].h[:, 256:384]
                DECp = ar.Sg[0].h[:, 128:256]
                rb_buf = dec_buf = ar.Sg[0].buf
                Mp = scr.h[:, 0:128]
                dtxp = scr.h[:, 128:192]
                btp = scr.h[:, 384:512]
                m_buf = ar.A[29].buf
                ycp = ar.Sg[0].h[:, 384:448]
                yc_buf = ar.Sg[0].buf
            else:
                A0 = ar.A[0]
                A1f = ar.A[1].h[:, :].bitcast(F32)
                RBp = A1f[:, 0:128]
                DECp = A1f[:, 128:256]
                rb_buf = dec_buf = ar.A[1].buf
                Mp = A0.h[:, 0:128]
                dtxp = A0.h[:, 128:192]
                btp = A0.h[:, 384:512]
                m_buf = ar.A[0].buf
                ycp = A1f[:, 256:320]
                yc_buf = ar.A[1].buf
            pR = ar.P[0 + pi]
            pYi = ar.P[2 + pi]
            pYb = ar.P[4 + pi]
            if stage == 0:
                p.v(lambda E, h=h, RBp=RBp: E.tensor_copy(
                    RBp, LAMT[:, h:h + 1].broadcast_to((128, 128))),
                    reads=(ar.A[27].buf,), writes=(rb_buf,))
                p.t(lambda E, RBp=RBp, pR=pR: E.transpose(
                    pR.h[:, 0:128], RBp, ar.identf.h[:, :]),
                    reads=(rb_buf, ar.identf.buf), writes=(pR.buf,))
                p.v(lambda E, h=h, DECp=DECp, pR=pR: E.tensor_scalar(
                    DECp, pR.h[:, 0:128], LAMT[:, h:h + 1], None, ALU.subtract),
                    reads=(pR.buf, ar.A[27].buf), writes=(dec_buf,))
                p.v(lambda E, DECp=DECp: E.tensor_add(DECp, DECp, ar.trils.h[:, :]),
                    reads=(dec_buf, ar.trils.buf), writes=(dec_buf,))
                p.s(lambda E, DECp=DECp: E.activation(DECp, DECp, AF.Exp),
                    reads=(dec_buf,), writes=(dec_buf,))
                p.v(lambda E, DECp=DECp, Mp=Mp: E.tensor_mul(Mp, DECp, GT),
                    reads=(dec_buf, ar.Sg[0].buf), writes=(m_buf,))
                p.t(lambda E, xt=xt, xrow=xrow, csl=csl, pi=pi: E.transpose(
                    ar.Tp[pi].h[:, 0:64], xt.h[xrow:xrow + 64, csl],
                    ar.ident.h[xrow:xrow + 64, xrow:xrow + 64]),
                    reads=(xt.buf, ar.ident.buf), writes=(ar.Tp[pi].buf,))
                p.v(lambda E, h=h, dtxp=dtxp, pi=pi: E.tensor_scalar(
                    dtxp, ar.Tp[pi].h[:, 0:64], DTT[:, h:h + 1], None, ALU.mult),
                    reads=(ar.Tp[pi].buf, ar.A[27].buf), writes=(m_buf,))
                p.v(lambda E, h=h, btp=btp: E.tensor_scalar(
                    btp, scr.h[:, 256:384], BDECT[:, h:h + 1], None, ALU.mult),
                    reads=(scr.buf, ar.A[27].buf), writes=(m_buf,))
            else:
                p.t(lambda E, pYi=pYi, Mp=Mp, dtxp=dtxp: E.matmul(
                    pYi.h[:, 0:64], Mp, dtxp, start=True, stop=True),
                    reads=(m_buf,), writes=(pYi.buf,))
                p.t(lambda E, pYb=pYb, csl=csl, h=h: E.matmul(
                    pYb.h[:, 0:64], ar.A[25].h[:, csl], H.h[:, h * 64:(h + 1) * 64],
                    start=True, stop=True),
                    reads=(ar.A[25].buf, H.buf), writes=(pYb.buf,))
                p.t(lambda E, pR=pR, btp=btp, dtxp=dtxp: E.matmul(
                    pR.h[:, 0:64], btp, dtxp, start=True, stop=True),
                    reads=(m_buf,), writes=(pR.buf,))
                p.v(lambda E, h=h, pR=pR: E.scalar_tensor_tensor(
                    H.h[:, h * 64:(h + 1) * 64], H.h[:, h * 64:(h + 1) * 64],
                    DTOT[:, h:h + 1], pR.h[:, 0:64], ALU.mult, ALU.add),
                    reads=(H.buf, ar.A[27].buf, pR.buf), writes=(H.buf,))
                p.v(lambda E, h=h, pYb=pYb, ycp=ycp: E.tensor_scalar(
                    ycp, pYb.h[:, 0:64], AEXPT[:, h:h + 1], None, ALU.mult),
                    reads=(pYb.buf, ar.A[27].buf), writes=(yc_buf,))
                p.v(lambda E, pYi=pYi, ycp=ycp: E.tensor_add(
                    ycp, ycp, pYi.h[:, 0:64]),
                    reads=(yc_buf, pYi.buf), writes=(yc_buf,))
                p.t(lambda E, ycp=ycp, pR=pR: E.transpose(
                    pR.h[0:64, 0:128], ycp, ar.identf.h[:, :]),
                    reads=(yc_buf, ar.identf.buf), writes=(pR.buf,))
                p.v(lambda E, xt=xt, xrow=xrow, csl=csl, h=h, pR=pR: E.scalar_tensor_tensor(
                    xt.h[xrow:xrow + 64, csl], xt.h[xrow:xrow + 64, csl],
                    ar.vec[2].h[xrow:xrow + 64, 88 + (h // 2):89 + (h // 2)],
                    pR.h[0:64, 0:128], ALU.mult, ALU.add),
                    reads=(xt.buf, ar.vec[2].buf, pR.buf), writes=(xt.buf,))

        for h in range(NH):
            m2_iter(h, 0)
            if h:
                m2_iter(h - 1, 1)
        m2_iter(NH - 1, 1)

    for i in range(16):
        zt = ar.M[0]
        p.dma("sync", zt.h[:, :], g1buf[i * 128:(i + 1) * 128, :], reads=(g1b,),
              writes=(zt.buf,))
        p.s(lambda E, zt=zt: E.activation(ar.M[1].h[:, :], zt.h[:, :], AF.Silu),
            reads=(zt.buf,), writes=(ar.M[1].buf,))
        p.v(lambda E, i=i: E.tensor_mul(ar.A[8 + i].h[:, :], ar.A[8 + i].h[:, :],
                                        ar.M[1].h[:, :]),
            reads=(ar.A[8 + i].buf, ar.M[1].buf), writes=(ar.A[8 + i].buf,))

    for ob in range(FD):
        slot = ar.W[ob % 2]
        load_wtile(p, ar, slot, opT, opb, None, 16, ob)
        p.dma("sync", ar.Sg[1].h[:, :], x_src[ob * 128:(ob + 1) * 128, :],
              reads=(xsb,), writes=(ar.Sg[1].buf,))
        banks = [ar.bank() for _ in range(4)]
        for tc in range(4):
            sl = slice(512 * tc, 512 * (tc + 1))
            for cch in range(16):
                p.t(lambda E, cch=cch, sl=sl, pk=banks[tc], slot=slot: E.matmul(
                    pk.h[:, :], slot.h[:, cch * 128:(cch + 1) * 128], ar.A[8 + cch].h[:, sl],
                    start=(cch == 0), stop=(cch == 15)),
                    reads=(slot.buf, ar.A[8 + cch].buf), writes=(banks[tc].buf,))
        for tc in range(4):
            sl = slice(512 * tc, 512 * (tc + 1))
            pk = banks[tc]
            p.v(lambda E, sl=sl, pk=pk, ob=ob: E.tensor_add(
                ar.Sg[0].h[:, sl], pk.h[:, :], ar.N[ob].h[:, sl]),
                reads=(pk.buf, ar.N[ob].buf), writes=(ar.Sg[0].buf,))
            p.v(lambda E, sl=sl: E.tensor_add(
                ar.Sg[0].h[:, sl], ar.Sg[0].h[:, sl], ar.Sg[1].h[:, sl]),
                reads=(ar.Sg[0].buf, ar.Sg[1].buf), writes=(ar.Sg[0].buf,))
        p.dma("sync", x_dst[ob * 128:(ob + 1) * 128, :], ar.Sg[0].h[:, :],
              reads=(ar.Sg[0].buf,), writes=(xdb,))


def decl_attn_l0(nc, wd, l):
    shapes = {
        "L0_attn_in_proj_w": (34, 128, 1024), "L0_attn_in_proj_last": (128, 256),
        "L0_attn_out_proj_w": (8, 128, 2048),
        "L0_attn_norm_w": (128, 8), "L0_attn_conv_w": (128, 64),
        "L0_attn_conv_b": (128, 16), "L0_attn_D": (128, 16),
        "L0_attn_A_log": (32, 1), "L0_attn_dt_bias": (32, 1),
    }
    for k, shp in shapes.items():
        if k not in wd:
            h = nc.dram_tensor(k, list(shp), F32, kind="ExternalInput")
            wd[k] = (h.ap(), Buf(k))


def prep_attn_l0(inputs, l):
    d = {}
    ipT = _t(inputs["L0_attn_in_proj_w"])
    d["L0_attn_in_proj_w"] = _wtiles(ipT[:, 0:4352], 8, 34)
    d["L0_attn_in_proj_last"] = _wtiles(ipT[:, 4352:4384], 8, 1, owid=32)[0]
    d["L0_attn_out_proj_w"] = _wtiles(_t(inputs["L0_attn_out_proj_w"]), 16, 8)
    d["L0_attn_norm_w"] = _cols(inputs["L0_attn_norm_w"])
    cw = np.asarray(inputs["L0_attn_conv_w"], np.float32)
    d["L0_attn_conv_w"] = np.concatenate(
        [_cols(cw[:, k]) for k in range(4)], axis=1)
    d["L0_attn_conv_b"] = _cols(inputs["L0_attn_conv_b"])
    d["L0_attn_D"] = _cols(np.repeat(np.asarray(inputs["L0_attn_D"], np.float32), 64))
    d["L0_attn_A_log"] = np.asarray(inputs["L0_attn_A_log"], np.float32).reshape(32, 1)
    d["L0_attn_dt_bias"] = np.asarray(inputs["L0_attn_dt_bias"], np.float32).reshape(32, 1)
    return d




def emit_attn_l2(p, ar, io, wd, l):
    x_src, xsb, x_dst, xdb = io
    ipT, ipb = wd["L2_attn_in_proj_w"]
    opT, opb = wd["L2_attn_out_proj_w"]
    nrm, nrmb = wd["L2_attn_norm_w"]
    g1buf, g1b = wd["g1buf"]
    bcb, bcbb = wd["bcbuf"]
    DI, DS, TC, TN = 2048, 16, 512, 128

    emit_layernorm(p, ar, x_src, xsb)

    p.dma("sync", ar.vec[2].h[:, 0:8], nrm[:, :], reads=(nrmb,), writes=(ar.vec[2].buf,))
    emit_rms_stats(p, ar, ar.N, 1e-5, (ar.A[30], ar.A[31]))
    for i in range(FD):
        p.v(lambda E, i=i: E.scalar_tensor_tensor(
            ar.A[i].h[:, :], ar.N[i].h[:, :], ar.vec[2].h[:, i:i + 1], ar.M[0].h[:, :],
            ALU.mult, ALU.mult),
            reads=(ar.N[i].buf, ar.vec[2].buf, ar.M[0].buf), writes=(ar.A[i].buf,))
    rn = ar.A[0:8]

    stg = ar.Sg[1].h[:, :].bitcast(BF16)
    for ob in range(32):
        slot = ar.W[ob % 2]
        load_wtile(p, ar, slot, ipT, ipb, None, 8, ob)
        banks = [ar.bank() for _ in range(4)]
        for tc in range(4):
            sl = slice(512 * tc, 512 * (tc + 1))
            for c in range(FD):
                p.t(lambda E, c=c, sl=sl, pk=banks[tc], slot=slot: E.matmul(
                    pk.h[:, :], slot.h[:, c * 128:(c + 1) * 128], rn[c].h[:, sl],
                    start=(c == 0), stop=(c == FD - 1)),
                    reads=(slot.buf, rn[c].buf), writes=(banks[tc].buf,))
        for tc in range(4):
            sl = slice(512 * tc, 512 * (tc + 1))
            ssl = slice(1024 * (tc % 2), 1024 * (tc % 2) + 512)
            p.s(lambda E, pk=banks[tc], ssl=ssl: E.activation(
                stg[:, ssl], pk.h[:, :], AF.Copy),
                reads=(banks[tc].buf,), writes=(ar.Sg[1].buf,))
            if ob < 16:
                p.dma("sync", g1buf[2048 + ob * 128:2048 + (ob + 1) * 128, sl], stg[:, ssl],
                      reads=(ar.Sg[1].buf,), writes=(g1b,))
            else:
                p.dma("sync", g1buf[(ob - 16) * 128:(ob - 15) * 128, sl], stg[:, ssl],
                      reads=(ar.Sg[1].buf,), writes=(g1b,))

    import os as _os
    _dirs = _os.environ.get("L2_DIRS", "fb")
    for direction in [d for d in "fb" if d in _dirs]:
        suf = "_" + direction
        cwc, cwcb = wd["L2_attn_conv" + suf]
        cbc, cbcb = wd["L2_attn_convb" + suf]
        xpT, xpTb = wd["L2_attn_xproj" + suf]
        dpT, dpTb = wd["L2_attn_dtproj" + suf]
        dpb, dpbb = wd["L2_attn_dtprojb" + suf]
        alg, algb = wd["L2_attn_alog" + suf]
        dcl, dclb = wd["L2_attn_D" + suf]

        p.dma("sync", ar.vec[2].h[:, 16:80], cwc[:, :], reads=(cwcb,), writes=(ar.vec[2].buf,))
        p.dma("sync", ar.vec[2].h[:, 80:96], cbc[:, :], reads=(cbcb,), writes=(ar.vec[2].buf,))
        p.dma("sync", ar.vec[2].h[:, 96:112], dpb[:, :], reads=(dpbb,), writes=(ar.vec[2].buf,))
        p.dma("sync", ar.vec[2].h[:, 112:128], dcl[:, :], reads=(dclb,), writes=(ar.vec[2].buf,))
        p.dma("sync", ar.vec[1].h[:, 0:256], alg[:, :], reads=(algb,), writes=(ar.vec[1].buf,))
        p.s(lambda E: E.activation(ar.vec[1].h[:, 0:256], ar.vec[1].h[:, 0:256], AF.Exp),
            reads=(ar.vec[1].buf,), writes=(ar.vec[1].buf,))
        p.v(lambda E: E.tensor_scalar_mul(ar.vec[1].h[:, 0:256], ar.vec[1].h[:, 0:256], -1.0),
            reads=(ar.vec[1].buf,), writes=(ar.vec[1].buf,))
        p.dma("gpsimd", ar.W[0].h[:, 0:1536], xpT[0, :, :], reads=(xpTb,), writes=(ar.W[0].buf,))
        p.dma("gpsimd", ar.M[1].h[0:64, :], dpT[:, :], reads=(dpTb,), writes=(ar.M[1].buf,))
        p.v(lambda E: E.memset(ar.vec[3].h[:, :], 0.0), writes=(ar.vec[3].buf,))

        F30 = ar.A[30].h[:, :].bitcast(F32)
        F28 = ar.A[28].h[:, :].bitcast(F32)
        for q in range(4):
            qsl = slice(512 * q, 512 * (q + 1))
            for g in range(16):
                stage2 = ar.A[31].h[:, 0:515]
                acc = ar.Sg[0].h[:, 0:512]
                if direction == "f":
                    lo = 512 * q - 3
                    zpad = max(0, -lo)
                    p.dma("sync", ar.A[31].h[:, zpad:515],
                          g1buf[2048 + g * 128:2048 + (g + 1) * 128, lo + zpad:512 * q + 512],
                          reads=(g1b,), writes=(ar.A[31].buf,))
                    if zpad:
                        p.v(lambda E, zpad=zpad: E.memset(ar.A[31].h[:, 0:zpad], 0.0),
                            writes=(ar.A[31].buf,))
                else:
                    TH = 2050 - 512 * q
                    TL = TH - 514
                    lo = max(0, TL)
                    hi = min(2047, TH)
                    n = hi - lo + 1
                    p.dma("sync", ar.A[31].h[:, 515:515 + n],
                          g1buf[2048 + g * 128:2048 + (g + 1) * 128, lo:hi + 1],
                          reads=(g1b,), writes=(ar.A[31].buf,))
                    jlo = TH - hi
                    jhi = TH - lo
                    if jlo > 0:
                        p.v(lambda E, jlo=jlo: E.memset(ar.A[31].h[:, 0:jlo], 0.0),
                            writes=(ar.A[31].buf,))
                    if jhi < 514:
                        p.v(lambda E, jhi=jhi: E.memset(ar.A[31].h[:, jhi + 1:515], 0.0),
                            writes=(ar.A[31].buf,))
                    p.v(lambda E, jlo=jlo, jhi=jhi, n=n: E.tensor_copy(
                        ar.A[31].h[:, jlo:jhi + 1], ar.A[31].h[:, 515 + n - 1:514:-1]),
                        reads=(ar.A[31].buf,), writes=(ar.A[31].buf,))
                p.v(lambda E, g=g: E.tensor_scalar(
                    acc, stage2[:, 0:512], ar.vec[2].h[:, 16 + 0 * 16 + g:17 + 0 * 16 + g],
                    ar.vec[2].h[:, 80 + g:81 + g], ALU.mult, ALU.add),
                    reads=(ar.A[31].buf, ar.vec[2].buf), writes=(ar.Sg[0].buf,))
                for k in range(1, 4):
                    p.v(lambda E, g=g, k=k: E.scalar_tensor_tensor(
                        acc, stage2[:, k:k + 512], ar.vec[2].h[:, 16 + k * 16 + g:17 + k * 16 + g],
                        acc, ALU.mult, ALU.add),
                        reads=(ar.A[31].buf, ar.vec[2].buf, ar.Sg[0].buf), writes=(ar.Sg[0].buf,))
                xcq = ar.A[24 + g // 4]
                xoff = (g % 4) * 512
                p.s(lambda E, xcq=xcq, xoff=xoff: E.activation(
                    xcq.h[:, xoff:xoff + 512], acc, AF.Silu),
                    reads=(ar.Sg[0].buf,), writes=(xcq.buf,))
            pk = ar.P[5]
            for g in range(16):
                xcq = ar.A[24 + g // 4]
                xoff = (g % 4) * 512
                p.t(lambda E, g=g, pk=pk, xcq=xcq, xoff=xoff: E.matmul(
                    pk.h[0:96, :], ar.W[0].h[:, g * 96:(g + 1) * 96],
                    xcq.h[:, xoff:xoff + 512],
                    start=(g == 0), stop=(g == 15)),
                    reads=(ar.W[0].buf, xcq.buf), writes=(pk.buf,))
            p.s(lambda E, pk=pk: E.activation(ar.Sg[1].h[0:96, 0:512], pk.h[0:96, :], AF.Copy),
                reads=(pk.buf,), writes=(ar.Sg[1].buf,))
            p.s(lambda E, pk=pk: E.activation(ar.M[0].h[0:64, 0:512], pk.h[0:64, :], AF.Copy),
                reads=(pk.buf,), writes=(ar.M[0].buf,))
            p.dma("gpsimd", bcb[:, :, :].rearrange("ts n t -> n ts t"),
                  ar.Sg[1].h[64:96, 0:512].rearrange("n (ts t) -> n ts t", t=128),
                  reads=(ar.Sg[1].buf,), writes=(bcbb,))
            M0f = ar.M[0].h[:, :].bitcast(F32)
            ABUF = (ar.Sg[0], ar.Sg[1])
            HBUF = (ar.A[29], ar.A[30])

            def l2_delta(g):
                xcq = ar.A[24 + g // 4]
                xoff = (g % 4) * 512
                dt_dst = ar.A[0 + g // 4]
                dx_dst = ar.A[4 + g // 4]
                doff = (g % 4) * 512
                F2 = ar.Sg[0].h[:, :]
                pk2 = ar.P[4 + g % 2]
                p.t(lambda E, g=g, pk2=pk2: E.matmul(
                    pk2.h[:, :], ar.M[1].h[0:64, g * 128:(g + 1) * 128],
                    ar.M[0].h[0:64, 0:512], start=True, stop=True),
                    reads=(ar.M[1].buf, ar.M[0].buf), writes=(pk2.buf,))
                sl2 = slice(0, 512) if g % 2 == 0 else slice(512, 1024)
                p.v(lambda E, g=g, pk2=pk2, F2=F2, sl2=sl2: E.tensor_scalar(
                    F2[:, sl2], pk2.h[:, :], ar.vec[2].h[:, 96 + g:97 + g], 80.0,
                    ALU.add, ALU.min),
                    reads=(pk2.buf, ar.vec[2].buf), writes=(ar.Sg[0].buf,))
                p.s(lambda E, F2=F2, sl2=sl2: E.activation(F2[:, sl2], F2[:, sl2], AF.Exp),
                    reads=(ar.Sg[0].buf,), writes=(ar.Sg[0].buf,))
                p.s(lambda E, F2=F2, sl2=sl2: E.activation(F2[:, sl2], F2[:, sl2], AF.Ln, bias=1.0),
                    reads=(ar.Sg[0].buf,), writes=(ar.Sg[0].buf,))
                p.s(lambda E, F2=F2, sl2=sl2, dt_dst=dt_dst, doff=doff: E.activation(
                    dt_dst.h[:, doff:doff + 512], F2[:, sl2], AF.Copy),
                    reads=(ar.Sg[0].buf,), writes=(dt_dst.buf,))
                p.v(lambda E, xcq=xcq, xoff=xoff, dt_dst=dt_dst, dx_dst=dx_dst, doff=doff: E.tensor_mul(
                    dx_dst.h[:, doff:doff + 512], dt_dst.h[:, doff:doff + 512],
                    xcq.h[:, xoff:xoff + 512]),
                    reads=(dt_dst.buf, xcq.buf), writes=(dx_dst.buf,))

            for g in range(16):
                l2_delta(g)

            def l2_s0(g, ts):
                pi = g % 2
                ab = ABUF[pi]
                hb = HBUF[pi]
                ABT = ab.h[:, :].bitcast(BF16)
                dt_src = ar.A[0 + g // 4]
                dx_src = ar.A[4 + g // 4]
                doff = (g % 4) * 512 + 128 * ts
                p.emit("gpsimd", lambda E, hb=hb, dx_src=dx_src, doff=doff: E.tensor_tensor(
                    hb.h[:, 0:2048].rearrange("p (n t) -> p n t", t=128),
                    dx_src.h[:, doff:doff + 128]
                    .unsqueeze(1).broadcast_to((128, 16, 128)),
                    ar.A[28].h[:, 0:2048].rearrange("p (n t) -> p n t", t=128), ALU.mult),
                    reads=(dx_src.buf, ar.A[28].buf), writes=(hb.buf,))
                p.group("scalar",
                        [(lambda E, ab=ab, n=n, dt_src=dt_src, doff=doff, g=g: E.activation(
                            ab.h[:, n * 128:(n + 1) * 128], dt_src.h[:, doff:doff + 128],
                            AF.Exp, scale=ar.vec[1].h[:, g * 16 + n:g * 16 + n + 1]))
                         for n in range(16)],
                        reads=(dt_src.buf, ar.vec[1].buf), writes=(ab.buf,))

            def l2_s1(g, ts):
                pi = g % 2
                ab = ABUF[pi]
                hb = HBUF[pi]
                ABT = ab.h[:, :].bitcast(BF16)
                xcq = ar.A[24 + g // 4]
                xoff = (g % 4) * 512 + 128 * ts
                hcol = ar.vec[3].h[:, g * 16:(g + 1) * 16]
                p.v(lambda E, hcol=hcol, ab=ab: E.tensor_mul(
                    ar.col[3].h[:, 0:16], ab.h[:, 0:2048:128], hcol),
                    reads=(ab.buf, ar.vec[3].buf), writes=(ar.col[3].buf,))
                p.v(lambda E, hb=hb: E.tensor_add(
                    hb.h[:, 0:2048:128], hb.h[:, 0:2048:128], ar.col[3].h[:, 0:16]),
                    reads=(hb.buf, ar.col[3].buf), writes=(hb.buf,))
                p.v(lambda E, ab=ab: E.memset(ab.h[:, 0:2048:128], 0.0),
                    writes=(ab.buf,))
                p.v(lambda E, ab=ab, hb=hb: E.tensor_tensor_scan(
                    hb.h[:, 0:2048], ab.h[:, 0:2048], hb.h[:, 0:2048], 0.0,
                    ALU.mult, ALU.add),
                    reads=(ab.buf, hb.buf), writes=(hb.buf,))
                p.v(lambda E, hcol=hcol, hb=hb: E.tensor_copy(hcol, hb.h[:, 127:2048:128]),
                    reads=(hb.buf,), writes=(ar.vec[3].buf,))
                p.v(lambda E, hb=hb, ABT=ABT: E.tensor_mul(
                    ABT[:, 0:2048], hb.h[:, 0:2048], ar.A[31].h[:, 0:2048]),
                    reads=(hb.buf, ar.A[31].buf), writes=(ab.buf,))
                p.v(lambda E, ABT=ABT: E.tensor_add(ABT[:, 0:1024], ABT[:, 0:1024],
                                                    ABT[:, 1024:2048]),
                    reads=(ab.buf,), writes=(ab.buf,))
                p.v(lambda E, ABT=ABT: E.tensor_add(ABT[:, 0:512], ABT[:, 0:512],
                                                    ABT[:, 512:1024]),
                    reads=(ab.buf,), writes=(ab.buf,))
                p.v(lambda E, ABT=ABT: E.tensor_add(ABT[:, 0:256], ABT[:, 0:256],
                                                    ABT[:, 256:512]),
                    reads=(ab.buf,), writes=(ab.buf,))
                yo = M0f[:, 256 + 128 * pi:384 + 128 * pi]
                p.v(lambda E, ABT=ABT, yo=yo: E.tensor_add(
                    yo, ABT[:, 0:128], ABT[:, 128:256]),
                    reads=(ab.buf,), writes=(ar.M[0].buf,))
                if direction == "f":
                    t0 = 512 * q + 128 * ts
                    p.v(lambda E, g=g, t0=t0, xcq=xcq, xoff=xoff, yo=yo: E.scalar_tensor_tensor(
                        ar.A[8 + g].h[:, t0:t0 + 128], xcq.h[:, xoff:xoff + 128],
                        ar.vec[2].h[:, 112 + g:113 + g], yo, ALU.mult, ALU.add),
                        reads=(xcq.buf, ar.vec[2].buf, ar.M[0].buf), writes=(ar.A[8 + g].buf,))
                else:
                    t0 = 2048 - 512 * q - 128 * (ts + 1)
                    p.v(lambda E, g=g, xcq=xcq, xoff=xoff, yo=yo: E.scalar_tensor_tensor(
                        yo, xcq.h[:, xoff:xoff + 128],
                        ar.vec[2].h[:, 112 + g:113 + g], yo, ALU.mult, ALU.add),
                        reads=(xcq.buf, ar.vec[2].buf, ar.M[0].buf), writes=(ar.M[0].buf,))
                    revout = M0f[:, 512 + 128 * pi:640 + 128 * pi]
                    p.v(lambda E, pi=pi, revout=revout: E.tensor_copy(
                        revout, M0f[:, 383 + 128 * pi:255 + 128 * pi:-1]),
                        reads=(ar.M[0].buf,), writes=(ar.M[0].buf,))
                    p.v(lambda E, g=g, t0=t0: E.tensor_scalar_mul(
                        ar.A[8 + g].h[:, t0:t0 + 128], ar.A[8 + g].h[:, t0:t0 + 128], 0.5),
                        reads=(ar.A[8 + g].buf,), writes=(ar.A[8 + g].buf,))
                    p.v(lambda E, g=g, t0=t0, revout=revout: E.scalar_tensor_tensor(
                        ar.A[8 + g].h[:, t0:t0 + 128], revout,
                        0.5, ar.A[8 + g].h[:, t0:t0 + 128], ALU.mult, ALU.add),
                        reads=(ar.M[0].buf, ar.A[8 + g].buf), writes=(ar.A[8 + g].buf,))

            for ts in range(4):
                p.dma("gpsimd", ar.A[28].h[:, 0:2048],
                      bcb[ts, 0:16, :].rearrange("n t -> (n t)").unsqueeze(0)
                      .partition_broadcast(128),
                      reads=(bcbb,), writes=(ar.A[28].buf,))
                p.dma("gpsimd", ar.A[31].h[:, 0:2048],
                      bcb[ts, 16:32, :].rearrange("n t -> (n t)").unsqueeze(0)
                      .partition_broadcast(128),
                      reads=(bcbb,), writes=(ar.A[31].buf,))
                for g in range(16):
                    l2_s0(g, ts)
                    if g:
                        l2_s1(g - 1, ts)
                l2_s1(15, ts)
    for i in range(16):
        zt = ar.M[i % 2]
        st = ar.W[i % 2]
        p.dma("sync", zt.h[:, :], g1buf[i * 128:(i + 1) * 128, :], reads=(g1b,),
              writes=(zt.buf,))
        p.s(lambda E, zt=zt, st=st: E.activation(st.h[:, :], zt.h[:, :], AF.Silu),
            reads=(zt.buf,), writes=(st.buf,))
        p.v(lambda E, i=i, st=st: E.tensor_mul(ar.A[8 + i].h[:, :], ar.A[8 + i].h[:, :],
                                               st.h[:, :]),
            reads=(ar.A[8 + i].buf, st.buf), writes=(ar.A[8 + i].buf,))

    if _os.environ.get("L2_DEBUG") == "y":
        off = 8 if _os.environ.get("L2_HALF") == "1" else 0
        for i in range(8):
            p.dma("gpsimd", x_dst[i * 128:(i + 1) * 128, :], ar.A[8 + off + i].h[:, :],
                  reads=(ar.A[8 + off + i].buf,), writes=(xdb,))
        return
    for ob in range(FD):
        slot = ar.W[1]
        load_wtile(p, ar, slot, opT, opb, None, 16, ob)
        p.dma("sync", ar.Sg[1].h[:, :], x_src[ob * 128:(ob + 1) * 128, :],
              reads=(xsb,), writes=(ar.Sg[1].buf,))
        banks = [ar.bank() for _ in range(4)]
        for tc in range(4):
            sl = slice(512 * tc, 512 * (tc + 1))
            for cch in range(16):
                p.t(lambda E, cch=cch, sl=sl, pk=banks[tc], slot=slot: E.matmul(
                    pk.h[:, :], slot.h[:, cch * 128:(cch + 1) * 128], ar.A[8 + cch].h[:, sl],
                    start=(cch == 0), stop=(cch == 15)),
                    reads=(slot.buf, ar.A[8 + cch].buf), writes=(banks[tc].buf,))
        for tc in range(4):
            sl = slice(512 * tc, 512 * (tc + 1))
            pk = banks[tc]
            p.v(lambda E, sl=sl, pk=pk, ob=ob: E.tensor_add(
                ar.Sg[0].h[:, sl], pk.h[:, :], ar.N[ob].h[:, sl]),
                reads=(pk.buf, ar.N[ob].buf), writes=(ar.Sg[0].buf,))
            p.v(lambda E, sl=sl: E.tensor_add(
                ar.Sg[0].h[:, sl], ar.Sg[0].h[:, sl], ar.Sg[1].h[:, sl]),
                reads=(ar.Sg[0].buf, ar.Sg[1].buf), writes=(ar.Sg[0].buf,))
        p.dma("sync", x_dst[ob * 128:(ob + 1) * 128, :], ar.Sg[0].h[:, :],
              reads=(ar.Sg[0].buf,), writes=(xdb,))


def decl_attn_l2(nc, wd, l):
    shapes = {
        "L2_attn_in_proj_w": (32, 128, 1024), "L2_attn_out_proj_w": (8, 128, 2048),
        "L2_attn_norm_w": (128, 8),
    }
    for suf in ("_f", "_b"):
        shapes["L2_attn_conv" + suf] = (128, 64)
        shapes["L2_attn_convb" + suf] = (128, 16)
        shapes["L2_attn_xproj" + suf] = (1, 128, 1536)
        shapes["L2_attn_dtproj" + suf] = (64, 2048)
        shapes["L2_attn_dtprojb" + suf] = (128, 16)
        shapes["L2_attn_alog" + suf] = (128, 256)
        shapes["L2_attn_D" + suf] = (128, 16)
    for k, shp in shapes.items():
        if k not in wd:
            h = nc.dram_tensor(k, list(shp), F32, kind="ExternalInput")
            wd[k] = (h.ap(), Buf(k))
    if "bcbuf" not in wd:
        h = nc.dram_tensor("bcbuf", [4, 32, 128], BF16, kind="Internal")
        wd["bcbuf"] = (h.ap(), Buf("bcbuf"))


def prep_attn_l2(inputs, l):
    d = {}
    d["L2_attn_in_proj_w"] = _wtiles(_t(inputs["L2_attn_in_proj_w"]), 8, 32)
    d["L2_attn_out_proj_w"] = _wtiles(_t(inputs["L2_attn_out_proj_w"]), 16, 8)
    d["L2_attn_norm_w"] = _cols(inputs["L2_attn_norm_w"])
    for suf in ("_f", "_b"):
        cw = np.asarray(inputs["L2_attn_conv_w" + suf], np.float32).reshape(2048, 4)
        d["L2_attn_conv" + suf] = np.concatenate([_cols(cw[:, k]) for k in range(4)], axis=1)
        d["L2_attn_convb" + suf] = _cols(inputs["L2_attn_conv_b" + suf])
        d["L2_attn_xproj" + suf] = _wtiles(_t(inputs["L2_attn_xproj_w" + suf]), 16, 1, owid=96)
        d["L2_attn_dtproj" + suf] = _t(inputs["L2_attn_dtproj_w" + suf])
        d["L2_attn_dtprojb" + suf] = _cols(inputs["L2_attn_dtproj_b" + suf])
        al = np.asarray(inputs["L2_attn_A_log" + suf], np.float32)
        d["L2_attn_alog" + suf] = np.ascontiguousarray(
            al.reshape(16, 128, 16).transpose(1, 0, 2).reshape(128, 256))
        d["L2_attn_D" + suf] = _cols(inputs["L2_attn_D" + suf])
    return d



def _t(a):
    return np.ascontiguousarray(np.asarray(a).T)


def _cols(a, pad_to=None):
    a = np.asarray(a)
    n = a.shape[0]
    nb = n // 128
    out = np.ascontiguousarray(a.reshape(nb, 128).T)
    return out


def prep_ffn_ternary(inputs, l):
    pre = f"L{l}_ffn_"
    d = {}
    d[pre + "w1_w"] = _wtiles(_t(inputs[pre + "w1_w"]), 8, 32)
    d[pre + "w2_w"] = _wtiles(_t(inputs[pre + "w2_w"]), 8, 32)
    d[pre + "h1_w"] = _wtiles(_t(inputs[pre + "h1_w"]), 32, 32, halves=2)
    d[pre + "h2_w"] = _wtiles(_t(inputs[pre + "h2_w"]), 32, 32, halves=2)
    d[pre + "w3_w"] = _wtiles(_t(inputs[pre + "w3_w"]), 32, 8, halves=2)
    sc = np.concatenate([
        _cols(inputs[pre + "w1_s"]), _cols(inputs[pre + "w2_s"]),
        _cols(inputs[pre + "h1_s"]), _cols(inputs[pre + "h2_s"]),
        _cols(inputs[pre + "w3_s"])], axis=1)
    d[pre + "scales"] = np.ascontiguousarray(sc)
    return d


def decl_ffn_ternary(nc, wd, l):
    pre = f"L{l}_ffn_"
    shapes = {
        pre + "w1_w": (32, 128, 1024), pre + "w2_w": (32, 128, 1024),
        pre + "h1_w": (32, 2, 128, 2048), pre + "h2_w": (32, 2, 128, 2048),
        pre + "w3_w": (8, 2, 128, 2048), pre + "scales": (128, 136),
    }
    for k, shp in shapes.items():
        if k not in wd:
            h = nc.dram_tensor(k, list(shp), F32, kind="ExternalInput")
            wd[k] = (h.ap(), Buf(k))



def sublayer_chain():
    chain = []
    for l in range(4):
        if l == 0:
            ffn = ("ffn0", None)
            mix = ("mix0", None)
        elif l == 1:
            ffn = ("ffnT", l)
            mix = ("mix1", None)
        elif l == 2:
            ffn = ("ffnT", l)
            mix = ("mix2", None)
        else:
            ffn = ("ffnT", l)
            mix = ("mix3", None)
        chain.append(("ffn", l))
        chain.append(("mix", l))
        chain.append(("ffn", l))
    return chain


EMITTERS = {}
EMITTERS["ffn_0"] = (emit_ffn0, decl_ffn0, prep_ffn0)
EMITTERS["ffn_1"] = (emit_ffn_ternary, decl_ffn_ternary, prep_ffn_ternary)
EMITTERS["ffn_2"] = (emit_ffn_ternary, decl_ffn_ternary, prep_ffn_ternary)
EMITTERS["ffn_3"] = (emit_ffn_ternary, decl_ffn_ternary, prep_ffn_ternary)
EMITTERS["mix_3"] = (emit_attn_l3, decl_attn_l3, prep_attn_l3)
EMITTERS["mix_1"] = (emit_attn_l1, decl_attn_l1, prep_attn_l1)
EMITTERS["mix_0"] = (emit_attn_l0, decl_attn_l0, prep_attn_l0)
EMITTERS["mix_2"] = (emit_attn_l2, decl_attn_l2, prep_attn_l2)


def build_program(sublayers):
    nc = bass.Bass("TRN2", target_bir_lowering=False)
    stack = ExitStack()
    x_in = nc.dram_tensor("x_in", [D, S], F32, kind="ExternalInput").ap()
    x_out = nc.dram_tensor("x_out", [D, S], F32, kind="ExternalOutput").ap()
    xb0 = nc.dram_tensor("xb0", [D, S], F32, kind="Internal").ap()
    xb1 = nc.dram_tensor("xb1", [D, S], F32, kind="Internal").ap()
    xbufs = [(x_in, Buf("x_in")), (xb0, Buf("xb0")), (xb1, Buf("xb1")),
             (x_out, Buf("x_out"))]

    wd = {}
    g1 = nc.dram_tensor("g1buf", [4096, S], BF16, kind="Internal").ap()
    g2 = nc.dram_tensor("g2buf", [4096, S], BF16, kind="Internal").ap()
    wd["g1buf"] = (g1, Buf("g1buf"))
    wd["g2buf"] = (g2, Buf("g2buf"))

    for kind, l in sublayers:
        _, decl, _ = EMITTERS[f"{kind}_{l}"]
        decl(nc, wd, l)

    with stack:
        ar = Arena(nc, stack)
        p = Prog(nc, stack)
        c_ones = nc.dram_tensor("c_ones", [128, 128], BF16, kind="ExternalInput").ap()
        c_ident = nc.dram_tensor("c_ident", [128, 128], BF16, kind="ExternalInput").ap()
        c_consts = nc.dram_tensor("c_consts", [128, len(CONST_VALS)], F32,
                                  kind="ExternalInput").ap()
        cst = TT(stack.enter_context(
            nc.sbuf_tensor("cst", [128, len(CONST_VALS)], F32)), "cst")
        p.dma("sync", ar.ones.h[:, :], c_ones, writes=(ar.ones.buf,))
        p.dma("sync", ar.ident.h[:, :], c_ident, writes=(ar.ident.buf,))
        p.dma("sync", cst.h[:, :], c_consts, writes=(cst.buf,))
        c_tri = nc.dram_tensor("c_tri", [128, 128], F32, kind="ExternalInput").ap()
        p.dma("sync", ar.tri.h[:, :], c_tri, writes=(ar.tri.buf,))
        c_esel = nc.dram_tensor("c_esel", [8, 1024], BF16, kind="ExternalInput").ap()
        p.dma("sync", ar.esel.h[0:8, :], c_esel, writes=(ar.esel.buf,))
        c_hsel2 = nc.dram_tensor("c_hsel2", [16, 1024], BF16, kind="ExternalInput").ap()
        p.dma("sync", ar.esel.h[32:48, :], c_hsel2, writes=(ar.esel.buf,))
        c_identf = nc.dram_tensor("c_identf", [128, 128], F32, kind="ExternalInput").ap()
        p.dma("sync", ar.identf.h[:, :], c_identf, writes=(ar.identf.buf,))
        c_onesf = nc.dram_tensor("c_onesf", [128, 128], F32, kind="ExternalInput").ap()
        p.dma("sync", ar.onesf.h[:, :], c_onesf, writes=(ar.onesf.buf,))
        c_trils = nc.dram_tensor("c_trils", [128, 128], F32, kind="ExternalInput").ap()
        p.dma("sync", ar.trils.h[:, :], c_trils, writes=(ar.trils.buf,))
        c_trili = nc.dram_tensor("c_trili", [128, 128], F32, kind="ExternalInput").ap()
        p.dma("sync", ar.trili.h[:, :], c_trili, writes=(ar.trili.buf,))
        for k, v in enumerate(CONST_VALS):
            nc.const_aps.register(F32, v, cst.h[:, k:k + 1])
            nc.const_aps.register(BF16, v, cst.h[:, k:k + 1])
        for e in ENGINES:
            if e in ("sync",):
                continue
            p.emit(e, lambda E: E.nop(),
                   reads=(cst.buf, ar.ones.buf, ar.ident.buf, ar.identf.buf,
                          ar.onesf.buf, ar.tri.buf, ar.esel.buf))

        n = len(sublayers)
        for idx, (kind, l) in enumerate(sublayers):
            if idx == 0:
                src = xbufs[0]
            else:
                src = xbufs[1 + ((idx - 1) % 2)]
            if idx == n - 1:
                dst = xbufs[3]
            else:
                dst = xbufs[1 + (idx % 2)]
            emit, _, _ = EMITTERS[f"{kind}_{l}"]
            io = (src[0], src[1], dst[0], dst[1])
            emit(p, ar, io, wd, l)
        p.finalize()
    return nc


def prep_inputs_common():
    consts = np.repeat(np.asarray(CONST_VALS, np.float32)[None, :], 128, axis=0)
    return {
        "c_ones": np.ones((128, 128), dtype=BF),
        "c_ident": np.eye(128, dtype=np.float32).astype(BF),
        "c_consts": np.ascontiguousarray(consts),
        "c_tri": np.triu(np.full((128, 128), -1.0e30, np.float32), 1),
        "c_esel": _esel(),
        "c_hsel2": _hsel2(),
        "c_identf": np.eye(128, dtype=np.float32),
        "c_trils": np.tril(np.full((128, 128), -1.0e30, np.float32), -1),
        "c_trili": np.tril(np.full((128, 128), -1.0e30, np.float32)),
        "c_onesf": _e127(),
    }


def _esel():
    m = np.zeros((8, 1024), np.float32)
    for e in range(8):
        m[e, e * 128:(e + 1) * 128] = 1.0
    return m.astype(BF)


def _e127():
    m = np.zeros((128, 128), np.float32)
    m[127, :] = 1.0
    return m


def _hsel2():
    m = np.zeros((16, 1024), np.float32)
    for k in range(16):
        hk = 2 * (k % 8) + (k // 8)
        i, r = hk // 2, hk % 2
        m[k, i * 128 + r * 64:i * 128 + r * 64 + 64] = 1.0
    return m.astype(BF)


def kernel(**inputs):
    chain = [(k, l) for (k, l) in sublayer_chain()]
    nc = build_program(chain)
    x = np.asarray(inputs["L0_attn_x"])
    per_core_common = {}
    for (kind, l) in set(chain):
        _, _, prep = EMITTERS[f"{kind}_{l}"]
        per_core_common.update(prep(inputs, l))
    per_core_common.update(prep_inputs_common())
    in_maps = []
    for c in range(N_CORES):
        m = dict(per_core_common)
        m["x_in"] = np.ascontiguousarray(x[c].T)
        in_maps.append(m)
    res = run_bass_kernel_spmd(nc, in_maps, core_ids=list(range(N_CORES)))
    outs = [np.ascontiguousarray(r["x_out"].T) for r in res.results]
    return np.stack(outs, axis=0)
